# version 04
# baseline (speedup 1.0000x reference)
import contextlib
import math

import numpy as np
import ml_dtypes

import concourse.bass as bass
import concourse.mybir as mybir
from concourse.bass_utils import run_bass_kernel_spmd

F32 = mybir.dt.float32
F16 = mybir.dt.float16
BF16 = mybir.dt.bfloat16
I32 = mybir.dt.int32
AF = mybir.ActivationFunctionType
ALU = mybir.AluOpType

S = 8192
D = 4096
NH = 32
HD = 128
INTER = 11008
NC = 8
TPC = S // NC
NSLOT = 4
NSL = INTER // NC
NQB = 16
QB = 512
NCH = D // 128
WIN_THRESH = 100.0

DEBUG_DUMPS = ()


def _alibi_slopes(n_heads):
    cp2 = 2 ** math.floor(math.log2(n_heads))
    base = 2 ** (-(2 ** (-(math.log2(cp2) - 3))))
    slopes = base ** np.arange(1, cp2 + 1, dtype=np.float64)
    if cp2 != n_heads:
        extra_base = 2 ** (-(2 ** (-(math.log2(2 * cp2) - 3))))
        n_rem = min(cp2, n_heads - cp2)
        extra = extra_base ** np.arange(1, 1 + 2 * n_rem, 2, dtype=np.float64)
        slopes = np.concatenate([slopes, extra])
    return slopes


SLOPES = _alibi_slopes(NH)

B_SLOT = []
for _s in range(NSLOT):
    sig_min = SLOPES[8 * _s + 7]
    B_SLOT.append(min(60, int(math.floor((127 + WIN_THRESH / sig_min) / 128.0))))

N_STEPS = [[4 + min(4 * b, B_SLOT[s]) for b in range(NQB)] for s in range(NSLOT)]
TOTAL_STEPS = sum(sum(r) for r in N_STEPS)

NT_SIZES = [128] * 10 + [96]
NNT = len(NT_SIZES)
NT_OFF = [sum(NT_SIZES[:i]) for i in range(NNT)]

STEP_U = []
BLOCK_RANGE = {}
_g = 0
for _s in range(NSLOT):
    for _b in range(NQB):
        n = N_STEPS[_s][_b]
        BLOCK_RANGE[(_s, _b)] = (_g, _g + n)
        for _u in range(-(n - 4), 4):
            STEP_U.append((_s, _b, _u))
        _g += n
assert _g == TOTAL_STEPS

DQ_UNITS = [("zprep", 0), ("zprep", 1)]
for _m in range(2):
    for _j in range(NCH):
        DQ_UNITS.append(("gu", _m, _j))
for _mt in range(NNT):
    for _ec in range(4):
        DQ_UNITS.append(("down", _mt, _ec))
N_DQ = len(DQ_UNITS)
DQL2_AT = []
_v2 = [0, 0]
for _u in range(N_DQ):
    _v2[_u % 2] += 32 if DQ_UNITS[_u][0] == "zprep" else 48
    DQL2_AT.append(_v2[_u % 2])
DQ_W_A = (0, 15)
DQ_W_B = (15, 51)
DQ_W_S = [(51, 71), (71, 91), (91, 101), (101, N_DQ)]


class Sem:
    __slots__ = ("h", "v")

    def __init__(self, h):
        self.h = h
        self.v = 0

    def inc(self, inst, n=1):
        inst.then_inc(self.h, n)
        self.v += n
        return self.v


def build_nc(debug=(), detect_races=True):
    nc = bass.Bass("TRN2", target_bir_lowering=False, detect_race_conditions=detect_races)

    xrows = nc.dram_tensor("xrows", [TPC, D], F32, kind="ExternalInput").ap()
    wqkv = nc.dram_tensor("wqkv", [D, NSLOT * 3 * HD], F32, kind="ExternalInput").ap()
    bqkvT = nc.dram_tensor("bqkvT", [HD, 12], F32, kind="ExternalInput").ap()
    wprojc = nc.dram_tensor("wprojc", [NSLOT * HD, D], F32, kind="ExternalInput").ap()
    bproj = nc.dram_tensor("bproj", [1, D], F32, kind="ExternalInput").ap()
    patt = nc.dram_tensor("patt", [NSLOT * 5 * 128, QB], BF16, kind="ExternalInput").ap()
    ebias = nc.dram_tensor("ebias", [128, NSLOT * 64], F32, kind="ExternalInput").ap()
    shifts_in = nc.dram_tensor("shifts", [128, 1], I32, kind="ExternalInput").ap()
    identf_in = nc.dram_tensor("identf", [128, 128], F32, kind="ExternalInput").ap()
    onesb_in = nc.dram_tensor("onesb", [128, 128], BF16, kind="ExternalInput").ap()
    gq = nc.dram_tensor("gq", [512, NSL], I32, kind="ExternalInput").ap()
    gs = nc.dram_tensor("gs", [32, NSL], F16, kind="ExternalInput").ap()
    gz = nc.dram_tensor("gz", [32, NSL // 8], I32, kind="ExternalInput").ap()
    uq = nc.dram_tensor("uq", [512, NSL], I32, kind="ExternalInput").ap()
    us = nc.dram_tensor("us", [32, NSL], F16, kind="ExternalInput").ap()
    uz = nc.dram_tensor("uz", [32, NSL // 8], I32, kind="ExternalInput").ap()
    dq = nc.dram_tensor("dq", [NSL // 8, D], I32, kind="ExternalInput").ap()
    dsx = nc.dram_tensor("dsx", [NSL, D], F16, kind="ExternalInput").ap()
    dzx = nc.dram_tensor("dzx", [NSL, D // 8], I32, kind="ExternalInput").ap()
    out = nc.dram_tensor("out", [TPC, D], F32, kind="ExternalOutput").ap()

    h0slab = nc.dram_tensor("h0slab", [D, TPC], BF16).ap()
    h0g = nc.dram_tensor("h0g", [NC * D, TPC], BF16).ap()
    qkvd = nc.dram_tensor("qkvd", [12, HD, S], BF16).ap()
    oaslab = nc.dram_tensor("oaslab", [NSLOT * HD, S], BF16).ap()
    y1dA = nc.dram_tensor("y1dA", [NC * QB, D], BF16).ap()
    y1dB = nc.dram_tensor("y1dB", [NC * QB, D], BF16).ap()
    y1rA = nc.dram_tensor("y1rA", [QB, D], BF16).ap()
    y1rB = nc.dram_tensor("y1rB", [QB, D], BF16).ap()
    x1d = nc.dram_tensor("x1d", [TPC, D], F32).ap()
    h2slabA = nc.dram_tensor("h2slabA", [D, QB], BF16).ap()
    h2slabB = nc.dram_tensor("h2slabB", [D, QB], BF16).ap()
    h2gA = nc.dram_tensor("h2gA", [NC * D, QB], BF16).ap()
    h2gB = nc.dram_tensor("h2gB", [NC * D, QB], BF16).ap()
    wgbf = nc.dram_tensor("wgbf", [128, NCH * NSL], BF16).ap()
    wubf = nc.dram_tensor("wubf", [128, NCH * NSL], BF16).ap()
    wdbf = nc.dram_tensor("wdbf", [NSL, D], BF16).ap()
    zpgd = nc.dram_tensor("zpgd", [32, NSL], F32).ap()
    zpud = nc.dram_tensor("zpud", [32, NSL], F32).ap()
    y2dA = nc.dram_tensor("y2dA", [NC * QB, D], BF16).ap()
    y2dB = nc.dram_tensor("y2dB", [NC * QB, D], BF16).ap()
    y2rA = nc.dram_tensor("y2rA", [QB, D], BF16).ap()
    y2rB = nc.dram_tensor("y2rB", [QB, D], BF16).ap()

    dbg = {}
    for name, src_shape, dt_ in (
        ("h0", [NC * D, TPC], BF16), ("qkv", [12 * HD, S], BF16),
        ("oa", [NSLOT * HD, S], BF16), ("y1a", [QB, D], BF16),
        ("x1", [TPC, D], F32), ("h2a", [NC * D, QB], BF16),
        ("h2b", [NC * D, QB], BF16), ("y2a", [QB, D], BF16),
        ("y2b", [QB, D], BF16),
    ):
        if name in debug or (name[:2] in ("h2", "y2") and name[:2] in debug) or \
           (name in ("y1a",) and "y1" in debug):
            dbg[name] = nc.dram_tensor("dbg_" + name, src_shape, dt_, kind="ExternalOutput").ap()

    RG = [list(range(NC))]

    ctx = contextlib.ExitStack()
    with ctx:
        e = ctx.enter_context

        shifts = e(nc.sbuf_tensor("s_shifts", [128, 1], I32))
        epsv = e(nc.sbuf_tensor("s_eps", [128, 1], F32))
        sbq = e(nc.sbuf_tensor("s_bqkv", [128, 12], F32))
        stat6 = e(nc.sbuf_tensor("s_stat6", [128, 8 * 6], F32))
        stat2 = e(nc.sbuf_tensor("s_stat2", [128, 2], F32))
        lnt1 = e(nc.sbuf_tensor("s_lnt1", [128, 1], F32))
        lnt2 = e(nc.sbuf_tensor("s_lnt2", [128, 1], F32))
        fscr = e(nc.sbuf_tensor("s_fscr", [128, 1], F32))
        fsc2 = e(nc.sbuf_tensor("s_fsc2", [128, 1], F32))
        sM = contextlib.ExitStack()
        eM = sM.enter_context
        identb = eM(nc.sbuf_tensor("s_identb", [128, 128], BF16))
        identf = eM(nc.sbuf_tensor("s_identf", [128, 128], F32))
        onesb = eM(nc.sbuf_tensor("s_onesb", [128, 128], BF16))
        sebias = eM(nc.sbuf_tensor("s_ebias", [128, NSLOT * 64], F32))

        pf = [e(nc.psum_tensor(f"pf{i}", [128, 512], F32)) for i in range(6)]
        pb = [e(nc.psum_tensor(f"pb{i}", [128, 1024], BF16)) for i in range(2)]

        def sem(name):
            return Sem(e(nc.semaphore(name)))

        s_init = sem("init")
        s_dql = [sem("dql0"), sem("dql1")]; s_dqv = sem("dqv")
        s_dqs = [sem("dqs0"), sem("dqs1")]
        s_p0l = [sem("p0l0"), sem("p0l1")]; s_p0v = sem("p0v"); s_rst = sem("rst")
        s_p0t = sem("p0t"); s_p0e = sem("p0e"); s_p0s = sem("p0s")
        s_ag0 = sem("ag0")
        s_p1w = [sem("p1w0"), sem("p1w1")]; s_p1wc = sem("p1wc")
        s_p1h = [sem("p1h0"), sem("p1h1"), sem("p1h2"), sem("p1h3")]
        s_p1j = sem("p1j"); s_p1e = sem("p1e")
        s_p1s = [sem("p1s0"), sem("p1s1")]
        s_patt = sem("patt"); s_kf = sem("skf"); s_vf = sem("svf"); s_qf = sem("sqf")
        s_vt = sem("vt"); s_vte = sem("vte")
        s_sS = sem("sS"); s_sX = sem("sX"); s_sM = sem("sM"); s_sAV = sem("sAV")
        s_sR = sem("sR"); s_sE = sem("sE"); s_sON = sem("sON"); s_oas = sem("oas")
        s_bp = sem("sbp")
        s_p3l = [sem("p3l0"), sem("p3l1")]
        s_p3w = [sem("p3w0"), sem("p3w1")]; s_p3wc = sem("p3wc")
        s_p3m = sem("p3m"); s_p3e = sem("p3e")
        s_p3s = [sem("p3s0"), sem("p3s1")]
        s_rsa1 = sem("rsa1"); s_rsb1 = sem("rsb1")
        s_r5l = sem("r5l"); s_r5v = sem("r5v"); s_r5r = sem("r5r")
        s_h2t = sem("h2t"); s_h2e = sem("h2e")
        s_x1s = sem("x1s"); s_h2sA = sem("h2sA"); s_h2sB = sem("h2sB")
        s_agh2a = sem("agh2a"); s_agh2b = sem("agh2b")
        s_fh = sem("fh"); s_fwg = sem("fwg"); s_fwu = sem("fwu"); s_fwd = sem("fwd")
        s_fg = sem("fg"); s_fsil = sem("fsil"); s_fint = sem("fint")
        s_fy = sem("fy"); s_fye = sem("fye")
        s_fys = [sem("fys0"), sem("fys1")]
        s_rsa2 = sem("rsa2"); s_rsb2 = sem("rsb2")
        s_p5l = sem("p5l"); s_p5v = sem("p5v"); s_p5r = sem("p5r"); s_p5s = sem("p5s")
        s_dv = sem("dvself"); s_sc = sem("scself")
        s_fab = sem("fab"); s_fbc = sem("fbc"); s_fcd = sem("fcd")
        s_fde = sem("fde"); s_fef = sem("fef")

        def vedge(vector, inst):
            inst.then_inc(s_dv.h, 1)
            s_dv.v += 1
            vector.wait_ge(s_dv.h, s_dv.v)

        def sedge(scalar, inst):
            inst.then_inc(s_sc.h, 1)
            s_sc.v += 1
            scalar.wait_ge(s_sc.h, s_sc.v)

        block = e(nc.Block())

        @block.sync
        def _(sync):
            sync.dma_start(out=shifts[:], in_=shifts_in).then_inc(s_init.h, 16)
            sync.dma_start(out=identf[:], in_=identf_in).then_inc(s_init.h, 16)
            sync.dma_start(out=onesb[:], in_=onesb_in).then_inc(s_init.h, 16)
            sync.dma_start(out=sebias[:], in_=ebias).then_inc(s_init.h, 16)
            sync.dma_start(out=sbq[:], in_=bqkvT).then_inc(s_init.h, 16)
            s_init.v += 80

        @block.vector
        def _(vector):
            vector.wait_ge(s_init.h, 80)
            vector.memset(epsv[:], 1e-5)
            vector.tensor_copy(out=identb[:], in_=identf[:]).then_inc(s_init.h, 1)
            s_init.v += 1
        INIT_DONE = s_init.v

        sDQ = contextlib.ExitStack()
        eQ = sDQ.enter_context
        dqr = [eQ(nc.sbuf_tensor(f"dq_qr{i}", [128, NSL], I32)) for i in range(2)]
        dsc = [eQ(nc.sbuf_tensor(f"dq_sc{i}", [128, NSL], F16)) for i in range(2)]
        dzp = [eQ(nc.sbuf_tensor(f"dq_zp{i}", [128, NSL], F32)) for i in range(2)]
        dnib = eQ(nc.sbuf_tensor("dq_nib", [128, NSL], I32))
        dt1 = eQ(nc.sbuf_tensor("dq_t1", [128, NSL], F32))
        dwb = [eQ(nc.sbuf_tensor(f"dq_wb{i}", [128, NSL], BF16)) for i in range(2)]
        dzrow = [eQ(nc.sbuf_tensor(f"dq_zrow{i}", [128, 128], I32)) for i in range(2)]
        dzaux = eQ(nc.sbuf_tensor("dq_zaux", [128, 1024], I32))
        zpz = eQ(nc.sbuf_tensor("dq_zpz", [32, NSL // 8], I32))
        zpe = eQ(nc.sbuf_tensor("dq_zpe", [32, NSL], I32))
        zps = eQ(nc.sbuf_tensor("dq_zps", [32, NSL], F16))
        zpf = eQ(nc.sbuf_tensor("dq_zpf", [32, NSL], F32))
        zpg = eQ(nc.sbuf_tensor("dq_zpg", [32, NSL], F32))

        gu_src = [(gq, gs, zpgd, wgbf), (uq, us, zpud, wubf)]

        def dq_loads(gpsimd, lo, hi):
            for u in range(lo, hi):
                kind = DQ_UNITS[u]
                sl_ = s_dql[u % 2]
                if kind[0] == "zprep":
                    m_ = kind[1]
                    if u == 1:
                        gpsimd.wait_ge(s_dqv.h, 1)
                    zsrc = gz if m_ == 0 else uz
                    ssrc = gs if m_ == 0 else us
                    gpsimd.dma_start(out=zpz[:], in_=zsrc).then_inc(sl_.h, 16)
                    gpsimd.dma_start(out=zps[:], in_=ssrc).then_inc(sl_.h, 16)
                    sl_.v += 32
                elif kind[0] == "gu":
                    m_, j_ = kind[1], kind[2]
                    qsrc, ssrc, zpd_, _w = gu_src[m_]
                    if u >= 2:
                        gpsimd.wait_ge(s_dqv.h, u - 1)
                    gpsimd.wait_ge(s_dqs[m_].h, 16)
                    rep = bass.AP(qsrc.tensor, 16 * j_ * NSL, [[NSL, 16], [0, 8], [1, NSL]])
                    gpsimd.dma_start(out=dqr[u % 2][:, 0:NSL], in_=rep).then_inc(sl_.h, 16)
                    srep = bass.AP(ssrc.tensor, j_ * NSL, [[0, 128], [1, NSL]])
                    gpsimd.dma_start(out=dsc[u % 2][:, 0:NSL], in_=srep).then_inc(sl_.h, 16)
                    zrep = bass.AP(zpd_.tensor, j_ * NSL, [[0, 128], [1, NSL]])
                    gpsimd.dma_start(out=dzp[u % 2][:, 0:NSL], in_=zrep).then_inc(sl_.h, 16)
                    sl_.v += 48
                else:
                    _, mt_, ec_ = kind
                    nw = NT_SIZES[mt_]
                    if u >= 2:
                        gpsimd.wait_ge(s_dqv.h, u - 1)
                    rep = bass.AP(dq.tensor, 16 * mt_ * D + 1024 * ec_, [[D, nw // 8], [0, 8], [1, 1024]])
                    gpsimd.dma_start(out=dqr[u % 2][0:nw, 0:1024], in_=rep).then_inc(sl_.h, 16)
                    gpsimd.dma_start(out=dsc[u % 2][0:nw, 0:1024],
                                     in_=dsx[128 * mt_:128 * mt_ + nw, 1024 * ec_:1024 * (ec_ + 1)]).then_inc(sl_.h, 16)
                    gpsimd.dma_start(out=dzrow[u % 2][0:nw, :],
                                     in_=dzx[128 * mt_:128 * mt_ + nw, 128 * ec_:128 * (ec_ + 1)]).then_inc(sl_.h, 16)
                    sl_.v += 48

        def dq_vec(vector, lo, hi):
            for u in range(lo, hi):
                kind = DQ_UNITS[u]
                vector.wait_ge(s_dql[u % 2].h, DQL2_AT[u])
                if u == 1:
                    vector.wait_ge(s_dqs[0].h, 16)
                if u >= 2:
                    vector.wait_ge(s_dqs[u % 2].h, 16 * (u // 2))
                if kind[0] == "zprep":
                    for jn in range(8):
                        inst = vector.tensor_scalar(out=zpe[:, jn::8], in0=zpz[:], scalar1=4 * jn,
                                                    scalar2=15, op0=ALU.logical_shift_right, op1=ALU.bitwise_and)
                    vedge(vector, inst)
                    vedge(vector, vector.tensor_scalar(out=zpf[:], in0=zpe[:], scalar1=1.0, scalar2=None, op0=ALU.add))
                    vector.tensor_tensor(out=zpg[:], in0=zpf[:], in1=zps[:], op=ALU.mult).then_inc(s_dqv.h, 1)
                elif kind[0] == "gu":
                    u2 = u % 2
                    vedge(vector, vector.tensor_scalar(out=dnib[:, 0:NSL], in0=dqr[u2][:, 0:NSL], scalar1=shifts[:],
                                                       scalar2=15, op0=ALU.logical_shift_right, op1=ALU.bitwise_and))
                    vedge(vector, vector.tensor_tensor(out=dt1[:, 0:NSL], in0=dnib[:, 0:NSL],
                                                       in1=dsc[u2][:, 0:NSL], op=ALU.mult))
                    vector.tensor_tensor(out=dwb[u2][:, 0:NSL], in0=dt1[:, 0:NSL], in1=dzp[u2][:, 0:NSL],
                                         op=ALU.subtract).then_inc(s_dqv.h, 1)
                else:
                    _, mt_, ec_ = kind
                    u2 = u % 2
                    for jn in range(8):
                        inst = vector.tensor_scalar(out=dzaux[:, jn::8], in0=dzrow[u % 2][:], scalar1=4 * jn,
                                                    scalar2=15, op0=ALU.logical_shift_right, op1=ALU.bitwise_and)
                    vedge(vector, inst)
                    vedge(vector, vector.tensor_scalar(out=dt1[:, 0:1024], in0=dzaux[:], scalar1=1.0,
                                                       scalar2=None, op0=ALU.add))
                    vedge(vector, vector.tensor_tensor(out=dzp[u2][:, 0:1024], in0=dt1[:, 0:1024],
                                                       in1=dsc[u2][:, 0:1024], op=ALU.mult))
                    vedge(vector, vector.tensor_scalar(out=dnib[:, 0:1024], in0=dqr[u2][:, 0:1024], scalar1=shifts[:],
                                                       scalar2=15, op0=ALU.logical_shift_right, op1=ALU.bitwise_and))
                    vedge(vector, vector.tensor_tensor(out=dt1[:, 0:1024], in0=dnib[:, 0:1024],
                                                       in1=dsc[u2][:, 0:1024], op=ALU.mult))
                    vector.tensor_tensor(out=dwb[u2][:, 0:1024], in0=dt1[:, 0:1024], in1=dzp[u2][:, 0:1024],
                                         op=ALU.subtract).then_inc(s_dqv.h, 1)
                s_dqv.v += 1

        def dq_stores(gpsimd, lo, hi):
            for u in range(lo, hi):
                kind = DQ_UNITS[u]
                ss_ = s_dqs[u % 2]
                gpsimd.wait_ge(s_dqv.h, u + 1)
                if kind[0] == "zprep":
                    dst = zpgd if kind[1] == 0 else zpud
                    gpsimd.dma_start(out=dst, in_=zpg[:]).then_inc(ss_.h, 16)
                elif kind[0] == "gu":
                    m_, j_ = kind[1], kind[2]
                    gpsimd.dma_start(out=gu_src[m_][3][:, j_ * NSL:(j_ + 1) * NSL],
                                     in_=dwb[u % 2][:, 0:NSL]).then_inc(ss_.h, 16)
                else:
                    _, mt_, ec_ = kind
                    nw = NT_SIZES[mt_]
                    gpsimd.dma_start(out=wdbf[128 * mt_:128 * mt_ + nw, 1024 * ec_:1024 * (ec_ + 1)],
                                     in_=dwb[u % 2][0:nw, 0:1024]).then_inc(ss_.h, 16)
                ss_.v += 16

        dq_ptr = [0]

        def dq_chunk(gpsimd, lo, hi, flush=False):
            for u in range(lo, hi):
                while dq_ptr[0] <= u - 2:
                    dq_stores(gpsimd, dq_ptr[0], dq_ptr[0] + 1)
                    dq_ptr[0] += 1
                dq_loads(gpsimd, u, u + 1)
            if flush:
                while dq_ptr[0] < hi:
                    dq_stores(gpsimd, dq_ptr[0], dq_ptr[0] + 1)
                    dq_ptr[0] += 1

        sA = contextlib.ExitStack()
        eA = sA.enter_context
        x0t = [eA(nc.sbuf_tensor(f"p0_x{i}", [128, D], F32)) for i in range(2)]
        h0t = eA(nc.sbuf_tensor("p0_h0", [128, D], BF16))
        h0stg = eA(nc.sbuf_tensor("p0_stg", [128, D], BF16))

        NT0 = TPC // 128

        @block.sync
        def _(sync):
            for i in range(NT0):
                if i >= 2:
                    sync.wait_ge(s_p0v.h, 2 * (i - 1))
                sp = s_p0l[i % 2]
                sync.dma_start(out=x0t[i % 2][:], in_=xrows[128 * i:128 * (i + 1), :]).then_inc(sp.h, 16)
                sp.v += 16

        @block.vector
        def _(vector):
            vector.wait_ge(s_init.h, INIT_DONE)
            for i in range(NT0):
                vector.wait_ge(s_p0l[i % 2].h, 16 * (i // 2 + 1))
                if i >= 1:
                    vector.wait_ge(s_rst.h, i)
                xt = x0t[i % 2]
                for ch in range(8):
                    inst = vector.bn_stats(out=stat6[:, 6 * ch:6 * ch + 6], in_=xt[:, 512 * ch:512 * (ch + 1)])
                vedge(vector, inst)
                vector.bn_aggr(out=stat2[:], in_=stat6[:]).then_inc(s_p0v.h, 1)
                s_p0v.v += 1
                vector.wait_ge(s_rst.h, i + 1)
                if i >= 1:
                    vector.wait_ge(s_p0t.h, 4 * i)
                vector.tensor_scalar(out=h0t[:], in0=xt[:], scalar1=stat2[:, 0:1],
                                     scalar2=lnt2[:], op0=ALU.subtract, op1=ALU.mult).then_inc(s_p0v.h, 1)
                s_p0v.v += 1
                for g in range(4):
                    vector.wait_ge(s_p0t.h, 4 * i + g + 1)
                    if g == 0 and i >= 1:
                        vector.wait_ge(s_p0s.h, 16 * i)
                    vector.tensor_copy(out=h0stg[:, 1024 * g:1024 * (g + 1)],
                                       in_=pb[g % 2][:]).then_inc(s_p0e.h, 1)
                    s_p0e.v += 1

        @block.scalar
        def _(scalar):
            for i in range(NT0):
                scalar.wait_ge(s_p0v.h, 2 * i + 1)
                sedge(scalar, scalar.activation(out=lnt1[:], in_=stat2[:, 1:2], func=AF.Ln, bias=epsv[:]))
                scalar.activation(out=lnt2[:], in_=lnt1[:], func=AF.Exp, scale=-0.5).then_inc(s_rst.h, 1)
                s_rst.v += 1

        @block.tensor
        def _(tensor):
            tensor.wait_ge(s_init.h, INIT_DONE)
            for i in range(NT0):
                tensor.wait_ge(s_p0v.h, 2 * i + 2)
                for g in range(4):
                    gg = 4 * i + g
                    if gg >= 2:
                        tensor.wait_ge(s_p0e.h, gg - 1)
                    for k in range(8):
                        ch = 8 * g + k
                        inst = tensor.transpose(pb[g % 2][:, 128 * k:128 * (k + 1)],
                                                h0t[:, 128 * ch:128 * (ch + 1)], identb[:])
                    inst.then_inc(s_p0t.h, 1)
                    s_p0t.v += 1

        @block.gpsimd
        def _(gpsimd):
            for i in range(NT0):
                gpsimd.wait_ge(s_p0e.h, 4 * (i + 1))
                dstap = bass.AP(h0slab.tensor, 128 * i, [[TPC, 128], [128 * TPC, 32], [1, 128]])
                gpsimd.dma_start(out=dstap,
                                 in_=h0stg.ap().rearrange("p (j t) -> p j t", t=128)).then_inc(s_p0s.h, 16)
                s_p0s.v += 16
            gpsimd.wait_ge(s_p0s.h, s_p0s.v)
            gpsimd.collective_compute("AllGather", ALU.bypass, RG, [h0slab.opt()], [h0g.opt()]).then_inc(s_ag0.h, 1)
            s_ag0.v += 1

        @block.gpsimd
        def _(gpsimd):
            gpsimd.wait_ge(s_init.h, 80)
            dq_chunk(gpsimd, *DQ_W_A)

        @block.vector
        def _(vector):
            dq_vec(vector, *DQ_W_A)

        FAB = 3

        @block.vector
        def _(vector):
            vector.tensor_copy(out=fscr[:], in_=epsv[:]).then_inc(s_fab.h, 1)

        @block.scalar
        def _(scalar):
            scalar.activation(out=fsc2[:], in_=epsv[:], func=AF.Copy).then_inc(s_fab.h, 1)

        @block.gpsimd
        def _(gpsimd):
            gpsimd.wait_ge(s_p0s.h, s_p0s.v)
            gpsimd.wait_ge(s_dqs[0].h, s_dqs[0].v)
            gpsimd.wait_ge(s_dqs[1].h, s_dqs[1].v)
            gpsimd.sem_inc(s_fab.h, 1)

        sA.close()
        sB = contextlib.ExitStack()
        eB = sB.enter_context
        wqsb = eB(nc.sbuf_tensor("p1_w", [128, NCH * 768], BF16))
        wqld = [eB(nc.sbuf_tensor(f"p1_wl{i}", [128, 768], F32)) for i in range(2)]
        h0rhs = [eB(nc.sbuf_tensor(f"p1_h{i}", [128, 512], BF16)) for i in range(4)]
        qkev = [eB(nc.sbuf_tensor(f"p1_e{i}", [128, 6 * 512], BF16)) for i in range(2)]

        @block.sync
        def _(sync):
            sync.wait_ge(s_fab.h, FAB)
            for sw in range(2):
                for j in range(NCH):
                    cidx = sw * NCH + j
                    if cidx >= 2:
                        sync.wait_ge(s_p1wc.h, cidx - 1)
                    sp = s_p1w[cidx % 2]
                    sync.dma_start(out=wqld[cidx % 2][:],
                                   in_=wqkv[128 * j:128 * (j + 1), 768 * sw:768 * (sw + 1)]).then_inc(sp.h, 16)
                    sp.v += 16
                if sw == 0:
                    sync.wait_ge(s_ag0.h, 1)
                for m in range(NQB):
                    for j in range(NCH):
                        idx = (sw * NQB + m) * NCH + j
                        if idx >= 4:
                            sync.wait_ge(s_p1j.h, idx - 3)
                        rk = m // 2
                        tl = 512 * (m % 2)
                        src = h0g[D * rk + 128 * j:D * rk + 128 * (j + 1), tl:tl + 512]
                        sp = s_p1h[idx % 4]
                        sync.dma_start(out=h0rhs[idx % 4][:], in_=src).then_inc(sp.h, 16)
                        sp.v += 16

        @block.vector
        def _(vector):
            vector.wait_ge(s_fab.h, FAB)
            for sw in range(2):
                if sw == 1:
                    vector.wait_ge(s_p1j.h, NCH * NQB)
                for j in range(NCH):
                    cidx = sw * NCH + j
                    vector.wait_ge(s_p1w[cidx % 2].h, 16 * (cidx // 2 + 1))
                    vector.tensor_copy(out=wqsb[:, 768 * j:768 * (j + 1)],
                                       in_=wqld[cidx % 2][:]).then_inc(s_p1wc.h, 1)
                    s_p1wc.v += 1

        @block.tensor
        def _(tensor):
            tensor.wait_ge(s_fab.h, FAB)
            for sw in range(2):
                tensor.wait_ge(s_p1wc.h, NCH * (sw + 1))
                for m in range(NQB):
                    gidx = sw * NQB + m
                    if gidx >= 1:
                        tensor.wait_ge(s_p1e.h, 6 * gidx)
                    for j in range(NCH):
                        idx = gidx * NCH + j
                        tensor.wait_ge(s_p1h[idx % 4].h, 16 * (idx // 4 + 1))
                        for q6 in range(6):
                            inst = tensor.matmul(pf[q6][:],
                                                 wqsb[:, 768 * j + 128 * q6:768 * j + 128 * (q6 + 1)],
                                                 h0rhs[idx % 4][:], start=(j == 0), stop=(j == NCH - 1))
                        inst.then_inc(s_p1j.h, 1)
                        s_p1j.v += 1

        @block.scalar
        def _(scalar):
            for sw in range(2):
                for m in range(NQB):
                    gidx = sw * NQB + m
                    scalar.wait_ge(s_p1j.h, NCH * (gidx + 1))
                    if gidx >= 2:
                        scalar.wait_ge(s_p1s[gidx % 2].h, 96 * (gidx // 2))
                    for q6 in range(6):
                        sq = 3 * (2 * sw + q6 // 3) + (q6 % 3)
                        scalar.activation(out=qkev[gidx % 2][:, 512 * q6:512 * (q6 + 1)], in_=pf[q6][:],
                                          func=AF.Identity, bias=sbq[:, sq:sq + 1],
                                          scale=1.0).then_inc(s_p1e.h, 1)
                        s_p1e.v += 1

        @block.gpsimd
        def _(gpsimd):
            for sw in range(2):
                for m in range(NQB):
                    gidx = sw * NQB + m
                    gpsimd.wait_ge(s_p1e.h, 6 * (gidx + 1))
                    sp = s_p1s[gidx % 2]
                    for q6 in range(6):
                        sq = 3 * (2 * sw + q6 // 3) + (q6 % 3)
                        gpsimd.dma_start(out=qkvd[sq, :, 512 * m:512 * (m + 1)],
                                         in_=qkev[gidx % 2][:, 512 * q6:512 * (q6 + 1)]).then_inc(sp.h, 16)
                        sp.v += 16

        @block.gpsimd
        def _(gpsimd):
            dq_chunk(gpsimd, *DQ_W_B)

        @block.vector
        def _(vector):
            dq_vec(vector, *DQ_W_B)

        FBC = 3

        @block.vector
        def _(vector):
            vector.tensor_copy(out=fscr[:], in_=epsv[:]).then_inc(s_fbc.h, 1)

        @block.scalar
        def _(scalar):
            scalar.activation(out=fsc2[:], in_=epsv[:], func=AF.Copy).then_inc(s_fbc.h, 1)

        @block.gpsimd
        def _(gpsimd):
            gpsimd.wait_ge(s_p1s[0].h, s_p1s[0].v)
            gpsimd.wait_ge(s_p1s[1].h, s_p1s[1].v)
            gpsimd.wait_ge(s_dqs[0].h, s_dqs[0].v)
            gpsimd.wait_ge(s_dqs[1].h, s_dqs[1].v)
            gpsimd.sem_inc(s_fbc.h, 1)

        sB.close()
        sC = contextlib.ExitStack()
        eC = sC.enter_context
        spatt = eC(nc.sbuf_tensor("p2_patt", [128, 20 * 512], BF16))
        kf = eC(nc.sbuf_tensor("p2_k", [128, S], BF16))
        vf = eC(nc.sbuf_tensor("p2_v", [128, S], BF16))
        qf = eC(nc.sbuf_tensor("p2_q", [128, S], BF16))
        vt = eC(nc.sbuf_tensor("p2_vt", [128, S], BF16))
        ptx = [eC(nc.sbuf_tensor(f"p2_px{i}", [128, 512], BF16)) for i in range(4)]
        ptm = [eC(nc.sbuf_tensor(f"p2_pm{i}", [128, 512], BF16)) for i in range(4)]
        osb = eC(nc.sbuf_tensor("p2_osb", [128, 512], BF16))
        rsb = eC(nc.sbuf_tensor("p2_rsb", [128, 512], F32))
        onb = eC(nc.sbuf_tensor("p2_onb", [128, 512], BF16))

        @block.sync
        def _(sync):
            sync.wait_ge(s_fbc.h, FBC)
            sync.dma_start(out=spatt[:],
                           in_=patt.rearrange("(v p) q -> p v q", p=128)).then_inc(s_patt.h, 16)
            s_patt.v += 16
            for s_ in range(NSLOT):
                if s_ > 0:
                    gend = BLOCK_RANGE[(s_ - 1, NQB - 1)][1]
                    sync.wait_ge(s_sAV.h, gend)
                    sync.wait_ge(s_vte.h, 8 * s_)
                sync.dma_start(out=kf[:], in_=qkvd[3 * s_ + 1, :, :]).then_inc(s_kf.h, 16)
                sync.dma_start(out=vf[:], in_=qkvd[3 * s_ + 2, :, :]).then_inc(s_vf.h, 16)
                sync.dma_start(out=qf[:], in_=qkvd[3 * s_ + 0, :, :]).then_inc(s_qf.h, 16)
                s_kf.v += 16; s_vf.v += 16; s_qf.v += 16

        for s_ in range(NSLOT):
            def make_tensor(s_=s_):
                def emit(tensor):
                    if s_ == 0:
                        tensor.wait_ge(s_fbc.h, FBC)
                    tensor.wait_ge(s_vf.h, 16 * (s_ + 1))
                    for gvt in range(8):
                        gg = 8 * s_ + gvt
                        if gg >= 2:
                            tensor.wait_ge(s_vte.h, gg - 1)
                        for k in range(8):
                            ch = 8 * gvt + k
                            inst = tensor.transpose(pb[gvt % 2][:, 128 * k:128 * (k + 1)],
                                                    vf[:, 128 * ch:128 * (ch + 1)], identb[:])
                        inst.then_inc(s_vt.h, 1)
                        s_vt.v += 1
                    tensor.wait_ge(s_kf.h, 16 * (s_ + 1))
                    tensor.wait_ge(s_qf.h, 16 * (s_ + 1))
                    tensor.wait_ge(s_patt.h, 16)
                    tensor.wait_ge(s_vte.h, 8 * (s_ + 1))
                    LA = 3
                    gs0 = BLOCK_RANGE[(s_, 0)][0]
                    gs1 = BLOCK_RANGE[(s_, NQB - 1)][1]

                    def emit_S(gi):
                        sx, bx, ux = STEP_U[gi]
                        k0x = QB * bx + 128 * ux
                        sbank = [pf[0], pf[1], pf[4], pf[5]][gi % 4]
                        if gi >= 4:
                            tensor.wait_ge(s_sX.h, gi - 3)
                        inst = tensor.matmul(sbank[:], kf[:, k0x:k0x + 128],
                                             qf[:, QB * bx:QB * (bx + 1)], start=True, stop=True)
                        inst.then_inc(s_sS.h, 1)
                        s_sS.v += 1

                    for gpre in range(gs0, min(gs0 + LA, gs1)):
                        emit_S(gpre)
                    for b_ in range(NQB):
                        g0, g1 = BLOCK_RANGE[(s_, b_)]
                        blkidx = s_ * NQB + b_
                        if blkidx >= 1:
                            tensor.wait_ge(s_sR.h, blkidx)
                            tensor.wait_ge(s_sE.h, blkidx)
                        for i, gi in enumerate(range(g0, g1)):
                            if gi + LA < gs1:
                                emit_S(gi + LA)
                            u_ = STEP_U[gi][2]
                            k0 = QB * b_ + 128 * u_
                            tensor.wait_ge(s_sM.h, gi + 1)
                            tensor.matmul(pf[2][:], onesb[:], ptm[gi % 4][:],
                                          start=(i == 0), stop=(i == g1 - g0 - 1))
                            ct = k0 // 128
                            inst = tensor.matmul(pf[3][:], vt[:, 128 * ct:128 * (ct + 1)],
                                                 ptm[gi % 4][:],
                                                 start=(i == 0), stop=(i == g1 - g0 - 1))
                            inst.then_inc(s_sAV.h, 1)
                            s_sAV.v += 1
                return emit
            block.tensor(make_tensor())

            def make_scalar(s_=s_):
                def emit(scalar):
                    for b_ in range(NQB):
                        g0, g1 = BLOCK_RANGE[(s_, b_)]
                        blkidx = s_ * NQB + b_
                        for i, gi in enumerate(range(g0, g1)):
                            u_ = STEP_U[gi][2]
                            scalar.wait_ge(s_sS.h, gi + 1)
                            if gi >= 4:
                                scalar.wait_ge(s_sM.h, gi - 3)
                            iu = 3 - u_
                            sbank = [pf[0], pf[1], pf[4], pf[5]][gi % 4]
                            scalar.activation(out=ptx[gi % 4][:], in_=sbank[:], func=AF.Exp,
                                              bias=sebias[:, 64 * s_ + iu:64 * s_ + iu + 1],
                                              scale=1.0 / 128).then_inc(s_sX.h, 1)
                            s_sX.v += 1
                        scalar.wait_ge(s_sAV.h, g1)
                        if blkidx >= 1:
                            scalar.wait_ge(s_sON.h, blkidx)
                        scalar.activation(out=osb[:], in_=pf[3][:], func=AF.Copy).then_inc(s_sE.h, 1)
                        s_sE.v += 1
                return emit
            block.scalar(make_scalar())

            def make_vector(s_=s_):
                def emit(vector):
                    if s_ == 0:
                        vector.wait_ge(s_patt.h, 16)
                    for gvt in range(8):
                        vector.wait_ge(s_vt.h, 8 * s_ + gvt + 1)
                        vector.tensor_copy(out=vt[:, 1024 * gvt:1024 * (gvt + 1)],
                                           in_=pb[gvt % 2][:]).then_inc(s_vte.h, 1)
                        s_vte.v += 1
                    for b_ in range(NQB):
                        g0, g1 = BLOCK_RANGE[(s_, b_)]
                        blkidx = s_ * NQB + b_
                        for i, gi in enumerate(range(g0, g1)):
                            u_ = STEP_U[gi][2]
                            vector.wait_ge(s_sX.h, gi + 1)
                            if gi >= 4:
                                vector.wait_ge(s_sAV.h, gi - 3)
                            v_ = u_ if 0 <= u_ else 4
                            pslice = spatt[:, 512 * (5 * s_ + v_):512 * (5 * s_ + v_ + 1)]
                            vector.tensor_tensor(out=ptm[gi % 4][:], in0=ptx[gi % 4][:],
                                                 in1=pslice, op=ALU.mult).then_inc(s_sM.h, 1)
                            s_sM.v += 1
                        vector.wait_ge(s_sAV.h, g1)
                        vector.reciprocal(out=rsb[:], in_=pf[2][:]).then_inc(s_sR.h, 1)
                        s_sR.v += 1
                        vector.wait_ge(s_sR.h, s_sR.v)
                        vector.wait_ge(s_sE.h, blkidx + 1)
                        if blkidx >= 1:
                            vector.wait_ge(s_oas.h, 16 * blkidx)
                        vector.tensor_tensor(out=onb[:], in0=osb[:], in1=rsb[:],
                                             op=ALU.mult).then_inc(s_sON.h, 1)
                        s_sON.v += 1
                return emit
            block.vector(make_vector())

            def make_gp(s_=s_):
                def emit(gpsimd):
                    for b_ in range(NQB):
                        blkidx = s_ * NQB + b_
                        gpsimd.wait_ge(s_sON.h, blkidx + 1)
                        gpsimd.dma_start(out=oaslab[128 * s_:128 * (s_ + 1), QB * b_:QB * (b_ + 1)],
                                         in_=onb[:]).then_inc(s_oas.h, 16)
                        s_oas.v += 16
                return emit
            block.gpsimd(make_gp())

            lo, hi = DQ_W_S[s_]

            def mk_dql(lo=lo, hi=hi, last=(s_ == NSLOT - 1)):
                def emit(gpsimd):
                    dq_chunk(gpsimd, lo, hi, flush=last)
                return emit
            block.gpsimd(mk_dql())

            def mk_dqv(lo=lo, hi=hi):
                def emit(vector):
                    dq_vec(vector, lo, hi)
                return emit
            block.vector(mk_dqv())



        FCD = 3

        @block.vector
        def _(vector):
            vector.tensor_copy(out=fscr[:], in_=epsv[:]).then_inc(s_fcd.h, 1)

        @block.scalar
        def _(scalar):
            scalar.activation(out=fsc2[:], in_=epsv[:], func=AF.Copy).then_inc(s_fcd.h, 1)

        @block.gpsimd
        def _(gpsimd):
            gpsimd.wait_ge(s_oas.h, s_oas.v)
            gpsimd.wait_ge(s_dqs[0].h, s_dqs[0].v)
            gpsimd.wait_ge(s_dqs[1].h, s_dqs[1].v)
            gpsimd.sem_inc(s_fcd.h, 1)

        sC.close()
        sDQ.close()
        sD = contextlib.ExitStack()
        eD = sD.enter_context
        wpld = [eD(nc.sbuf_tensor(f"p3_wl{i}", [128, D], F32)) for i in range(2)]
        wpcb = eD(nc.sbuf_tensor("p3_w", [128, NSLOT * D], BF16))
        oalhs = [eD(nc.sbuf_tensor(f"p3_o{i}", [128, NSLOT * 512], BF16)) for i in range(2)]
        yev = [eD(nc.sbuf_tensor(f"p3_y{i}", [128, 512], BF16)) for i in range(2)]
        bprep = eD(nc.sbuf_tensor("p3_bp", [128, D], F32))
        ylt = eD(nc.sbuf_tensor("p3_yl", [128, D], BF16))
        ybt = eD(nc.sbuf_tensor("p3_yb", [128, D], F32))
        lnb = eD(nc.sbuf_tensor("p3_ln", [128, D], F32))
        xt3 = eD(nc.sbuf_tensor("p3_x", [128, D], F32))
        x1b = eD(nc.sbuf_tensor("p3_x1", [128, D], F32))
        h2t = eD(nc.sbuf_tensor("p3_h2", [128, D], BF16))
        h2stg = eD(nc.sbuf_tensor("p3_h2s", [128, D], BF16))

        P3_TILES = [(r, q, 0) for r in range(NC) for q in range(4)] + \
                   [(r, q, 1) for r in range(NC) for q in range(4)]

        @block.sync
        def _(sync):
            sync.wait_ge(s_fcd.h, FCD)
            brep = bass.AP(bproj.tensor, 0, [[0, 128], [1, D]])
            sync.dma_start(out=bprep[:], in_=brep).then_inc(s_bp.h, 16)
            s_bp.v += 16
            for sl in range(NSLOT):
                if sl >= 2:
                    sync.wait_ge(s_p3wc.h, sl - 1)
                sp = s_p3w[sl % 2]
                sync.dma_start(out=wpld[sl % 2][:],
                               in_=wprojc[128 * sl:128 * (sl + 1), :]).then_inc(sp.h, 16)
                sp.v += 16
            for lg in range(16):
                r, hf = lg % 8, lg // 8
                if lg >= 2:
                    sync.wait_ge(s_p3m.h, 8 * 4 * (lg - 1))
                src = bass.AP(oaslab.tensor, 1024 * r + 512 * hf,
                              [[S, 128], [128 * S, NSLOT], [1, 512]])
                sp = s_p3l[lg % 2]
                sync.dma_start(out=oalhs[lg % 2].ap().rearrange("p (s t) -> p s t", t=512),
                               in_=src).then_inc(sp.h, 16)
                sp.v += 16

        @block.vector
        def _(vector):
            vector.wait_ge(s_fcd.h, FCD)
            for sl in range(NSLOT):
                vector.wait_ge(s_p3w[sl % 2].h, 16 * (sl // 2 + 1))
                vector.tensor_copy(out=wpcb[:, D * sl:D * (sl + 1)],
                                   in_=wpld[sl % 2][:]).then_inc(s_p3wc.h, 1)
                s_p3wc.v += 1

        @block.tensor
        def _(tensor):
            tensor.wait_ge(s_fcd.h, FCD)
            tensor.wait_ge(s_p3wc.h, NSLOT)
            for ti, (r, q, hf) in enumerate(P3_TILES):
                lg = 8 * hf + r
                tensor.wait_ge(s_p3l[lg % 2].h, 16 * (lg // 2 + 1))
                for ec in range(8):
                    eidx = 8 * ti + ec
                    if eidx >= 4:
                        tensor.wait_ge(s_p3e.h, eidx - 3)
                    for sl in range(NSLOT):
                        inst = tensor.matmul(pf[ec % 4][:],
                                             oalhs[lg % 2][:, 512 * sl + 128 * q:512 * sl + 128 * (q + 1)],
                                             wpcb[:, D * sl + 512 * ec:D * sl + 512 * (ec + 1)],
                                             start=(sl == 0), stop=(sl == NSLOT - 1))
                    inst.then_inc(s_p3m.h, 1)
                    s_p3m.v += 1

        @block.scalar
        def _(scalar):
            for ti in range(len(P3_TILES)):
                for ec in range(8):
                    eidx = 8 * ti + ec
                    scalar.wait_ge(s_p3m.h, eidx + 1)
                    if eidx >= 2:
                        scalar.wait_ge(s_p3s[eidx % 2].h, 16 * (eidx // 2))
                    scalar.activation(out=yev[eidx % 2][:], in_=pf[ec % 4][:],
                                      func=AF.Copy).then_inc(s_p3e.h, 1)
                    s_p3e.v += 1

        @block.gpsimd
        def _(gpsimd):
            for ti, (r, q, hf) in enumerate(P3_TILES):
                ydst = y1dA if hf == 0 else y1dB
                for ec in range(8):
                    eidx = 8 * ti + ec
                    gpsimd.wait_ge(s_p3e.h, eidx + 1)
                    sp = s_p3s[eidx % 2]
                    gpsimd.dma_start(out=ydst[QB * r + 128 * q:QB * r + 128 * (q + 1),
                                              512 * ec:512 * (ec + 1)],
                                     in_=yev[eidx % 2][:]).then_inc(sp.h, 16)
                    sp.v += 16
                if ti == NC * 4 - 1:
                    gpsimd.wait_ge(s_p3s[0].h, 16 * 4 * NC * 4)
                    gpsimd.wait_ge(s_p3s[1].h, 16 * 4 * NC * 4)
                    gpsimd.collective_compute("ReduceScatter", ALU.add, RG,
                                              [y1dA.opt()], [y1rA.opt()]).then_inc(s_rsa1.h, 1)
                    s_rsa1.v += 1
            gpsimd.wait_ge(s_p3s[0].h, 16 * 4 * NC * 8)
            gpsimd.wait_ge(s_p3s[1].h, 16 * 4 * NC * 8)
            gpsimd.collective_compute("ReduceScatter", ALU.add, RG,
                                      [y1dB.opt()], [y1rB.opt()]).then_inc(s_rsb1.h, 1)
            s_rsb1.v += 1

        @block.sync
        def _(sync):
            for hf in range(2):
                sync.wait_ge((s_rsa1 if hf == 0 else s_rsb1).h, 1)
                ysrc = y1rA if hf == 0 else y1rB
                for q in range(4):
                    rowi = 4 * hf + q
                    if rowi >= 1:
                        sync.wait_ge(s_r5v.h, 5 * rowi)
                    sync.dma_start(out=ylt[:], in_=ysrc[128 * q:128 * (q + 1), :]).then_inc(s_r5l.h, 16)
                    sync.dma_start(out=xt3[:],
                                   in_=xrows[QB * hf + 128 * q:QB * hf + 128 * (q + 1), :]).then_inc(s_r5l.h, 16)
                    s_r5l.v += 32

        @block.vector
        def _(vector):
            for hf in range(2):
                for q in range(4):
                    rowi = 4 * hf + q
                    vector.wait_ge(s_r5l.h, 32 * (rowi + 1))
                    if rowi == 0:
                        vector.wait_ge(s_bp.h, 16)
                    if rowi >= 1:
                        vector.wait_ge(s_r5r.h, 2 * rowi)
                    vedge(vector, vector.tensor_tensor(out=ybt[:], in0=ylt[:], in1=bprep[:], op=ALU.add))
                    for ch in range(8):
                        inst = vector.bn_stats(out=stat6[:, 6 * ch:6 * ch + 6], in_=ybt[:, 512 * ch:512 * (ch + 1)])
                    vedge(vector, inst)
                    vector.bn_aggr(out=stat2[:], in_=stat6[:]).then_inc(s_r5v.h, 1)
                    vector.wait_ge(s_r5r.h, 2 * rowi + 1)
                    vector.tensor_scalar(out=lnb[:], in0=ybt[:], scalar1=stat2[:, 0:1],
                                         scalar2=lnt2[:], op0=ALU.subtract, op1=ALU.mult).then_inc(s_r5v.h, 1)
                    vector.wait_ge(s_r5v.h, 5 * rowi + 2)
                    if rowi >= 1:
                        vector.wait_ge(s_x1s.h, 16 * rowi)
                    vector.tensor_tensor(out=x1b[:], in0=xt3[:], in1=lnb[:], op=ALU.add).then_inc(s_r5v.h, 1)
                    vector.wait_ge(s_r5v.h, 5 * rowi + 3)
                    for ch in range(8):
                        inst = vector.bn_stats(out=stat6[:, 6 * ch:6 * ch + 6], in_=x1b[:, 512 * ch:512 * (ch + 1)])
                    vedge(vector, inst)
                    vector.bn_aggr(out=stat2[:], in_=stat6[:]).then_inc(s_r5v.h, 1)
                    vector.wait_ge(s_r5r.h, 2 * rowi + 2)
                    if rowi >= 1:
                        vector.wait_ge(s_h2t.h, 4 * rowi)
                    vector.tensor_scalar(out=h2t[:], in0=x1b[:], scalar1=stat2[:, 0:1],
                                         scalar2=lnt2[:], op0=ALU.subtract, op1=ALU.mult).then_inc(s_r5v.h, 1)
                    s_r5v.v += 5
                    for g in range(4):
                        gg = 4 * rowi + g
                        vector.wait_ge(s_h2t.h, gg + 1)
                        if g == 0 and rowi >= 1:
                            if rowi <= 4:
                                vector.wait_ge(s_h2sA.h, 16 * min(rowi, 4))
                            if rowi > 4:
                                vector.wait_ge(s_h2sB.h, 16 * (rowi - 4))
                        vector.tensor_copy(out=h2stg[:, 1024 * g:1024 * (g + 1)],
                                           in_=pb[g % 2][:]).then_inc(s_h2e.h, 1)
                        s_h2e.v += 1

        @block.scalar
        def _(scalar):
            for rowi in range(8):
                for k2 in range(2):
                    scalar.wait_ge(s_r5v.h, 5 * rowi + (1 if k2 == 0 else 4))
                    sedge(scalar, scalar.activation(out=lnt1[:], in_=stat2[:, 1:2], func=AF.Ln, bias=epsv[:]))
                    scalar.activation(out=lnt2[:], in_=lnt1[:], func=AF.Exp, scale=-0.5).then_inc(s_r5r.h, 1)
                    s_r5r.v += 1

        @block.tensor
        def _(tensor):
            for rowi in range(8):
                tensor.wait_ge(s_r5v.h, 5 * rowi + 5)
                for g in range(4):
                    gg = 4 * rowi + g
                    if gg >= 2:
                        tensor.wait_ge(s_h2e.h, gg - 1)
                    for k in range(8):
                        ch = 8 * g + k
                        inst = tensor.transpose(pb[g % 2][:, 128 * k:128 * (k + 1)],
                                                h2t[:, 128 * ch:128 * (ch + 1)], identb[:])
                    inst.then_inc(s_h2t.h, 1)
                    s_h2t.v += 1

        @block.gpsimd
        def _(gpsimd):
            for hf in range(2):
                h2dst = h2slabA if hf == 0 else h2slabB
                sh2 = s_h2sA if hf == 0 else s_h2sB
                for q in range(4):
                    rowi = 4 * hf + q
                    gpsimd.wait_ge(s_r5v.h, 5 * rowi + 3)
                    gpsimd.dma_start(out=x1d[QB * hf + 128 * q:QB * hf + 128 * (q + 1), :],
                                     in_=x1b[:]).then_inc(s_x1s.h, 16)
                    s_x1s.v += 16
                    gpsimd.wait_ge(s_h2e.h, 4 * (rowi + 1))
                    dstap = bass.AP(h2dst.tensor, 128 * q, [[QB, 128], [128 * QB, 32], [1, 128]])
                    gpsimd.dma_start(out=dstap,
                                     in_=h2stg.ap().rearrange("p (j t) -> p j t", t=128)).then_inc(sh2.h, 16)
                    sh2.v += 16

        @block.gpsimd
        def _(gpsimd):
            gpsimd.wait_ge(s_h2sA.h, 16 * 4)
            gpsimd.collective_compute("AllGather", ALU.bypass, RG,
                                      [h2slabA.opt()], [h2gA.opt()]).then_inc(s_agh2a.h, 1)
            s_agh2a.v += 1
            gpsimd.wait_ge(s_h2sB.h, 16 * 4)
            gpsimd.collective_compute("AllGather", ALU.bypass, RG,
                                      [h2slabB.opt()], [h2gB.opt()]).then_inc(s_agh2b.h, 1)
            s_agh2b.v += 1

        FDE = 3

        @block.vector
        def _(vector):
            vector.tensor_copy(out=fscr[:], in_=epsv[:]).then_inc(s_fde.h, 1)

        @block.scalar
        def _(scalar):
            scalar.activation(out=fsc2[:], in_=epsv[:], func=AF.Copy).then_inc(s_fde.h, 1)

        @block.gpsimd
        def _(gpsimd):
            gpsimd.wait_ge(s_x1s.h, s_x1s.v)
            gpsimd.wait_ge(s_h2sA.h, s_h2sA.v)
            gpsimd.wait_ge(s_h2sB.h, s_h2sB.v)
            gpsimd.sem_inc(s_fde.h, 1)

        sD.close()
        sM.close()
        sE = contextlib.ExitStack()
        eE = sE.enter_context
        h2sb = eE(nc.sbuf_tensor("f_h2", [128, NCH * 1024], BF16))
        wgsb0 = eE(nc.sbuf_tensor("f_wg", [128, NCH * 128], BF16))
        wusb0 = eE(nc.sbuf_tensor("f_wu", [128, NCH * 128], BF16))
        wgsb = [wgsb0, wgsb0]
        wusb = [wusb0, wusb0]
        wdsb = eE(nc.sbuf_tensor("f_wd", [128, NNT * D], BF16))
        intr = eE(nc.sbuf_tensor("f_int", [128, NNT * 1024], BF16))
        silb = [eE(nc.sbuf_tensor(f"f_sil{i}", [128, 512], BF16)) for i in range(2)]
        y2ev = silb

        NGG = NNT * 4
        NDG = 64

        @block.sync
        def _(sync):
            sync.wait_ge(s_fde.h, FDE)
            for i in range(8):
                hf = i // 4
                rp = i % 4
                h2src = h2gA if hf == 0 else h2gB
                sync.wait_ge((s_agh2a if hf == 0 else s_agh2b).h, 1)
                if i >= 1:
                    sync.wait_ge(s_fg.h, NGG * i)
                for rr in range(2):
                    srcap = bass.AP(h2src.tensor, D * QB * (2 * rp + rr),
                                    [[QB, 128], [128 * QB, NCH], [1, QB]])
                    dstap = h2sb.ap().rearrange("p (j rt) -> p j rt", rt=1024)[:, :, 512 * rr:512 * (rr + 1)]
                    sync.dma_start(out=dstap, in_=srcap).then_inc(s_fh.h, 16)
                s_fh.v += 32
                for nt in range(NNT):
                    nw = NT_SIZES[nt]
                    widx = i * NNT + nt
                    if widx >= 1:
                        sync.wait_ge(s_fg.h, 4 * widx - 2)
                    gsrc = bass.AP(wgbf.tensor, NT_OFF[nt],
                                   [[NCH * NSL, 128], [NSL, NCH], [1, nw]])
                    sync.dma_start(out=wgsb[widx % 2][:, 0:NCH * nw].rearrange("p (j n) -> p j n", n=nw),
                                   in_=gsrc).then_inc(s_fwg.h, 16)
                    s_fwg.v += 16
                    if widx >= 1:
                        sync.wait_ge(s_fg.h, 4 * widx)
                    usrc = bass.AP(wubf.tensor, NT_OFF[nt],
                                   [[NCH * NSL, 128], [NSL, NCH], [1, nw]])
                    sync.dma_start(out=wusb[widx % 2][:, 0:NCH * nw].rearrange("p (j n) -> p j n", n=nw),
                                   in_=usrc).then_inc(s_fwu.h, 16)
                    s_fwu.v += 16
                    if i >= 1 and nt == 0:
                        sync.wait_ge(s_fy.h, NDG * i)
                    sync.dma_start(out=wdsb[0:nw, D * nt:D * (nt + 1)],
                                   in_=wdbf[128 * nt:128 * nt + nw, :]).then_inc(s_fwd.h, 16)
                    s_fwd.v += 16

        @block.tensor
        def _(tensor):
            tensor.wait_ge(s_fde.h, FDE)
            for i in range(8):
                tensor.wait_ge(s_fh.h, 32 * (i + 1))
                for nt in range(NNT):
                    nw = NT_SIZES[nt]
                    widx = i * NNT + nt
                    tensor.wait_ge(s_fwg.h, 16 * (widx + 1))
                    tensor.wait_ge(s_fwu.h, 16 * (widx + 1))
                    for mat in range(2):
                        wsb = (wgsb if mat == 0 else wusb)[widx % 2]
                        for rr in range(2):
                            bank = pf[2 * mat + rr]
                            if widx >= 1:
                                if mat == 0:
                                    tensor.wait_ge(s_fsil.h, 2 * widx - 2 + rr + 1)
                                else:
                                    tensor.wait_ge(s_fint.h, 2 * widx - 2 + rr + 1)
                            for j in range(NCH):
                                inst = tensor.matmul(bank[0:nw, 0:512],
                                                     wsb[:, nw * j:nw * j + nw],
                                                     h2sb[:, 1024 * j + 512 * rr:1024 * j + 512 * (rr + 1)],
                                                     start=(j == 0), stop=(j == NCH - 1))
                            inst.then_inc(s_fg.h, 1)
                            s_fg.v += 1
                tensor.wait_ge(s_fint.h, 2 * NNT * (i + 1))
                tensor.wait_ge(s_fwd.h, 16 * NNT * (i + 1))
                for ts in range(8):
                    for ec in range(8):
                        didx = NDG * i + 8 * ts + ec
                        if didx >= 2:
                            tensor.wait_ge(s_fye.h, didx - 1)
                        for nt in range(NNT):
                            nw = NT_SIZES[nt]
                            inst = tensor.matmul(pf[4 + didx % 2][:],
                                                 intr[0:nw, 1024 * nt + 128 * ts:1024 * nt + 128 * (ts + 1)],
                                                 wdsb[0:nw, D * nt + 512 * ec:D * nt + 512 * (ec + 1)],
                                                 start=(nt == 0), stop=(nt == NNT - 1))
                        inst.then_inc(s_fy.h, 1)
                        s_fy.v += 1

        @block.scalar
        def _(scalar):
            for i in range(8):
                if i >= 1:
                    scalar.wait_ge(s_fys[0].h, 16 * NDG * i // 2)
                    scalar.wait_ge(s_fys[1].h, 16 * NDG * i // 2)
                for nt in range(NNT):
                    widx = i * NNT + nt
                    nw = NT_SIZES[nt]
                    for rr in range(2):
                        sidx = 2 * widx + rr
                        scalar.wait_ge(s_fg.h, 4 * widx + rr + 1)
                        if sidx >= 2:
                            scalar.wait_ge(s_fint.h, sidx - 1)
                        scalar.activation(out=silb[sidx % 2][0:nw, :], in_=pf[rr][0:nw, 0:512],
                                          func=AF.Silu).then_inc(s_fsil.h, 1)
                        s_fsil.v += 1
                for ts in range(8):
                    for ec in range(8):
                        didx = NDG * i + 8 * ts + ec
                        scalar.wait_ge(s_fy.h, didx + 1)
                        if didx >= 2:
                            scalar.wait_ge(s_fys[didx % 2].h, 16 * (didx // 2))
                        scalar.activation(out=y2ev[didx % 2][:], in_=pf[4 + didx % 2][:],
                                          func=AF.Copy).then_inc(s_fye.h, 1)
                        s_fye.v += 1

        @block.vector
        def _(vector):
            vector.wait_ge(s_fde.h, FDE)
            for i in range(8):
                if i >= 1:
                    vector.wait_ge(s_fy.h, NDG * i)
                for nt in range(NNT):
                    widx = i * NNT + nt
                    nw = NT_SIZES[nt]
                    for rr in range(2):
                        sidx = 2 * widx + rr
                        vector.wait_ge(s_fsil.h, sidx + 1)
                        vector.wait_ge(s_fg.h, 4 * widx + 2 + rr + 1)
                        vector.tensor_tensor(out=intr[0:nw, 1024 * nt + 512 * rr:1024 * nt + 512 * (rr + 1)],
                                             in0=silb[sidx % 2][0:nw, :],
                                             in1=pf[2 + rr][0:nw, 0:512],
                                             op=ALU.mult).then_inc(s_fint.h, 1)
                        s_fint.v += 1

        @block.gpsimd
        def _(gpsimd):
            for i in range(8):
                hf = i // 4
                rp = i % 4
                ydst = y2dA if hf == 0 else y2dB
                for ts in range(8):
                    for ec in range(8):
                        didx = NDG * i + 8 * ts + ec
                        gpsimd.wait_ge(s_fye.h, didx + 1)
                        r_ = 2 * rp + ts // 4
                        tq = ts % 4
                        sp = s_fys[didx % 2]
                        gpsimd.dma_start(out=ydst[QB * r_ + 128 * tq:QB * r_ + 128 * (tq + 1),
                                                  512 * ec:512 * (ec + 1)],
                                         in_=y2ev[didx % 2][:]).then_inc(sp.h, 16)
                        sp.v += 16
                if i == 3:
                    gpsimd.wait_ge(s_fys[0].h, 16 * NDG * 2)
                    gpsimd.wait_ge(s_fys[1].h, 16 * NDG * 2)
                    gpsimd.collective_compute("ReduceScatter", ALU.add, RG,
                                              [y2dA.opt()], [y2rA.opt()]).then_inc(s_rsa2.h, 1)
                    s_rsa2.v += 1
            gpsimd.wait_ge(s_fys[0].h, 16 * NDG * 4)
            gpsimd.wait_ge(s_fys[1].h, 16 * NDG * 4)
            gpsimd.collective_compute("ReduceScatter", ALU.add, RG,
                                      [y2dB.opt()], [y2rB.opt()]).then_inc(s_rsb2.h, 1)
            s_rsb2.v += 1

        FEF = 3

        @block.vector
        def _(vector):
            vector.tensor_copy(out=fscr[:], in_=epsv[:]).then_inc(s_fef.h, 1)

        @block.scalar
        def _(scalar):
            scalar.activation(out=fsc2[:], in_=epsv[:], func=AF.Copy).then_inc(s_fef.h, 1)

        @block.gpsimd
        def _(gpsimd):
            gpsimd.wait_ge(s_fys[0].h, s_fys[0].v)
            gpsimd.wait_ge(s_fys[1].h, s_fys[1].v)
            gpsimd.sem_inc(s_fef.h, 1)

        sE.close()
        sF = contextlib.ExitStack()
        eF = sF.enter_context
        y5l = eF(nc.sbuf_tensor("p5_y", [128, D], BF16))
        x5l = eF(nc.sbuf_tensor("p5_x", [128, D], F32))
        ln5 = eF(nc.sbuf_tensor("p5_ln", [128, D], F32))
        dln = eF(nc.sbuf_tensor("p5_d", [128, D], F32))
        o5 = eF(nc.sbuf_tensor("p5_o", [128, D], F32))
        amx = eF(nc.sbuf_tensor("p5_amx", [128, 1], F32))
        msk = eF(nc.sbuf_tensor("p5_msk", [128, 1], F32))
        rst2 = eF(nc.sbuf_tensor("p5_rst2", [128, 1], F32))

        @block.sync
        def _(sync):
            sync.wait_ge(s_fef.h, FEF)
            for hf in range(2):
                sync.wait_ge((s_rsa2 if hf == 0 else s_rsb2).h, 1)
                ysrc = y2rA if hf == 0 else y2rB
                for q in range(4):
                    rowi = 4 * hf + q
                    if rowi >= 1:
                        sync.wait_ge(s_p5v.h, 3 * rowi)
                    sync.dma_start(out=y5l[:], in_=ysrc[128 * q:128 * (q + 1), :]).then_inc(s_p5l.h, 16)
                    sync.dma_start(out=x5l[:],
                                   in_=x1d[QB * hf + 128 * q:QB * hf + 128 * (q + 1), :]).then_inc(s_p5l.h, 16)
                    s_p5l.v += 32

        @block.vector
        def _(vector):
            vector.wait_ge(s_fef.h, FEF)
            for rowi in range(8):
                vector.wait_ge(s_p5l.h, 32 * (rowi + 1))
                if rowi >= 1:
                    vector.wait_ge(s_p5r.h, rowi)
                for ch in range(8):
                    inst = vector.bn_stats(out=stat6[:, 6 * ch:6 * ch + 6], in_=y5l[:, 512 * ch:512 * (ch + 1)])
                vedge(vector, inst)
                vector.bn_aggr(out=stat2[:], in_=stat6[:]).then_inc(s_p5v.h, 1)
                vector.wait_ge(s_p5v.h, 3 * rowi + 1)
                vedge(vector, vector.tensor_scalar(out=msk[:], in0=stat2[:, 1:2], scalar1=65520.0,
                                                   scalar2=None, op0=ALU.is_lt))
                vector.wait_ge(s_p5r.h, rowi + 1)
                vedge(vector, vector.tensor_tensor(out=rst2[:], in0=msk[:], in1=lnt2[:], op=ALU.mult))
                vector.tensor_scalar(out=ln5[:], in0=y5l[:], scalar1=stat2[:, 0:1],
                                     scalar2=rst2[:], op0=ALU.subtract, op1=ALU.mult).then_inc(s_p5v.h, 1)
                vector.wait_ge(s_p5v.h, 3 * rowi + 2)
                if rowi >= 1:
                    vector.wait_ge(s_p5s.h, 16 * rowi)
                vector.tensor_tensor(out=o5[:], in0=x5l[:], in1=ln5[:], op=ALU.add).then_inc(s_p5v.h, 1)
                s_p5v.v += 3

        @block.scalar
        def _(scalar):
            for rowi in range(8):
                scalar.wait_ge(s_p5v.h, 3 * rowi + 1)
                sedge(scalar, scalar.activation(out=lnt1[:], in_=stat2[:, 1:2], func=AF.Ln, bias=epsv[:]))
                scalar.activation(out=lnt2[:], in_=lnt1[:], func=AF.Exp, scale=-0.5).then_inc(s_p5r.h, 1)
                s_p5r.v += 1

        @block.gpsimd
        def _(gpsimd):
            for hf in range(2):
                for q in range(4):
                    rowi = 4 * hf + q
                    gpsimd.wait_ge(s_p5v.h, 3 * rowi + 3)
                    gpsimd.dma_start(out=out[QB * hf + 128 * q:QB * hf + 128 * (q + 1), :],
                                     in_=o5[:]).then_inc(s_p5s.h, 16)
                    s_p5s.v += 16
            if dbg:
                srcs = {"h0": h0g, "qkv": qkvd.rearrange("a p t -> (a p) t"), "oa": oaslab,
                        "y1a": y1rA, "x1": x1d, "h2a": h2gA, "h2b": h2gB,
                        "y2a": y2rA, "y2b": y2rB}
                for name, ap_ in dbg.items():
                    gpsimd.dma_start(out=ap_, in_=srcs[name]).then_inc(s_p5s.h, 16)
                    s_p5s.v += 16
            gpsimd.wait_ge(s_p5s.h, s_p5s.v)

        sF.close()

    return nc



def stage_inputs(inputs):
    x = np.asarray(inputs["L0_attn_hidden_states"]).reshape(S, D)
    w_attn = np.asarray(inputs["L0_attn_w_attn"])
    b_attn = np.asarray(inputs["L0_attn_b_attn"])
    w_proj = np.asarray(inputs["L0_attn_w_proj"])
    b_proj = np.asarray(inputs["L0_attn_b_proj"])
    gqw = np.asarray(inputs["L0_ffn_gate_qweight"])
    gsc = np.asarray(inputs["L0_ffn_gate_scales"])
    gzr = np.asarray(inputs["L0_ffn_gate_zeros"])
    uqw = np.asarray(inputs["L0_ffn_up_qweight"])
    usc = np.asarray(inputs["L0_ffn_up_scales"])
    uzr = np.asarray(inputs["L0_ffn_up_zeros"])
    dqw = np.asarray(inputs["L0_ffn_down_qweight"])
    dsc_ = np.asarray(inputs["L0_ffn_down_scales"])
    dzr = np.asarray(inputs["L0_ffn_down_zeros"])

    bf = ml_dtypes.bfloat16
    p = np.arange(128)
    shifts = (4 * (p % 8)).astype(np.int32).reshape(128, 1)
    identf = np.eye(128, dtype=np.float32)
    onesb = np.ones((128, 128), dtype=bf)
    g_of_m = None

    in_maps = []
    for c in range(NC):
        d = {}
        d["xrows"] = np.ascontiguousarray(x[TPC * c:TPC * (c + 1), :])
        cols = []
        bcols = []
        for s in range(NSLOT):
            h = 8 * s + c
            for g0, base in ((0, 0), (1, D), (2, 2 * D)):
                cols.append(w_attn[:, base + HD * h:base + HD * (h + 1)])
                bcols.append(b_attn[base + HD * h:base + HD * (h + 1)])
        d["wqkv"] = np.ascontiguousarray(np.concatenate(cols, axis=1))
        d["bqkvT"] = np.ascontiguousarray(np.stack(bcols, axis=1))
        d["wprojc"] = np.ascontiguousarray(
            np.concatenate([w_proj[HD * (8 * s + c):HD * (8 * s + c + 1), :] for s in range(NSLOT)], axis=0))
        d["bproj"] = np.ascontiguousarray(b_proj.reshape(1, D))

        pt = np.zeros((NSLOT * 5 * 128, QB), dtype=bf)
        eb = np.zeros((128, NSLOT * 64), dtype=np.float32)
        jj = np.arange(QB)
        for s in range(NSLOT):
            sig = float(SLOPES[8 * s + c])
            for v in range(4):
                arg = sig * (128.0 * v + p[:, None] - jj[None, :])
                m = (jj[None, :] >= 128 * v + p[:, None])
                with np.errstate(over="ignore", under="ignore"):
                    tile = np.where(m, np.exp(np.minimum(arg, 0.0)), 0.0)
                pt[(5 * s + v) * 128:(5 * s + v + 1) * 128, :] = tile.astype(bf)
            arg = sig * (p[:, None] - jj[None, :] - 128.0)
            with np.errstate(under="ignore"):
                pt[(5 * s + 4) * 128:(5 * s + 5) * 128, :] = np.exp(arg).astype(bf)
            for iu in range(64):
                u = 3 - iu
                if u <= -1:
                    eb[:, 64 * s + iu] = np.float32(max(128.0 * sig * (u + 1), -1e30))
        d["patt"] = pt
        d["ebias"] = eb
        d["shifts"] = shifts
        d["identf"] = identf
        d["onesb"] = onesb

        nsl0 = NSL * c
        d["gq"] = np.ascontiguousarray(gqw[:, nsl0:nsl0 + NSL])
        d["gs"] = np.ascontiguousarray(gsc[:, nsl0:nsl0 + NSL])
        d["gz"] = np.ascontiguousarray(gzr[:, nsl0 // 8:(nsl0 + NSL) // 8])
        d["uq"] = np.ascontiguousarray(uqw[:, nsl0:nsl0 + NSL])
        d["us"] = np.ascontiguousarray(usc[:, nsl0:nsl0 + NSL])
        d["uz"] = np.ascontiguousarray(uzr[:, nsl0 // 8:(nsl0 + NSL) // 8])
        d["dq"] = np.ascontiguousarray(dqw[nsl0 // 8:(nsl0 + NSL) // 8, :])
        if g_of_m is None:
            pass
        gm = (nsl0 + np.arange(NSL)) // 128
        d["dsx"] = np.ascontiguousarray(dsc_[gm, :])
        d["dzx"] = np.ascontiguousarray(dzr[gm, :])
        in_maps.append(d)
    return in_maps


def kernel(**inputs):
    in_maps = stage_inputs(inputs)
    nc = build_nc(debug=DEBUG_DUMPS)
    res = run_bass_kernel_spmd(nc, in_maps, core_ids=list(range(NC)))
    outs = [np.asarray(res.results[c]["out"]) for c in range(NC)]
    full = np.concatenate(outs, axis=0).reshape(1, S, D).astype(np.float32)
    if DEBUG_DUMPS:
        kernel._dbg = res.results
    return full


# version 06
# speedup vs baseline: 1.0608x; 1.0608x over previous
import contextlib
import math

import numpy as np
import ml_dtypes

import concourse.bass as bass
import concourse.mybir as mybir
from concourse.bass_utils import run_bass_kernel_spmd

F32 = mybir.dt.float32
F16 = mybir.dt.float16
BF16 = mybir.dt.bfloat16
I32 = mybir.dt.int32
AF = mybir.ActivationFunctionType
ALU = mybir.AluOpType

S = 8192
D = 4096
NH = 32
HD = 128
INTER = 11008
NC = 8
TPC = S // NC
NSLOT = 4
NSL = INTER // NC
NQB = 16
QB = 512
NCH = D // 128
WIN_THRESH = 100.0

DEBUG_DUMPS = ()


def _alibi_slopes(n_heads):
    cp2 = 2 ** math.floor(math.log2(n_heads))
    base = 2 ** (-(2 ** (-(math.log2(cp2) - 3))))
    slopes = base ** np.arange(1, cp2 + 1, dtype=np.float64)
    if cp2 != n_heads:
        extra_base = 2 ** (-(2 ** (-(math.log2(2 * cp2) - 3))))
        n_rem = min(cp2, n_heads - cp2)
        extra = extra_base ** np.arange(1, 1 + 2 * n_rem, 2, dtype=np.float64)
        slopes = np.concatenate([slopes, extra])
    return slopes


SLOPES = _alibi_slopes(NH)

B_SLOT = []
for _s in range(NSLOT):
    sig_min = SLOPES[8 * _s + 7]
    B_SLOT.append(min(60, int(math.floor((127 + WIN_THRESH / sig_min) / 128.0))))

N_STEPS = [[4 + min(4 * b, B_SLOT[s]) for b in range(NQB)] for s in range(NSLOT)]
TOTAL_STEPS = sum(sum(r) for r in N_STEPS)

NT_SIZES = [128] * 10 + [96]
NNT = len(NT_SIZES)
NT_OFF = [sum(NT_SIZES[:i]) for i in range(NNT)]

STEP_U = []
BLOCK_RANGE = {}
_g = 0
for _s in range(NSLOT):
    for _b in range(NQB):
        n = N_STEPS[_s][_b]
        BLOCK_RANGE[(_s, _b)] = (_g, _g + n)
        for _u in range(-(n - 4), 4):
            STEP_U.append((_s, _b, _u))
        _g += n
assert _g == TOTAL_STEPS

DQ_UNITS = [("zprep", 0), ("zprep", 1)]
for _m in range(2):
    for _j in range(NCH):
        DQ_UNITS.append(("gu", _m, _j))
for _mt in range(NNT):
    for _ec in range(4):
        DQ_UNITS.append(("down", _mt, _ec))
N_DQ = len(DQ_UNITS)
DQL2_AT = []
_v2 = [0, 0]
for _u in range(N_DQ):
    _v2[_u % 2] += 32 if DQ_UNITS[_u][0] == "zprep" else 48
    DQL2_AT.append(_v2[_u % 2])
DQS2_AT = []
_v3 = [0, 0]
for _u in range(N_DQ):
    _v3[_u % 2] += 16 * (NNT if DQ_UNITS[_u][0] == "gu" else 1)
    DQS2_AT.append(_v3[_u % 2])
DQ_W_A = (0, 15)
DQ_W_B = (15, 51)
DQ_W_S = [(51, 71), (71, 91), (91, 101), (101, N_DQ)]


class Sem:
    __slots__ = ("h", "v")

    def __init__(self, h):
        self.h = h
        self.v = 0

    def inc(self, inst, n=1):
        inst.then_inc(self.h, n)
        self.v += n
        return self.v


def build_nc(debug=(), detect_races=True):
    nc = bass.Bass("TRN2", target_bir_lowering=False, detect_race_conditions=detect_races)

    xrows = nc.dram_tensor("xrows", [TPC, D], F32, kind="ExternalInput").ap()
    wqkv = nc.dram_tensor("wqkv", [D, NSLOT * 3 * HD], F32, kind="ExternalInput").ap()
    bqkvT = nc.dram_tensor("bqkvT", [HD, 12], F32, kind="ExternalInput").ap()
    wprojc = nc.dram_tensor("wprojc", [NSLOT * HD, D], F32, kind="ExternalInput").ap()
    bproj = nc.dram_tensor("bproj", [1, D], F32, kind="ExternalInput").ap()
    patt = nc.dram_tensor("patt", [NSLOT * 5 * 128, QB], BF16, kind="ExternalInput").ap()
    ebias = nc.dram_tensor("ebias", [128, NSLOT * 64], F32, kind="ExternalInput").ap()
    shifts_in = nc.dram_tensor("shifts", [128, 1], I32, kind="ExternalInput").ap()
    identf_in = nc.dram_tensor("identf", [128, 128], F32, kind="ExternalInput").ap()
    onesb_in = nc.dram_tensor("onesb", [128, 128], BF16, kind="ExternalInput").ap()
    gq = nc.dram_tensor("gq", [512, NSL], I32, kind="ExternalInput").ap()
    gs = nc.dram_tensor("gs", [32, NSL], F16, kind="ExternalInput").ap()
    gz = nc.dram_tensor("gz", [32, NSL // 8], I32, kind="ExternalInput").ap()
    uq = nc.dram_tensor("uq", [512, NSL], I32, kind="ExternalInput").ap()
    us = nc.dram_tensor("us", [32, NSL], F16, kind="ExternalInput").ap()
    uz = nc.dram_tensor("uz", [32, NSL // 8], I32, kind="ExternalInput").ap()
    dq = nc.dram_tensor("dq", [NSL // 8, D], I32, kind="ExternalInput").ap()
    dsx = nc.dram_tensor("dsx", [NSL, D], F16, kind="ExternalInput").ap()
    dzx = nc.dram_tensor("dzx", [NSL, D // 8], I32, kind="ExternalInput").ap()
    out = nc.dram_tensor("out", [TPC, D], F32, kind="ExternalOutput").ap()

    h0slab = nc.dram_tensor("h0slab", [D, TPC], BF16).ap()
    h0g = nc.dram_tensor("h0g", [NC * D, TPC], BF16).ap()
    qkvd = nc.dram_tensor("qkvd", [12, HD, S], BF16).ap()
    oaslab = nc.dram_tensor("oaslab", [NSLOT * HD, S], BF16).ap()
    y1dA = nc.dram_tensor("y1dA", [NC * QB, D], BF16).ap()
    y1dB = nc.dram_tensor("y1dB", [NC * QB, D], BF16).ap()
    y1rA = nc.dram_tensor("y1rA", [QB, D], BF16).ap()
    y1rB = nc.dram_tensor("y1rB", [QB, D], BF16).ap()
    x1d = nc.dram_tensor("x1d", [TPC, D], F32).ap()
    h2slabA = nc.dram_tensor("h2slabA", [D, QB], BF16).ap()
    h2slabB = nc.dram_tensor("h2slabB", [D, QB], BF16).ap()
    h2gA = nc.dram_tensor("h2gA", [NC * D, QB], BF16).ap()
    h2gB = nc.dram_tensor("h2gB", [NC * D, QB], BF16).ap()
    wgbf = nc.dram_tensor("wgbf", [NNT * 128, NCH * 128], BF16).ap()
    wubf = nc.dram_tensor("wubf", [NNT * 128, NCH * 128], BF16).ap()
    wdbf = nc.dram_tensor("wdbf", [NSL, D], BF16).ap()
    zpgd = nc.dram_tensor("zpgd", [32, NSL], F32).ap()
    zpud = nc.dram_tensor("zpud", [32, NSL], F32).ap()
    y2dA = nc.dram_tensor("y2dA", [NC * QB, D], BF16).ap()
    y2dB = nc.dram_tensor("y2dB", [NC * QB, D], BF16).ap()
    y2rA = nc.dram_tensor("y2rA", [QB, D], BF16).ap()
    y2rB = nc.dram_tensor("y2rB", [QB, D], BF16).ap()

    dbg = {}
    for name, src_shape, dt_ in (
        ("h0", [NC * D, TPC], BF16), ("qkv", [12 * HD, S], BF16),
        ("oa", [NSLOT * HD, S], BF16), ("y1a", [QB, D], BF16),
        ("x1", [TPC, D], F32), ("h2a", [NC * D, QB], BF16),
        ("h2b", [NC * D, QB], BF16), ("y2a", [QB, D], BF16),
        ("y2b", [QB, D], BF16),
    ):
        if name in debug or (name[:2] in ("h2", "y2") and name[:2] in debug) or \
           (name in ("y1a",) and "y1" in debug):
            dbg[name] = nc.dram_tensor("dbg_" + name, src_shape, dt_, kind="ExternalOutput").ap()

    RG = [list(range(NC))]

    ctx = contextlib.ExitStack()
    with ctx:
        e = ctx.enter_context

        shifts = e(nc.sbuf_tensor("s_shifts", [128, 1], I32))
        epsv = e(nc.sbuf_tensor("s_eps", [128, 1], F32))
        sbq = e(nc.sbuf_tensor("s_bqkv", [128, 12], F32))
        stat6 = e(nc.sbuf_tensor("s_stat6", [128, 8 * 6], F32))
        stat2 = e(nc.sbuf_tensor("s_stat2", [128, 2], F32))
        lnt1 = e(nc.sbuf_tensor("s_lnt1", [128, 1], F32))
        lnt2 = e(nc.sbuf_tensor("s_lnt2", [128, 1], F32))
        fscr = e(nc.sbuf_tensor("s_fscr", [128, 1], F32))
        fsc2 = e(nc.sbuf_tensor("s_fsc2", [128, 1], F32))
        sM = contextlib.ExitStack()
        eM = sM.enter_context
        identb = eM(nc.sbuf_tensor("s_identb", [128, 128], BF16))
        identf = eM(nc.sbuf_tensor("s_identf", [128, 128], F32))
        onesb = eM(nc.sbuf_tensor("s_onesb", [128, 128], BF16))
        sebias = eM(nc.sbuf_tensor("s_ebias", [128, NSLOT * 64], F32))

        pf = [e(nc.psum_tensor(f"pf{i}", [128, 512], F32)) for i in range(6)]
        pb = [e(nc.psum_tensor(f"pb{i}", [128, 1024], BF16)) for i in range(2)]

        def sem(name):
            return Sem(e(nc.semaphore(name)))

        s_init = sem("init")
        s_dql = [sem("dql0"), sem("dql1")]; s_dqv = sem("dqv")
        s_dqs = [sem("dqs0"), sem("dqs1")]
        s_p0l = [sem("p0l0"), sem("p0l1")]; s_p0v = sem("p0v"); s_rst = sem("rst")
        s_p0t = sem("p0t"); s_p0e = sem("p0e"); s_p0s = sem("p0s")
        s_ag0 = sem("ag0")
        s_p1w = [sem("p1w0"), sem("p1w1")]; s_p1wc = sem("p1wc")
        s_p1h = [sem("p1h0"), sem("p1h1"), sem("p1h2"), sem("p1h3")]
        s_p1j = sem("p1j"); s_p1e = sem("p1e")
        s_p1s = [sem("p1s0"), sem("p1s1")]
        s_patt = sem("patt"); s_kf = sem("skf"); s_vf = sem("svf"); s_qf = sem("sqf")
        s_vt = sem("vt"); s_vte = sem("vte")
        s_sS = sem("sS"); s_sX = sem("sX"); s_sM = sem("sM"); s_sAV = sem("sAV")
        s_sR = sem("sR"); s_sE = sem("sE"); s_sON = sem("sON"); s_oas = sem("oas")
        s_bp = sem("sbp")
        s_p3l = [sem("p3l0"), sem("p3l1")]
        s_p3w = [sem("p3w0"), sem("p3w1")]; s_p3wc = sem("p3wc")
        s_p3m = sem("p3m"); s_p3e = sem("p3e")
        s_p3s = [sem("p3s0"), sem("p3s1")]
        s_rsa1 = sem("rsa1"); s_rsb1 = sem("rsb1")
        s_r5l = sem("r5l"); s_r5v = sem("r5v"); s_r5r = sem("r5r")
        s_h2t = sem("h2t"); s_h2e = sem("h2e")
        s_x1s = sem("x1s"); s_h2sA = sem("h2sA"); s_h2sB = sem("h2sB")
        s_agh2a = sem("agh2a"); s_agh2b = sem("agh2b")
        s_fh = sem("fh"); s_fwg = sem("fwg"); s_fwu = sem("fwu"); s_fwd = sem("fwd")
        s_fg = sem("fg"); s_fsil = sem("fsil"); s_fint = sem("fint")
        s_fy = sem("fy"); s_fye = sem("fye")
        s_fys = [sem("fys0"), sem("fys1")]
        s_rsa2 = sem("rsa2"); s_rsb2 = sem("rsb2")
        s_p5l = [sem("p5l0"), sem("p5l1")]
        s_p5v = sem("p5v"); s_p5r = sem("p5r"); s_p5s = sem("p5s")
        s_dv = sem("dvself"); s_sc = sem("scself")
        s_fab = sem("fab"); s_fbc = sem("fbc"); s_fcd = sem("fcd")
        s_fde = sem("fde"); s_fef = sem("fef")

        def vedge(vector, inst):
            inst.then_inc(s_dv.h, 1)
            s_dv.v += 1
            vector.wait_ge(s_dv.h, s_dv.v)

        def sedge(scalar, inst):
            inst.then_inc(s_sc.h, 1)
            s_sc.v += 1
            scalar.wait_ge(s_sc.h, s_sc.v)

        block = e(nc.Block())

        @block.sync
        def _(sync):
            sync.dma_start(out=shifts[:], in_=shifts_in).then_inc(s_init.h, 16)
            sync.dma_start(out=identf[:], in_=identf_in).then_inc(s_init.h, 16)
            sync.dma_start(out=onesb[:], in_=onesb_in).then_inc(s_init.h, 16)
            sync.dma_start(out=sebias[:], in_=ebias).then_inc(s_init.h, 16)
            sync.dma_start(out=sbq[:], in_=bqkvT).then_inc(s_init.h, 16)
            s_init.v += 80

        @block.vector
        def _(vector):
            vector.wait_ge(s_init.h, 80)
            vector.memset(epsv[:], 1e-5)
            vector.tensor_copy(out=identb[:], in_=identf[:]).then_inc(s_init.h, 1)
            s_init.v += 1
        INIT_DONE = s_init.v

        sDQ = contextlib.ExitStack()
        eQ = sDQ.enter_context
        dqr = [eQ(nc.sbuf_tensor(f"dq_qr{i}", [128, NSL], I32)) for i in range(2)]
        dsc = [eQ(nc.sbuf_tensor(f"dq_sc{i}", [128, NSL], F16)) for i in range(2)]
        dzp = [eQ(nc.sbuf_tensor(f"dq_zp{i}", [128, NSL], F32)) for i in range(2)]
        dnib = eQ(nc.sbuf_tensor("dq_nib", [128, NSL], I32))
        dt1 = eQ(nc.sbuf_tensor("dq_t1", [128, NSL], F32))
        dwb = [eQ(nc.sbuf_tensor(f"dq_wb{i}", [128, NSL], BF16)) for i in range(2)]
        dzrow = [eQ(nc.sbuf_tensor(f"dq_zrow{i}", [128, 128], I32)) for i in range(2)]
        dzaux = eQ(nc.sbuf_tensor("dq_zaux", [128, 1024], I32))
        zpz = eQ(nc.sbuf_tensor("dq_zpz", [32, NSL // 8], I32))
        zpe = eQ(nc.sbuf_tensor("dq_zpe", [32, NSL], I32))
        zps = eQ(nc.sbuf_tensor("dq_zps", [32, NSL], F16))
        zpf = eQ(nc.sbuf_tensor("dq_zpf", [32, NSL], F32))
        zpg = eQ(nc.sbuf_tensor("dq_zpg", [32, NSL], F32))

        gu_src = [(gq, gs, zpgd, wgbf), (uq, us, zpud, wubf)]

        def dq_loads(gpsimd, lo, hi):
            for u in range(lo, hi):
                kind = DQ_UNITS[u]
                sl_ = s_dql[u % 2]
                if kind[0] == "zprep":
                    m_ = kind[1]
                    if u == 1:
                        gpsimd.wait_ge(s_dqv.h, 1)
                    zsrc = gz if m_ == 0 else uz
                    ssrc = gs if m_ == 0 else us
                    gpsimd.dma_start(out=zpz[:], in_=zsrc).then_inc(sl_.h, 16)
                    gpsimd.dma_start(out=zps[:], in_=ssrc).then_inc(sl_.h, 16)
                    sl_.v += 32
                elif kind[0] == "gu":
                    m_, j_ = kind[1], kind[2]
                    qsrc, ssrc, zpd_, _w = gu_src[m_]
                    if u >= 2:
                        gpsimd.wait_ge(s_dqv.h, u - 1)
                    gpsimd.wait_ge(s_dqs[m_].h, 16)
                    rep = bass.AP(qsrc.tensor, 16 * j_ * NSL, [[NSL, 16], [0, 8], [1, NSL]])
                    gpsimd.dma_start(out=dqr[u % 2][:, 0:NSL], in_=rep).then_inc(sl_.h, 16)
                    srep = bass.AP(ssrc.tensor, j_ * NSL, [[0, 128], [1, NSL]])
                    gpsimd.dma_start(out=dsc[u % 2][:, 0:NSL], in_=srep).then_inc(sl_.h, 16)
                    zrep = bass.AP(zpd_.tensor, j_ * NSL, [[0, 128], [1, NSL]])
                    gpsimd.dma_start(out=dzp[u % 2][:, 0:NSL], in_=zrep).then_inc(sl_.h, 16)
                    sl_.v += 48
                else:
                    _, mt_, ec_ = kind
                    nw = NT_SIZES[mt_]
                    if u >= 2:
                        gpsimd.wait_ge(s_dqv.h, u - 1)
                    rep = bass.AP(dq.tensor, 16 * mt_ * D + 1024 * ec_, [[D, nw // 8], [0, 8], [1, 1024]])
                    gpsimd.dma_start(out=dqr[u % 2][0:nw, 0:1024], in_=rep).then_inc(sl_.h, 16)
                    gpsimd.dma_start(out=dsc[u % 2][0:nw, 0:1024],
                                     in_=dsx[128 * mt_:128 * mt_ + nw, 1024 * ec_:1024 * (ec_ + 1)]).then_inc(sl_.h, 16)
                    gpsimd.dma_start(out=dzrow[u % 2][0:nw, :],
                                     in_=dzx[128 * mt_:128 * mt_ + nw, 128 * ec_:128 * (ec_ + 1)]).then_inc(sl_.h, 16)
                    sl_.v += 48

        def dq_vec(vector, lo, hi):
            for u in range(lo, hi):
                kind = DQ_UNITS[u]
                vector.wait_ge(s_dql[u % 2].h, DQL2_AT[u])
                if u == 1:
                    vector.wait_ge(s_dqs[0].h, 16)
                if u >= 2:
                    vector.wait_ge(s_dqs[u % 2].h, DQS2_AT[u - 2])
                if kind[0] == "zprep":
                    for jn in range(8):
                        inst = vector.tensor_scalar(out=zpe[:, jn::8], in0=zpz[:], scalar1=4 * jn,
                                                    scalar2=15, op0=ALU.logical_shift_right, op1=ALU.bitwise_and)
                    vedge(vector, inst)
                    vedge(vector, vector.tensor_scalar(out=zpf[:], in0=zpe[:], scalar1=1.0, scalar2=None, op0=ALU.add))
                    vector.tensor_tensor(out=zpg[:], in0=zpf[:], in1=zps[:], op=ALU.mult).then_inc(s_dqv.h, 1)
                elif kind[0] == "gu":
                    u2 = u % 2
                    vedge(vector, vector.tensor_scalar(out=dnib[:, 0:NSL], in0=dqr[u2][:, 0:NSL], scalar1=shifts[:],
                                                       scalar2=15, op0=ALU.logical_shift_right, op1=ALU.bitwise_and))
                    vedge(vector, vector.tensor_tensor(out=dt1[:, 0:NSL], in0=dnib[:, 0:NSL],
                                                       in1=dsc[u2][:, 0:NSL], op=ALU.mult))
                    vector.tensor_tensor(out=dwb[u2][:, 0:NSL], in0=dt1[:, 0:NSL], in1=dzp[u2][:, 0:NSL],
                                         op=ALU.subtract).then_inc(s_dqv.h, 1)
                else:
                    _, mt_, ec_ = kind
                    u2 = u % 2
                    for jn in range(8):
                        inst = vector.tensor_scalar(out=dzaux[:, jn::8], in0=dzrow[u % 2][:], scalar1=4 * jn,
                                                    scalar2=15, op0=ALU.logical_shift_right, op1=ALU.bitwise_and)
                    vedge(vector, inst)
                    vedge(vector, vector.tensor_scalar(out=dt1[:, 0:1024], in0=dzaux[:], scalar1=1.0,
                                                       scalar2=None, op0=ALU.add))
                    vedge(vector, vector.tensor_tensor(out=dzp[u2][:, 0:1024], in0=dt1[:, 0:1024],
                                                       in1=dsc[u2][:, 0:1024], op=ALU.mult))
                    vedge(vector, vector.tensor_scalar(out=dnib[:, 0:1024], in0=dqr[u2][:, 0:1024], scalar1=shifts[:],
                                                       scalar2=15, op0=ALU.logical_shift_right, op1=ALU.bitwise_and))
                    vedge(vector, vector.tensor_tensor(out=dt1[:, 0:1024], in0=dnib[:, 0:1024],
                                                       in1=dsc[u2][:, 0:1024], op=ALU.mult))
                    vector.tensor_tensor(out=dwb[u2][:, 0:1024], in0=dt1[:, 0:1024], in1=dzp[u2][:, 0:1024],
                                         op=ALU.subtract).then_inc(s_dqv.h, 1)
                s_dqv.v += 1

        def dq_stores(gpsimd, lo, hi):
            for u in range(lo, hi):
                kind = DQ_UNITS[u]
                ss_ = s_dqs[u % 2]
                gpsimd.wait_ge(s_dqv.h, u + 1)
                if kind[0] == "zprep":
                    dst = zpgd if kind[1] == 0 else zpud
                    gpsimd.dma_start(out=dst, in_=zpg[:]).then_inc(ss_.h, 16)
                elif kind[0] == "gu":
                    m_, j_ = kind[1], kind[2]
                    wdst = gu_src[m_][3]
                    for nt in range(NNT):
                        nw = NT_SIZES[nt]
                        gpsimd.dma_start(out=wdst[128 * nt:128 * (nt + 1), j_ * nw:(j_ + 1) * nw],
                                         in_=dwb[u % 2][:, NT_OFF[nt]:NT_OFF[nt] + nw]).then_inc(ss_.h, 16)
                        ss_.v += 16
                    ss_.v -= 16
                else:
                    _, mt_, ec_ = kind
                    nw = NT_SIZES[mt_]
                    gpsimd.dma_start(out=wdbf[128 * mt_:128 * mt_ + nw, 1024 * ec_:1024 * (ec_ + 1)],
                                     in_=dwb[u % 2][0:nw, 0:1024]).then_inc(ss_.h, 16)
                ss_.v += 16

        dq_ptr = [0]

        def dq_chunk(gpsimd, lo, hi, flush=False):
            for u in range(lo, hi):
                while dq_ptr[0] <= u - 2:
                    dq_stores(gpsimd, dq_ptr[0], dq_ptr[0] + 1)
                    dq_ptr[0] += 1
                dq_loads(gpsimd, u, u + 1)
            if flush:
                while dq_ptr[0] < hi:
                    dq_stores(gpsimd, dq_ptr[0], dq_ptr[0] + 1)
                    dq_ptr[0] += 1

        sA = contextlib.ExitStack()
        eA = sA.enter_context
        x0t = [eA(nc.sbuf_tensor(f"p0_x{i}", [128, D], F32)) for i in range(2)]
        h0t = eA(nc.sbuf_tensor("p0_h0", [128, D], BF16))
        h0stg = eA(nc.sbuf_tensor("p0_stg", [128, D], BF16))

        NT0 = TPC // 128

        @block.sync
        def _(sync):
            for i in range(NT0):
                if i >= 2:
                    sync.wait_ge(s_p0v.h, 2 * (i - 1))
                sp = s_p0l[i % 2]
                sync.dma_start(out=x0t[i % 2][:], in_=xrows[128 * i:128 * (i + 1), :]).then_inc(sp.h, 16)
                sp.v += 16

        @block.vector
        def _(vector):
            vector.wait_ge(s_init.h, INIT_DONE)
            for i in range(NT0):
                vector.wait_ge(s_p0l[i % 2].h, 16 * (i // 2 + 1))
                if i >= 1:
                    vector.wait_ge(s_rst.h, i)
                xt = x0t[i % 2]
                for ch in range(8):
                    inst = vector.bn_stats(out=stat6[:, 6 * ch:6 * ch + 6], in_=xt[:, 512 * ch:512 * (ch + 1)])
                vedge(vector, inst)
                vector.bn_aggr(out=stat2[:], in_=stat6[:]).then_inc(s_p0v.h, 1)
                s_p0v.v += 1
                vector.wait_ge(s_rst.h, i + 1)
                if i >= 1:
                    vector.wait_ge(s_p0t.h, 4 * i)
                vector.tensor_scalar(out=h0t[:], in0=xt[:], scalar1=stat2[:, 0:1],
                                     scalar2=lnt2[:], op0=ALU.subtract, op1=ALU.mult).then_inc(s_p0v.h, 1)
                s_p0v.v += 1
                for g in range(4):
                    vector.wait_ge(s_p0t.h, 4 * i + g + 1)
                    if g == 0 and i >= 1:
                        vector.wait_ge(s_p0s.h, 16 * i)
                    vector.tensor_copy(out=h0stg[:, 1024 * g:1024 * (g + 1)],
                                       in_=pb[g % 2][:]).then_inc(s_p0e.h, 1)
                    s_p0e.v += 1

        @block.scalar
        def _(scalar):
            for i in range(NT0):
                scalar.wait_ge(s_p0v.h, 2 * i + 1)
                sedge(scalar, scalar.activation(out=lnt1[:], in_=stat2[:, 1:2], func=AF.Ln, bias=epsv[:]))
                scalar.activation(out=lnt2[:], in_=lnt1[:], func=AF.Exp, scale=-0.5).then_inc(s_rst.h, 1)
                s_rst.v += 1

        @block.tensor
        def _(tensor):
            tensor.wait_ge(s_init.h, INIT_DONE)
            for i in range(NT0):
                tensor.wait_ge(s_p0v.h, 2 * i + 2)
                for g in range(4):
                    gg = 4 * i + g
                    if gg >= 2:
                        tensor.wait_ge(s_p0e.h, gg - 1)
                    for k in range(8):
                        ch = 8 * g + k
                        inst = tensor.transpose(pb[g % 2][:, 128 * k:128 * (k + 1)],
                                                h0t[:, 128 * ch:128 * (ch + 1)], identb[:])
                    inst.then_inc(s_p0t.h, 1)
                    s_p0t.v += 1

        @block.gpsimd
        def _(gpsimd):
            for i in range(NT0):
                gpsimd.wait_ge(s_p0e.h, 4 * (i + 1))
                dstap = bass.AP(h0slab.tensor, 128 * i, [[TPC, 128], [128 * TPC, 32], [1, 128]])
                gpsimd.dma_start(out=dstap,
                                 in_=h0stg.ap().rearrange("p (j t) -> p j t", t=128)).then_inc(s_p0s.h, 16)
                s_p0s.v += 16
            gpsimd.wait_ge(s_p0s.h, s_p0s.v)
            gpsimd.collective_compute("AllGather", ALU.bypass, RG, [h0slab.opt()], [h0g.opt()]).then_inc(s_ag0.h, 1)
            s_ag0.v += 1

        @block.gpsimd
        def _(gpsimd):
            gpsimd.wait_ge(s_init.h, 80)
            dq_chunk(gpsimd, *DQ_W_A)

        @block.vector
        def _(vector):
            dq_vec(vector, *DQ_W_A)

        FAB = 3

        @block.vector
        def _(vector):
            vector.tensor_copy(out=fscr[:], in_=epsv[:]).then_inc(s_fab.h, 1)

        @block.scalar
        def _(scalar):
            scalar.activation(out=fsc2[:], in_=epsv[:], func=AF.Copy).then_inc(s_fab.h, 1)

        @block.gpsimd
        def _(gpsimd):
            gpsimd.wait_ge(s_p0s.h, s_p0s.v)
            gpsimd.wait_ge(s_dqs[0].h, s_dqs[0].v)
            gpsimd.wait_ge(s_dqs[1].h, s_dqs[1].v)
            gpsimd.sem_inc(s_fab.h, 1)

        sA.close()
        sB = contextlib.ExitStack()
        eB = sB.enter_context
        wqsb = eB(nc.sbuf_tensor("p1_w", [128, NCH * 768], BF16))
        wqld = [eB(nc.sbuf_tensor(f"p1_wl{i}", [128, 768], F32)) for i in range(2)]
        h0rhs = [eB(nc.sbuf_tensor(f"p1_h{i}", [128, 512], BF16)) for i in range(4)]
        qkev = [eB(nc.sbuf_tensor(f"p1_e{i}", [128, 6 * 512], BF16)) for i in range(2)]

        @block.sync
        def _(sync):
            sync.wait_ge(s_fab.h, FAB)
            for sw in range(2):
                for j in range(NCH):
                    cidx = sw * NCH + j
                    if cidx >= 2:
                        sync.wait_ge(s_p1wc.h, cidx - 1)
                    sp = s_p1w[cidx % 2]
                    sync.dma_start(out=wqld[cidx % 2][:],
                                   in_=wqkv[128 * j:128 * (j + 1), 768 * sw:768 * (sw + 1)]).then_inc(sp.h, 16)
                    sp.v += 16
                if sw == 0:
                    sync.wait_ge(s_ag0.h, 1)
                for m in range(NQB):
                    for j in range(NCH):
                        idx = (sw * NQB + m) * NCH + j
                        if idx >= 4:
                            sync.wait_ge(s_p1j.h, idx - 3)
                        rk = m // 2
                        tl = 512 * (m % 2)
                        src = h0g[D * rk + 128 * j:D * rk + 128 * (j + 1), tl:tl + 512]
                        sp = s_p1h[idx % 4]
                        sync.dma_start(out=h0rhs[idx % 4][:], in_=src).then_inc(sp.h, 16)
                        sp.v += 16

        @block.vector
        def _(vector):
            vector.wait_ge(s_fab.h, FAB)
            for sw in range(2):
                if sw == 1:
                    vector.wait_ge(s_p1j.h, NCH * NQB)
                for j in range(NCH):
                    cidx = sw * NCH + j
                    vector.wait_ge(s_p1w[cidx % 2].h, 16 * (cidx // 2 + 1))
                    vector.tensor_copy(out=wqsb[:, 768 * j:768 * (j + 1)],
                                       in_=wqld[cidx % 2][:]).then_inc(s_p1wc.h, 1)
                    s_p1wc.v += 1

        @block.tensor
        def _(tensor):
            tensor.wait_ge(s_fab.h, FAB)
            for sw in range(2):
                tensor.wait_ge(s_p1wc.h, NCH * (sw + 1))
                for m in range(NQB):
                    gidx = sw * NQB + m
                    if gidx >= 1:
                        tensor.wait_ge(s_p1e.h, 6 * gidx)
                    for j in range(NCH):
                        idx = gidx * NCH + j
                        tensor.wait_ge(s_p1h[idx % 4].h, 16 * (idx // 4 + 1))
                        for q6 in range(6):
                            inst = tensor.matmul(pf[q6][:],
                                                 wqsb[:, 768 * j + 128 * q6:768 * j + 128 * (q6 + 1)],
                                                 h0rhs[idx % 4][:], start=(j == 0), stop=(j == NCH - 1))
                        inst.then_inc(s_p1j.h, 1)
                        s_p1j.v += 1

        @block.scalar
        def _(scalar):
            for sw in range(2):
                for m in range(NQB):
                    gidx = sw * NQB + m
                    scalar.wait_ge(s_p1j.h, NCH * (gidx + 1))
                    if gidx >= 2:
                        scalar.wait_ge(s_p1s[gidx % 2].h, 96 * (gidx // 2))
                    for q6 in range(6):
                        sq = 3 * (2 * sw + q6 // 3) + (q6 % 3)
                        scalar.activation(out=qkev[gidx % 2][:, 512 * q6:512 * (q6 + 1)], in_=pf[q6][:],
                                          func=AF.Identity, bias=sbq[:, sq:sq + 1],
                                          scale=1.0).then_inc(s_p1e.h, 1)
                        s_p1e.v += 1

        @block.gpsimd
        def _(gpsimd):
            for sw in range(2):
                for m in range(NQB):
                    gidx = sw * NQB + m
                    gpsimd.wait_ge(s_p1e.h, 6 * (gidx + 1))
                    sp = s_p1s[gidx % 2]
                    for q6 in range(6):
                        sq = 3 * (2 * sw + q6 // 3) + (q6 % 3)
                        gpsimd.dma_start(out=qkvd[sq, :, 512 * m:512 * (m + 1)],
                                         in_=qkev[gidx % 2][:, 512 * q6:512 * (q6 + 1)]).then_inc(sp.h, 16)
                        sp.v += 16

        @block.gpsimd
        def _(gpsimd):
            dq_chunk(gpsimd, *DQ_W_B)

        @block.vector
        def _(vector):
            dq_vec(vector, *DQ_W_B)

        FBC = 3

        @block.vector
        def _(vector):
            vector.tensor_copy(out=fscr[:], in_=epsv[:]).then_inc(s_fbc.h, 1)

        @block.scalar
        def _(scalar):
            scalar.activation(out=fsc2[:], in_=epsv[:], func=AF.Copy).then_inc(s_fbc.h, 1)

        @block.gpsimd
        def _(gpsimd):
            gpsimd.wait_ge(s_p1s[0].h, s_p1s[0].v)
            gpsimd.wait_ge(s_p1s[1].h, s_p1s[1].v)
            gpsimd.wait_ge(s_dqs[0].h, s_dqs[0].v)
            gpsimd.wait_ge(s_dqs[1].h, s_dqs[1].v)
            gpsimd.sem_inc(s_fbc.h, 1)

        sB.close()
        sC = contextlib.ExitStack()
        eC = sC.enter_context
        spatt = eC(nc.sbuf_tensor("p2_patt", [128, 20 * 512], BF16))
        kf = eC(nc.sbuf_tensor("p2_k", [128, S], BF16))
        vf = eC(nc.sbuf_tensor("p2_v", [128, S], BF16))
        qf = eC(nc.sbuf_tensor("p2_q", [128, S], BF16))
        vt = eC(nc.sbuf_tensor("p2_vt", [128, S], BF16))
        ptx = [eC(nc.sbuf_tensor(f"p2_px{i}", [128, 512], BF16)) for i in range(4)]
        ptm = [eC(nc.sbuf_tensor(f"p2_pm{i}", [128, 512], BF16)) for i in range(4)]
        osb = eC(nc.sbuf_tensor("p2_osb", [128, 512], BF16))
        rsb = eC(nc.sbuf_tensor("p2_rsb", [128, 512], F32))
        onb = eC(nc.sbuf_tensor("p2_onb", [128, 512], BF16))

        @block.sync
        def _(sync):
            sync.wait_ge(s_fbc.h, FBC)
            sync.dma_start(out=spatt[:],
                           in_=patt.rearrange("(v p) q -> p v q", p=128)).then_inc(s_patt.h, 16)
            s_patt.v += 16
            for s_ in range(NSLOT):
                if s_ > 0:
                    gend = BLOCK_RANGE[(s_ - 1, NQB - 1)][1]
                    sync.wait_ge(s_sAV.h, gend)
                    sync.wait_ge(s_vte.h, 8 * s_)
                sync.dma_start(out=kf[:], in_=qkvd[3 * s_ + 1, :, :]).then_inc(s_kf.h, 16)
                sync.dma_start(out=vf[:], in_=qkvd[3 * s_ + 2, :, :]).then_inc(s_vf.h, 16)
                sync.dma_start(out=qf[:], in_=qkvd[3 * s_ + 0, :, :]).then_inc(s_qf.h, 16)
                s_kf.v += 16; s_vf.v += 16; s_qf.v += 16

        for s_ in range(NSLOT):
            def make_tensor(s_=s_):
                def emit(tensor):
                    if s_ == 0:
                        tensor.wait_ge(s_fbc.h, FBC)
                    tensor.wait_ge(s_vf.h, 16 * (s_ + 1))
                    for gvt in range(8):
                        gg = 8 * s_ + gvt
                        if gg >= 2:
                            tensor.wait_ge(s_vte.h, gg - 1)
                        for k in range(8):
                            ch = 8 * gvt + k
                            inst = tensor.transpose(pb[gvt % 2][:, 128 * k:128 * (k + 1)],
                                                    vf[:, 128 * ch:128 * (ch + 1)], identb[:])
                        inst.then_inc(s_vt.h, 1)
                        s_vt.v += 1
                    tensor.wait_ge(s_kf.h, 16 * (s_ + 1))
                    tensor.wait_ge(s_qf.h, 16 * (s_ + 1))
                    tensor.wait_ge(s_patt.h, 16)
                    tensor.wait_ge(s_vte.h, 8 * (s_ + 1))
                    LA = 3
                    gs0 = BLOCK_RANGE[(s_, 0)][0]
                    gs1 = BLOCK_RANGE[(s_, NQB - 1)][1]

                    def emit_S(gi):
                        sx, bx, ux = STEP_U[gi]
                        k0x = QB * bx + 128 * ux
                        sbank = [pf[0], pf[1], pf[4], pf[5]][gi % 4]
                        if gi >= 4:
                            tensor.wait_ge(s_sX.h, gi - 3)
                        inst = tensor.matmul(sbank[:], kf[:, k0x:k0x + 128],
                                             qf[:, QB * bx:QB * (bx + 1)], start=True, stop=True)
                        inst.then_inc(s_sS.h, 1)
                        s_sS.v += 1

                    for gpre in range(gs0, min(gs0 + LA, gs1)):
                        emit_S(gpre)
                    for b_ in range(NQB):
                        g0, g1 = BLOCK_RANGE[(s_, b_)]
                        blkidx = s_ * NQB + b_
                        if blkidx >= 1:
                            tensor.wait_ge(s_sR.h, blkidx)
                            tensor.wait_ge(s_sE.h, blkidx)
                        for i, gi in enumerate(range(g0, g1)):
                            if gi + LA < gs1:
                                emit_S(gi + LA)
                            u_ = STEP_U[gi][2]
                            k0 = QB * b_ + 128 * u_
                            tensor.wait_ge(s_sM.h, gi + 1)
                            tensor.matmul(pf[2][:], onesb[:], ptm[gi % 4][:],
                                          start=(i == 0), stop=(i == g1 - g0 - 1))
                            ct = k0 // 128
                            inst = tensor.matmul(pf[3][:], vt[:, 128 * ct:128 * (ct + 1)],
                                                 ptm[gi % 4][:],
                                                 start=(i == 0), stop=(i == g1 - g0 - 1))
                            inst.then_inc(s_sAV.h, 1)
                            s_sAV.v += 1
                return emit
            block.tensor(make_tensor())

            def make_scalar(s_=s_):
                def emit(scalar):
                    for b_ in range(NQB):
                        g0, g1 = BLOCK_RANGE[(s_, b_)]
                        blkidx = s_ * NQB + b_
                        for i, gi in enumerate(range(g0, g1)):
                            u_ = STEP_U[gi][2]
                            scalar.wait_ge(s_sS.h, gi + 1)
                            if gi >= 4:
                                scalar.wait_ge(s_sM.h, gi - 3)
                            iu = 3 - u_
                            sbank = [pf[0], pf[1], pf[4], pf[5]][gi % 4]
                            scalar.activation(out=ptx[gi % 4][:], in_=sbank[:], func=AF.Exp,
                                              bias=sebias[:, 64 * s_ + iu:64 * s_ + iu + 1],
                                              scale=1.0 / 128).then_inc(s_sX.h, 1)
                            s_sX.v += 1
                        scalar.wait_ge(s_sAV.h, g1)
                        if blkidx >= 1:
                            scalar.wait_ge(s_sON.h, blkidx)
                        scalar.activation(out=osb[:], in_=pf[3][:], func=AF.Copy).then_inc(s_sE.h, 1)
                        s_sE.v += 1
                return emit
            block.scalar(make_scalar())

            def make_vector(s_=s_):
                def emit(vector):
                    if s_ == 0:
                        vector.wait_ge(s_patt.h, 16)
                    for gvt in range(8):
                        vector.wait_ge(s_vt.h, 8 * s_ + gvt + 1)
                        vector.tensor_copy(out=vt[:, 1024 * gvt:1024 * (gvt + 1)],
                                           in_=pb[gvt % 2][:]).then_inc(s_vte.h, 1)
                        s_vte.v += 1
                    for b_ in range(NQB):
                        g0, g1 = BLOCK_RANGE[(s_, b_)]
                        blkidx = s_ * NQB + b_
                        for i, gi in enumerate(range(g0, g1)):
                            u_ = STEP_U[gi][2]
                            vector.wait_ge(s_sX.h, gi + 1)
                            if gi >= 4:
                                vector.wait_ge(s_sAV.h, gi - 3)
                            v_ = u_ if 0 <= u_ else 4
                            pslice = spatt[:, 512 * (5 * s_ + v_):512 * (5 * s_ + v_ + 1)]
                            vector.tensor_tensor(out=ptm[gi % 4][:], in0=ptx[gi % 4][:],
                                                 in1=pslice, op=ALU.mult).then_inc(s_sM.h, 1)
                            s_sM.v += 1
                        vector.wait_ge(s_sAV.h, g1)
                        vector.reciprocal(out=rsb[:], in_=pf[2][:]).then_inc(s_sR.h, 1)
                        s_sR.v += 1
                        vector.wait_ge(s_sR.h, s_sR.v)
                        vector.wait_ge(s_sE.h, blkidx + 1)
                        if blkidx >= 1:
                            vector.wait_ge(s_oas.h, 16 * blkidx)
                        vector.tensor_tensor(out=onb[:], in0=osb[:], in1=rsb[:],
                                             op=ALU.mult).then_inc(s_sON.h, 1)
                        s_sON.v += 1
                return emit
            block.vector(make_vector())

            def make_gp(s_=s_):
                def emit(gpsimd):
                    for b_ in range(NQB):
                        blkidx = s_ * NQB + b_
                        gpsimd.wait_ge(s_sON.h, blkidx + 1)
                        gpsimd.dma_start(out=oaslab[128 * s_:128 * (s_ + 1), QB * b_:QB * (b_ + 1)],
                                         in_=onb[:]).then_inc(s_oas.h, 16)
                        s_oas.v += 16
                return emit
            block.gpsimd(make_gp())

            lo, hi = DQ_W_S[s_]

            def mk_dql(lo=lo, hi=hi, last=(s_ == NSLOT - 1)):
                def emit(gpsimd):
                    dq_chunk(gpsimd, lo, hi, flush=last)
                return emit
            block.gpsimd(mk_dql())

            def mk_dqv(lo=lo, hi=hi):
                def emit(vector):
                    dq_vec(vector, lo, hi)
                return emit
            block.vector(mk_dqv())



        FCD = 3

        @block.vector
        def _(vector):
            vector.tensor_copy(out=fscr[:], in_=epsv[:]).then_inc(s_fcd.h, 1)

        @block.scalar
        def _(scalar):
            scalar.activation(out=fsc2[:], in_=epsv[:], func=AF.Copy).then_inc(s_fcd.h, 1)

        @block.gpsimd
        def _(gpsimd):
            gpsimd.wait_ge(s_oas.h, s_oas.v)
            gpsimd.wait_ge(s_dqs[0].h, s_dqs[0].v)
            gpsimd.wait_ge(s_dqs[1].h, s_dqs[1].v)
            gpsimd.sem_inc(s_fcd.h, 1)

        sC.close()
        sDQ.close()
        sD = contextlib.ExitStack()
        eD = sD.enter_context
        wpld = [eD(nc.sbuf_tensor(f"p3_wl{i}", [128, D], F32)) for i in range(2)]
        wpcb = eD(nc.sbuf_tensor("p3_w", [128, NSLOT * D], BF16))
        oalhs = [eD(nc.sbuf_tensor(f"p3_o{i}", [128, NSLOT * 512], BF16)) for i in range(2)]
        yev = [eD(nc.sbuf_tensor(f"p3_y{i}", [128, D], BF16)) for i in range(2)]
        bprep = eD(nc.sbuf_tensor("p3_bp", [128, D], F32))
        ylt = eD(nc.sbuf_tensor("p3_yl", [128, D], BF16))
        ybt = eD(nc.sbuf_tensor("p3_yb", [128, D], F32))
        lnb = eD(nc.sbuf_tensor("p3_ln", [128, D], F32))
        xt3 = eD(nc.sbuf_tensor("p3_x", [128, D], F32))
        x1b = eD(nc.sbuf_tensor("p3_x1", [128, D], F32))
        h2t = eD(nc.sbuf_tensor("p3_h2", [128, D], BF16))
        h2stg = eD(nc.sbuf_tensor("p3_h2s", [128, D], BF16))

        P3_TILES = [(r, q, 0) for r in range(NC) for q in range(4)] + \
                   [(r, q, 1) for r in range(NC) for q in range(4)]

        @block.sync
        def _(sync):
            sync.wait_ge(s_fcd.h, FCD)
            brep = bass.AP(bproj.tensor, 0, [[0, 128], [1, D]])
            sync.dma_start(out=bprep[:], in_=brep).then_inc(s_bp.h, 16)
            s_bp.v += 16
            for sl in range(NSLOT):
                if sl >= 2:
                    sync.wait_ge(s_p3wc.h, sl - 1)
                sp = s_p3w[sl % 2]
                sync.dma_start(out=wpld[sl % 2][:],
                               in_=wprojc[128 * sl:128 * (sl + 1), :]).then_inc(sp.h, 16)
                sp.v += 16
            for lg in range(16):
                r, hf = lg % 8, lg // 8
                if lg >= 2:
                    sync.wait_ge(s_p3m.h, 8 * 4 * (lg - 1))
                src = bass.AP(oaslab.tensor, 1024 * r + 512 * hf,
                              [[S, 128], [128 * S, NSLOT], [1, 512]])
                sp = s_p3l[lg % 2]
                sync.dma_start(out=oalhs[lg % 2].ap().rearrange("p (s t) -> p s t", t=512),
                               in_=src).then_inc(sp.h, 16)
                sp.v += 16

        @block.vector
        def _(vector):
            vector.wait_ge(s_fcd.h, FCD)
            for sl in range(NSLOT):
                vector.wait_ge(s_p3w[sl % 2].h, 16 * (sl // 2 + 1))
                vector.tensor_copy(out=wpcb[:, D * sl:D * (sl + 1)],
                                   in_=wpld[sl % 2][:]).then_inc(s_p3wc.h, 1)
                s_p3wc.v += 1

        @block.tensor
        def _(tensor):
            tensor.wait_ge(s_fcd.h, FCD)
            tensor.wait_ge(s_p3wc.h, NSLOT)
            for ti, (r, q, hf) in enumerate(P3_TILES):
                lg = 8 * hf + r
                tensor.wait_ge(s_p3l[lg % 2].h, 16 * (lg // 2 + 1))
                for ec in range(8):
                    eidx = 8 * ti + ec
                    if eidx >= 4:
                        tensor.wait_ge(s_p3e.h, eidx - 3)
                    for sl in range(NSLOT):
                        inst = tensor.matmul(pf[ec % 4][:],
                                             oalhs[lg % 2][:, 512 * sl + 128 * q:512 * sl + 128 * (q + 1)],
                                             wpcb[:, D * sl + 512 * ec:D * sl + 512 * (ec + 1)],
                                             start=(sl == 0), stop=(sl == NSLOT - 1))
                    inst.then_inc(s_p3m.h, 1)
                    s_p3m.v += 1

        @block.scalar
        def _(scalar):
            for ti in range(len(P3_TILES)):
                if ti >= 2:
                    scalar.wait_ge(s_p3s[ti % 2].h, 16 * (ti // 2))
                for ec in range(8):
                    eidx = 8 * ti + ec
                    scalar.wait_ge(s_p3m.h, eidx + 1)
                    scalar.activation(out=yev[ti % 2][:, 512 * ec:512 * (ec + 1)], in_=pf[ec % 4][:],
                                      func=AF.Copy).then_inc(s_p3e.h, 1)
                    s_p3e.v += 1

        @block.gpsimd
        def _(gpsimd):
            for ti, (r, q, hf) in enumerate(P3_TILES):
                ydst = y1dA if hf == 0 else y1dB
                gpsimd.wait_ge(s_p3e.h, 8 * (ti + 1))
                sp = s_p3s[ti % 2]
                gpsimd.dma_start(out=ydst[QB * r + 128 * q:QB * r + 128 * (q + 1), :],
                                 in_=yev[ti % 2][:]).then_inc(sp.h, 16)
                sp.v += 16
                if ti == NC * 4 - 1:
                    gpsimd.wait_ge(s_p3s[0].h, 16 * NC * 2)
                    gpsimd.wait_ge(s_p3s[1].h, 16 * NC * 2)
                    gpsimd.collective_compute("ReduceScatter", ALU.add, RG,
                                              [y1dA.opt()], [y1rA.opt()]).then_inc(s_rsa1.h, 1)
                    s_rsa1.v += 1
            gpsimd.wait_ge(s_p3s[0].h, 16 * NC * 4)
            gpsimd.wait_ge(s_p3s[1].h, 16 * NC * 4)
            gpsimd.collective_compute("ReduceScatter", ALU.add, RG,
                                      [y1dB.opt()], [y1rB.opt()]).then_inc(s_rsb1.h, 1)
            s_rsb1.v += 1

        @block.sync
        def _(sync):
            for hf in range(2):
                sync.wait_ge((s_rsa1 if hf == 0 else s_rsb1).h, 1)
                ysrc = y1rA if hf == 0 else y1rB
                for q in range(4):
                    rowi = 4 * hf + q
                    if rowi >= 1:
                        sync.wait_ge(s_r5v.h, 5 * rowi)
                    sync.dma_start(out=ylt[:], in_=ysrc[128 * q:128 * (q + 1), :]).then_inc(s_r5l.h, 16)
                    sync.dma_start(out=xt3[:],
                                   in_=xrows[QB * hf + 128 * q:QB * hf + 128 * (q + 1), :]).then_inc(s_r5l.h, 16)
                    s_r5l.v += 32

        @block.vector
        def _(vector):
            for hf in range(2):
                for q in range(4):
                    rowi = 4 * hf + q
                    vector.wait_ge(s_r5l.h, 32 * (rowi + 1))
                    if rowi == 0:
                        vector.wait_ge(s_bp.h, 16)
                    if rowi >= 1:
                        vector.wait_ge(s_r5r.h, 2 * rowi)
                    vedge(vector, vector.tensor_tensor(out=ybt[:], in0=ylt[:], in1=bprep[:], op=ALU.add))
                    for ch in range(8):
                        inst = vector.bn_stats(out=stat6[:, 6 * ch:6 * ch + 6], in_=ybt[:, 512 * ch:512 * (ch + 1)])
                    vedge(vector, inst)
                    vector.bn_aggr(out=stat2[:], in_=stat6[:]).then_inc(s_r5v.h, 1)
                    vector.wait_ge(s_r5r.h, 2 * rowi + 1)
                    vector.tensor_scalar(out=lnb[:], in0=ybt[:], scalar1=stat2[:, 0:1],
                                         scalar2=lnt2[:], op0=ALU.subtract, op1=ALU.mult).then_inc(s_r5v.h, 1)
                    vector.wait_ge(s_r5v.h, 5 * rowi + 2)
                    if rowi >= 1:
                        vector.wait_ge(s_x1s.h, 16 * rowi)
                    vector.tensor_tensor(out=x1b[:], in0=xt3[:], in1=lnb[:], op=ALU.add).then_inc(s_r5v.h, 1)
                    vector.wait_ge(s_r5v.h, 5 * rowi + 3)
                    for ch in range(8):
                        inst = vector.bn_stats(out=stat6[:, 6 * ch:6 * ch + 6], in_=x1b[:, 512 * ch:512 * (ch + 1)])
                    vedge(vector, inst)
                    vector.bn_aggr(out=stat2[:], in_=stat6[:]).then_inc(s_r5v.h, 1)
                    vector.wait_ge(s_r5r.h, 2 * rowi + 2)
                    if rowi >= 1:
                        vector.wait_ge(s_h2t.h, 4 * rowi)
                    vector.tensor_scalar(out=h2t[:], in0=x1b[:], scalar1=stat2[:, 0:1],
                                         scalar2=lnt2[:], op0=ALU.subtract, op1=ALU.mult).then_inc(s_r5v.h, 1)
                    s_r5v.v += 5
                    for g in range(4):
                        gg = 4 * rowi + g
                        vector.wait_ge(s_h2t.h, gg + 1)
                        if g == 0 and rowi >= 1:
                            if rowi <= 4:
                                vector.wait_ge(s_h2sA.h, 16 * min(rowi, 4))
                            if rowi > 4:
                                vector.wait_ge(s_h2sB.h, 16 * (rowi - 4))
                        vector.tensor_copy(out=h2stg[:, 1024 * g:1024 * (g + 1)],
                                           in_=pb[g % 2][:]).then_inc(s_h2e.h, 1)
                        s_h2e.v += 1

        @block.scalar
        def _(scalar):
            for rowi in range(8):
                for k2 in range(2):
                    scalar.wait_ge(s_r5v.h, 5 * rowi + (1 if k2 == 0 else 4))
                    sedge(scalar, scalar.activation(out=lnt1[:], in_=stat2[:, 1:2], func=AF.Ln, bias=epsv[:]))
                    scalar.activation(out=lnt2[:], in_=lnt1[:], func=AF.Exp, scale=-0.5).then_inc(s_r5r.h, 1)
                    s_r5r.v += 1

        @block.tensor
        def _(tensor):
            for rowi in range(8):
                tensor.wait_ge(s_r5v.h, 5 * rowi + 5)
                for g in range(4):
                    gg = 4 * rowi + g
                    if gg >= 2:
                        tensor.wait_ge(s_h2e.h, gg - 1)
                    for k in range(8):
                        ch = 8 * g + k
                        inst = tensor.transpose(pb[g % 2][:, 128 * k:128 * (k + 1)],
                                                h2t[:, 128 * ch:128 * (ch + 1)], identb[:])
                    inst.then_inc(s_h2t.h, 1)
                    s_h2t.v += 1

        @block.gpsimd
        def _(gpsimd):
            for hf in range(2):
                h2dst = h2slabA if hf == 0 else h2slabB
                sh2 = s_h2sA if hf == 0 else s_h2sB
                for q in range(4):
                    rowi = 4 * hf + q
                    gpsimd.wait_ge(s_r5v.h, 5 * rowi + 3)
                    gpsimd.dma_start(out=x1d[QB * hf + 128 * q:QB * hf + 128 * (q + 1), :],
                                     in_=x1b[:]).then_inc(s_x1s.h, 16)
                    s_x1s.v += 16
                    gpsimd.wait_ge(s_h2e.h, 4 * (rowi + 1))
                    dstap = bass.AP(h2dst.tensor, 128 * q, [[QB, 128], [128 * QB, 32], [1, 128]])
                    gpsimd.dma_start(out=dstap,
                                     in_=h2stg.ap().rearrange("p (j t) -> p j t", t=128)).then_inc(sh2.h, 16)
                    sh2.v += 16

        @block.gpsimd
        def _(gpsimd):
            gpsimd.wait_ge(s_h2sA.h, 16 * 4)
            gpsimd.collective_compute("AllGather", ALU.bypass, RG,
                                      [h2slabA.opt()], [h2gA.opt()]).then_inc(s_agh2a.h, 1)
            s_agh2a.v += 1
            gpsimd.wait_ge(s_h2sB.h, 16 * 4)
            gpsimd.collective_compute("AllGather", ALU.bypass, RG,
                                      [h2slabB.opt()], [h2gB.opt()]).then_inc(s_agh2b.h, 1)
            s_agh2b.v += 1

        FDE = 3

        @block.vector
        def _(vector):
            vector.tensor_copy(out=fscr[:], in_=epsv[:]).then_inc(s_fde.h, 1)

        @block.scalar
        def _(scalar):
            scalar.activation(out=fsc2[:], in_=epsv[:], func=AF.Copy).then_inc(s_fde.h, 1)

        @block.gpsimd
        def _(gpsimd):
            gpsimd.wait_ge(s_x1s.h, s_x1s.v)
            gpsimd.wait_ge(s_h2sA.h, s_h2sA.v)
            gpsimd.wait_ge(s_h2sB.h, s_h2sB.v)
            gpsimd.sem_inc(s_fde.h, 1)

        sD.close()
        sM.close()
        sE = contextlib.ExitStack()
        eE = sE.enter_context
        h2sb = eE(nc.sbuf_tensor("f_h2", [128, NCH * 1024], BF16))
        wgsb0 = eE(nc.sbuf_tensor("f_wg", [128, NCH * 128], BF16))
        wusb0 = eE(nc.sbuf_tensor("f_wu", [128, NCH * 128], BF16))
        wgsb = [wgsb0, wgsb0]
        wusb = [wusb0, wusb0]
        wdsb = eE(nc.sbuf_tensor("f_wd", [128, NNT * D], BF16))
        intr = eE(nc.sbuf_tensor("f_int", [128, NNT * 1024], BF16))
        silb = [eE(nc.sbuf_tensor(f"f_sil{i}", [128, 512], BF16)) for i in range(2)]
        y2ev = silb

        NGG = NNT * 4
        NDG = 64

        @block.sync
        def _(sync):
            sync.wait_ge(s_fde.h, FDE)
            for i in range(8):
                hf = i // 4
                rp = i % 4
                h2src = h2gA if hf == 0 else h2gB
                sync.wait_ge((s_agh2a if hf == 0 else s_agh2b).h, 1)
                if i >= 1:
                    sync.wait_ge(s_fg.h, NGG * i)
                for rr in range(2):
                    srcap = bass.AP(h2src.tensor, D * QB * (2 * rp + rr),
                                    [[QB, 128], [128 * QB, NCH], [1, QB]])
                    dstap = h2sb.ap().rearrange("p (j rt) -> p j rt", rt=1024)[:, :, 512 * rr:512 * (rr + 1)]
                    sync.dma_start(out=dstap, in_=srcap).then_inc(s_fh.h, 16)
                s_fh.v += 32
                for nt in range(NNT):
                    nw = NT_SIZES[nt]
                    widx = i * NNT + nt
                    if widx >= 1:
                        sync.wait_ge(s_fg.h, 4 * widx - 2)
                    sync.dma_start(out=wgsb[widx % 2][:, 0:NCH * nw],
                                   in_=wgbf[128 * nt:128 * (nt + 1), 0:NCH * nw]).then_inc(s_fwg.h, 16)
                    s_fwg.v += 16
                    if widx >= 1:
                        sync.wait_ge(s_fg.h, 4 * widx)
                    sync.dma_start(out=wusb[widx % 2][:, 0:NCH * nw],
                                   in_=wubf[128 * nt:128 * (nt + 1), 0:NCH * nw]).then_inc(s_fwu.h, 16)
                    s_fwu.v += 16
                    if i >= 1 and nt == 0:
                        sync.wait_ge(s_fy.h, NDG * i)
                    sync.dma_start(out=wdsb[0:nw, D * nt:D * (nt + 1)],
                                   in_=wdbf[128 * nt:128 * nt + nw, :]).then_inc(s_fwd.h, 16)
                    s_fwd.v += 16

        @block.tensor
        def _(tensor):
            tensor.wait_ge(s_fde.h, FDE)
            for i in range(8):
                tensor.wait_ge(s_fh.h, 32 * (i + 1))
                for nt in range(NNT):
                    nw = NT_SIZES[nt]
                    widx = i * NNT + nt
                    tensor.wait_ge(s_fwg.h, 16 * (widx + 1))
                    tensor.wait_ge(s_fwu.h, 16 * (widx + 1))
                    for mat in range(2):
                        wsb = (wgsb if mat == 0 else wusb)[widx % 2]
                        for rr in range(2):
                            bank = pf[2 * mat + rr]
                            if widx >= 1:
                                if mat == 0:
                                    tensor.wait_ge(s_fsil.h, 2 * widx - 2 + rr + 1)
                                else:
                                    tensor.wait_ge(s_fint.h, 2 * widx - 2 + rr + 1)
                            for j in range(NCH):
                                inst = tensor.matmul(bank[0:nw, 0:512],
                                                     wsb[:, nw * j:nw * j + nw],
                                                     h2sb[:, 1024 * j + 512 * rr:1024 * j + 512 * (rr + 1)],
                                                     start=(j == 0), stop=(j == NCH - 1))
                            inst.then_inc(s_fg.h, 1)
                            s_fg.v += 1
                tensor.wait_ge(s_fint.h, 2 * NNT * (i + 1))
                tensor.wait_ge(s_fwd.h, 16 * NNT * (i + 1))
                for ts in range(8):
                    for ec in range(8):
                        didx = NDG * i + 8 * ts + ec
                        if didx >= 2:
                            tensor.wait_ge(s_fye.h, didx - 1)
                        for nt in range(NNT):
                            nw = NT_SIZES[nt]
                            inst = tensor.matmul(pf[4 + didx % 2][:],
                                                 intr[0:nw, 1024 * nt + 128 * ts:1024 * nt + 128 * (ts + 1)],
                                                 wdsb[0:nw, D * nt + 512 * ec:D * nt + 512 * (ec + 1)],
                                                 start=(nt == 0), stop=(nt == NNT - 1))
                        inst.then_inc(s_fy.h, 1)
                        s_fy.v += 1

        @block.scalar
        def _(scalar):
            for i in range(8):
                if i >= 1:
                    scalar.wait_ge(s_fys[0].h, 16 * NDG * i // 2)
                    scalar.wait_ge(s_fys[1].h, 16 * NDG * i // 2)
                for nt in range(NNT):
                    widx = i * NNT + nt
                    nw = NT_SIZES[nt]
                    for rr in range(2):
                        sidx = 2 * widx + rr
                        scalar.wait_ge(s_fg.h, 4 * widx + rr + 1)
                        if sidx >= 2:
                            scalar.wait_ge(s_fint.h, sidx - 1)
                        scalar.activation(out=silb[sidx % 2][0:nw, :], in_=pf[rr][0:nw, 0:512],
                                          func=AF.Silu).then_inc(s_fsil.h, 1)
                        s_fsil.v += 1
                for ts in range(8):
                    for ec in range(8):
                        didx = NDG * i + 8 * ts + ec
                        scalar.wait_ge(s_fy.h, didx + 1)
                        if didx >= 2:
                            scalar.wait_ge(s_fys[didx % 2].h, 16 * (didx // 2))
                        scalar.activation(out=y2ev[didx % 2][:], in_=pf[4 + didx % 2][:],
                                          func=AF.Copy).then_inc(s_fye.h, 1)
                        s_fye.v += 1

        @block.vector
        def _(vector):
            vector.wait_ge(s_fde.h, FDE)
            for i in range(8):
                if i >= 1:
                    vector.wait_ge(s_fy.h, NDG * i)
                for nt in range(NNT):
                    widx = i * NNT + nt
                    nw = NT_SIZES[nt]
                    for rr in range(2):
                        sidx = 2 * widx + rr
                        vector.wait_ge(s_fsil.h, sidx + 1)
                        vector.wait_ge(s_fg.h, 4 * widx + 2 + rr + 1)
                        vector.tensor_tensor(out=intr[0:nw, 1024 * nt + 512 * rr:1024 * nt + 512 * (rr + 1)],
                                             in0=silb[sidx % 2][0:nw, :],
                                             in1=pf[2 + rr][0:nw, 0:512],
                                             op=ALU.mult).then_inc(s_fint.h, 1)
                        s_fint.v += 1

        @block.gpsimd
        def _(gpsimd):
            for i in range(8):
                hf = i // 4
                rp = i % 4
                ydst = y2dA if hf == 0 else y2dB
                for ts in range(8):
                    for ec in range(8):
                        didx = NDG * i + 8 * ts + ec
                        gpsimd.wait_ge(s_fye.h, didx + 1)
                        r_ = 2 * rp + ts // 4
                        tq = ts % 4
                        sp = s_fys[didx % 2]
                        gpsimd.dma_start(out=ydst[QB * r_ + 128 * tq:QB * r_ + 128 * (tq + 1),
                                                  512 * ec:512 * (ec + 1)],
                                         in_=y2ev[didx % 2][:]).then_inc(sp.h, 16)
                        sp.v += 16
                if i == 3:
                    gpsimd.wait_ge(s_fys[0].h, 16 * NDG * 2)
                    gpsimd.wait_ge(s_fys[1].h, 16 * NDG * 2)
                    gpsimd.collective_compute("ReduceScatter", ALU.add, RG,
                                              [y2dA.opt()], [y2rA.opt()]).then_inc(s_rsa2.h, 1)
                    s_rsa2.v += 1
            gpsimd.wait_ge(s_fys[0].h, 16 * NDG * 4)
            gpsimd.wait_ge(s_fys[1].h, 16 * NDG * 4)
            gpsimd.collective_compute("ReduceScatter", ALU.add, RG,
                                      [y2dB.opt()], [y2rB.opt()]).then_inc(s_rsb2.h, 1)
            s_rsb2.v += 1

        FEF = 3

        @block.vector
        def _(vector):
            vector.tensor_copy(out=fscr[:], in_=epsv[:]).then_inc(s_fef.h, 1)

        @block.scalar
        def _(scalar):
            scalar.activation(out=fsc2[:], in_=epsv[:], func=AF.Copy).then_inc(s_fef.h, 1)

        @block.gpsimd
        def _(gpsimd):
            gpsimd.wait_ge(s_fys[0].h, s_fys[0].v)
            gpsimd.wait_ge(s_fys[1].h, s_fys[1].v)
            gpsimd.sem_inc(s_fef.h, 1)

        sE.close()
        sF = contextlib.ExitStack()
        eF = sF.enter_context
        y5l2 = [eF(nc.sbuf_tensor(f"p5_y{i}", [128, D], BF16)) for i in range(2)]
        x5l2 = [eF(nc.sbuf_tensor(f"p5_x{i}", [128, D], F32)) for i in range(2)]
        ln5 = eF(nc.sbuf_tensor("p5_ln", [128, D], F32))
        dln = eF(nc.sbuf_tensor("p5_d", [128, D], F32))
        o5 = eF(nc.sbuf_tensor("p5_o", [128, D], F32))
        amx = eF(nc.sbuf_tensor("p5_amx", [128, 1], F32))
        msk = eF(nc.sbuf_tensor("p5_msk", [128, 1], F32))
        rst2 = eF(nc.sbuf_tensor("p5_rst2", [128, 1], F32))

        @block.sync
        def _(sync):
            sync.wait_ge(s_fef.h, FEF)
            for hf in range(2):
                sync.wait_ge((s_rsa2 if hf == 0 else s_rsb2).h, 1)
                ysrc = y2rA if hf == 0 else y2rB
                for q in range(4):
                    rowi = 4 * hf + q
                    if rowi >= 2:
                        sync.wait_ge(s_p5v.h, 3 * (rowi - 1))
                    sp = s_p5l[rowi % 2]
                    sync.dma_start(out=y5l2[rowi % 2][:], in_=ysrc[128 * q:128 * (q + 1), :]).then_inc(sp.h, 16)
                    sync.dma_start(out=x5l2[rowi % 2][:],
                                   in_=x1d[QB * hf + 128 * q:QB * hf + 128 * (q + 1), :]).then_inc(sp.h, 16)
                    sp.v += 32

        @block.vector
        def _(vector):
            vector.wait_ge(s_fef.h, FEF)
            for rowi in range(8):
                y5l = y5l2[rowi % 2]
                x5l = x5l2[rowi % 2]
                vector.wait_ge(s_p5l[rowi % 2].h, 32 * (rowi // 2 + 1))
                if rowi >= 1:
                    vector.wait_ge(s_p5r.h, rowi)
                for ch in range(8):
                    inst = vector.bn_stats(out=stat6[:, 6 * ch:6 * ch + 6], in_=y5l[:, 512 * ch:512 * (ch + 1)])
                vedge(vector, inst)
                vector.bn_aggr(out=stat2[:], in_=stat6[:]).then_inc(s_p5v.h, 1)
                vector.wait_ge(s_p5v.h, 3 * rowi + 1)
                vedge(vector, vector.tensor_scalar(out=msk[:], in0=stat2[:, 1:2], scalar1=65520.0,
                                                   scalar2=None, op0=ALU.is_lt))
                vector.wait_ge(s_p5r.h, rowi + 1)
                vedge(vector, vector.tensor_tensor(out=rst2[:], in0=msk[:], in1=lnt2[:], op=ALU.mult))
                vector.tensor_scalar(out=ln5[:], in0=y5l[:], scalar1=stat2[:, 0:1],
                                     scalar2=rst2[:], op0=ALU.subtract, op1=ALU.mult).then_inc(s_p5v.h, 1)
                vector.wait_ge(s_p5v.h, 3 * rowi + 2)
                if rowi >= 1:
                    vector.wait_ge(s_p5s.h, 16 * rowi)
                vector.tensor_tensor(out=o5[:], in0=x5l[:], in1=ln5[:], op=ALU.add).then_inc(s_p5v.h, 1)
                s_p5v.v += 3

        @block.scalar
        def _(scalar):
            for rowi in range(8):
                scalar.wait_ge(s_p5v.h, 3 * rowi + 1)
                sedge(scalar, scalar.activation(out=lnt1[:], in_=stat2[:, 1:2], func=AF.Ln, bias=epsv[:]))
                scalar.activation(out=lnt2[:], in_=lnt1[:], func=AF.Exp, scale=-0.5).then_inc(s_p5r.h, 1)
                s_p5r.v += 1

        @block.gpsimd
        def _(gpsimd):
            for hf in range(2):
                for q in range(4):
                    rowi = 4 * hf + q
                    gpsimd.wait_ge(s_p5v.h, 3 * rowi + 3)
                    gpsimd.dma_start(out=out[QB * hf + 128 * q:QB * hf + 128 * (q + 1), :],
                                     in_=o5[:]).then_inc(s_p5s.h, 16)
                    s_p5s.v += 16
            if dbg:
                srcs = {"h0": h0g, "qkv": qkvd.rearrange("a p t -> (a p) t"), "oa": oaslab,
                        "y1a": y1rA, "x1": x1d, "h2a": h2gA, "h2b": h2gB,
                        "y2a": y2rA, "y2b": y2rB}
                for name, ap_ in dbg.items():
                    gpsimd.dma_start(out=ap_, in_=srcs[name]).then_inc(s_p5s.h, 16)
                    s_p5s.v += 16
            gpsimd.wait_ge(s_p5s.h, s_p5s.v)

        sF.close()

    return nc



def stage_inputs(inputs):
    x = np.asarray(inputs["L0_attn_hidden_states"]).reshape(S, D)
    w_attn = np.asarray(inputs["L0_attn_w_attn"])
    b_attn = np.asarray(inputs["L0_attn_b_attn"])
    w_proj = np.asarray(inputs["L0_attn_w_proj"])
    b_proj = np.asarray(inputs["L0_attn_b_proj"])
    gqw = np.asarray(inputs["L0_ffn_gate_qweight"])
    gsc = np.asarray(inputs["L0_ffn_gate_scales"])
    gzr = np.asarray(inputs["L0_ffn_gate_zeros"])
    uqw = np.asarray(inputs["L0_ffn_up_qweight"])
    usc = np.asarray(inputs["L0_ffn_up_scales"])
    uzr = np.asarray(inputs["L0_ffn_up_zeros"])
    dqw = np.asarray(inputs["L0_ffn_down_qweight"])
    dsc_ = np.asarray(inputs["L0_ffn_down_scales"])
    dzr = np.asarray(inputs["L0_ffn_down_zeros"])

    bf = ml_dtypes.bfloat16
    p = np.arange(128)
    shifts = (4 * (p % 8)).astype(np.int32).reshape(128, 1)
    identf = np.eye(128, dtype=np.float32)
    onesb = np.ones((128, 128), dtype=bf)
    g_of_m = None

    in_maps = []
    for c in range(NC):
        d = {}
        d["xrows"] = np.ascontiguousarray(x[TPC * c:TPC * (c + 1), :])
        cols = []
        bcols = []
        for s in range(NSLOT):
            h = 8 * s + c
            for g0, base in ((0, 0), (1, D), (2, 2 * D)):
                cols.append(w_attn[:, base + HD * h:base + HD * (h + 1)])
                bcols.append(b_attn[base + HD * h:base + HD * (h + 1)])
        d["wqkv"] = np.ascontiguousarray(np.concatenate(cols, axis=1))
        d["bqkvT"] = np.ascontiguousarray(np.stack(bcols, axis=1))
        d["wprojc"] = np.ascontiguousarray(
            np.concatenate([w_proj[HD * (8 * s + c):HD * (8 * s + c + 1), :] for s in range(NSLOT)], axis=0))
        d["bproj"] = np.ascontiguousarray(b_proj.reshape(1, D))

        pt = np.zeros((NSLOT * 5 * 128, QB), dtype=bf)
        eb = np.zeros((128, NSLOT * 64), dtype=np.float32)
        jj = np.arange(QB)
        for s in range(NSLOT):
            sig = float(SLOPES[8 * s + c])
            for v in range(4):
                arg = sig * (128.0 * v + p[:, None] - jj[None, :])
                m = (jj[None, :] >= 128 * v + p[:, None])
                with np.errstate(over="ignore", under="ignore"):
                    tile = np.where(m, np.exp(np.minimum(arg, 0.0)), 0.0)
                pt[(5 * s + v) * 128:(5 * s + v + 1) * 128, :] = tile.astype(bf)
            arg = sig * (p[:, None] - jj[None, :] - 128.0)
            with np.errstate(under="ignore"):
                pt[(5 * s + 4) * 128:(5 * s + 5) * 128, :] = np.exp(arg).astype(bf)
            for iu in range(64):
                u = 3 - iu
                if u <= -1:
                    eb[:, 64 * s + iu] = np.float32(max(128.0 * sig * (u + 1), -1e30))
        d["patt"] = pt
        d["ebias"] = eb
        d["shifts"] = shifts
        d["identf"] = identf
        d["onesb"] = onesb

        nsl0 = NSL * c
        d["gq"] = np.ascontiguousarray(gqw[:, nsl0:nsl0 + NSL])
        d["gs"] = np.ascontiguousarray(gsc[:, nsl0:nsl0 + NSL])
        d["gz"] = np.ascontiguousarray(gzr[:, nsl0 // 8:(nsl0 + NSL) // 8])
        d["uq"] = np.ascontiguousarray(uqw[:, nsl0:nsl0 + NSL])
        d["us"] = np.ascontiguousarray(usc[:, nsl0:nsl0 + NSL])
        d["uz"] = np.ascontiguousarray(uzr[:, nsl0 // 8:(nsl0 + NSL) // 8])
        d["dq"] = np.ascontiguousarray(dqw[nsl0 // 8:(nsl0 + NSL) // 8, :])
        if g_of_m is None:
            pass
        gm = (nsl0 + np.arange(NSL)) // 128
        d["dsx"] = np.ascontiguousarray(dsc_[gm, :])
        d["dzx"] = np.ascontiguousarray(dzr[gm, :])
        in_maps.append(d)
    return in_maps


def kernel(**inputs):
    in_maps = stage_inputs(inputs)
    nc = build_nc(debug=DEBUG_DUMPS)
    res = run_bass_kernel_spmd(nc, in_maps, core_ids=list(range(NC)))
    outs = [np.asarray(res.results[c]["out"]) for c in range(NC)]
    full = np.concatenate(outs, axis=0).reshape(1, S, D).astype(np.float32)
    if DEBUG_DUMPS:
        kernel._dbg = res.results
    return full


# version 07
# speedup vs baseline: 1.0616x; 1.0008x over previous
import contextlib
import math

import numpy as np
import ml_dtypes

import concourse.bass as bass
import concourse.mybir as mybir
from concourse.bass_utils import run_bass_kernel_spmd

F32 = mybir.dt.float32
F16 = mybir.dt.float16
BF16 = mybir.dt.bfloat16
I32 = mybir.dt.int32
AF = mybir.ActivationFunctionType
ALU = mybir.AluOpType

S = 8192
D = 4096
NH = 32
HD = 128
INTER = 11008
NC = 8
TPC = S // NC
NSLOT = 4
NSL = INTER // NC
NQB = 16
QB = 512
NCH = D // 128
WIN_THRESH = 100.0

DEBUG_DUMPS = ()


def _alibi_slopes(n_heads):
    cp2 = 2 ** math.floor(math.log2(n_heads))
    base = 2 ** (-(2 ** (-(math.log2(cp2) - 3))))
    slopes = base ** np.arange(1, cp2 + 1, dtype=np.float64)
    if cp2 != n_heads:
        extra_base = 2 ** (-(2 ** (-(math.log2(2 * cp2) - 3))))
        n_rem = min(cp2, n_heads - cp2)
        extra = extra_base ** np.arange(1, 1 + 2 * n_rem, 2, dtype=np.float64)
        slopes = np.concatenate([slopes, extra])
    return slopes


SLOPES = _alibi_slopes(NH)

B_SLOT = []
for _s in range(NSLOT):
    sig_min = SLOPES[8 * _s + 7]
    B_SLOT.append(min(60, int(math.floor((127 + WIN_THRESH / sig_min) / 128.0))))

N_STEPS = [[4 + min(4 * b, B_SLOT[s]) for b in range(NQB)] for s in range(NSLOT)]
TOTAL_STEPS = sum(sum(r) for r in N_STEPS)

NT_SIZES = [128] * 10 + [96]
NNT = len(NT_SIZES)
NT_OFF = [sum(NT_SIZES[:i]) for i in range(NNT)]

STEP_U = []
BLOCK_RANGE = {}
_g = 0
for _s in range(NSLOT):
    for _b in range(NQB):
        n = N_STEPS[_s][_b]
        BLOCK_RANGE[(_s, _b)] = (_g, _g + n)
        for _u in range(-(n - 4), 4):
            STEP_U.append((_s, _b, _u))
        _g += n
assert _g == TOTAL_STEPS

DQ_UNITS = [("zprep", 0), ("zprep", 1)]
for _m in range(2):
    for _j in range(NCH):
        DQ_UNITS.append(("gu", _m, _j))
for _mt in range(NNT):
    for _ec in range(4):
        DQ_UNITS.append(("down", _mt, _ec))
N_DQ = len(DQ_UNITS)
DQL2_AT = []
_v2 = [0, 0]
for _u in range(N_DQ):
    _v2[_u % 2] += 32 if DQ_UNITS[_u][0] == "zprep" else 48
    DQL2_AT.append(_v2[_u % 2])
DQS2_AT = []
_v3 = [0, 0]
for _u in range(N_DQ):
    _v3[_u % 2] += 16 * (NNT if DQ_UNITS[_u][0] == "gu" else 1)
    DQS2_AT.append(_v3[_u % 2])
DQ_W_A = (0, 15)
DQ_W_B = (15, 51)
DQ_W_S = [(51, 71), (71, 91), (91, 101), (101, N_DQ)]


class Sem:
    __slots__ = ("h", "v")

    def __init__(self, h):
        self.h = h
        self.v = 0

    def inc(self, inst, n=1):
        inst.then_inc(self.h, n)
        self.v += n
        return self.v


def build_nc(debug=(), detect_races=True):
    nc = bass.Bass("TRN2", target_bir_lowering=False, detect_race_conditions=detect_races)

    xrows = nc.dram_tensor("xrows", [TPC, D], F32, kind="ExternalInput").ap()
    wqkv = nc.dram_tensor("wqkv", [D, NSLOT * 3 * HD], F32, kind="ExternalInput").ap()
    bqkvT = nc.dram_tensor("bqkvT", [HD, 12], F32, kind="ExternalInput").ap()
    wprojc = nc.dram_tensor("wprojc", [NSLOT * HD, D], F32, kind="ExternalInput").ap()
    bproj = nc.dram_tensor("bproj", [1, D], F32, kind="ExternalInput").ap()
    patt = nc.dram_tensor("patt", [NSLOT * 5 * 128, QB], BF16, kind="ExternalInput").ap()
    ebias = nc.dram_tensor("ebias", [128, NSLOT * 64], F32, kind="ExternalInput").ap()
    shifts_in = nc.dram_tensor("shifts", [128, 1], I32, kind="ExternalInput").ap()
    identf_in = nc.dram_tensor("identf", [128, 128], F32, kind="ExternalInput").ap()
    onesb_in = nc.dram_tensor("onesb", [128, 128], BF16, kind="ExternalInput").ap()
    gq = nc.dram_tensor("gq", [512, NSL], I32, kind="ExternalInput").ap()
    gs = nc.dram_tensor("gs", [32, NSL], F16, kind="ExternalInput").ap()
    gz = nc.dram_tensor("gz", [32, NSL // 8], I32, kind="ExternalInput").ap()
    uq = nc.dram_tensor("uq", [512, NSL], I32, kind="ExternalInput").ap()
    us = nc.dram_tensor("us", [32, NSL], F16, kind="ExternalInput").ap()
    uz = nc.dram_tensor("uz", [32, NSL // 8], I32, kind="ExternalInput").ap()
    dq = nc.dram_tensor("dq", [NSL // 8, D], I32, kind="ExternalInput").ap()
    dsx = nc.dram_tensor("dsx", [NSL, D], F16, kind="ExternalInput").ap()
    dzx = nc.dram_tensor("dzx", [NSL, D // 8], I32, kind="ExternalInput").ap()
    out = nc.dram_tensor("out", [TPC, D], F32, kind="ExternalOutput").ap()

    h0slab = nc.dram_tensor("h0slab", [D, TPC], BF16).ap()
    h0g = nc.dram_tensor("h0g", [NC * D, TPC], BF16).ap()
    qkvd = nc.dram_tensor("qkvd", [12, HD, S], BF16).ap()
    oaslab = nc.dram_tensor("oaslab", [NSLOT * HD, S], BF16).ap()
    y1dA = nc.dram_tensor("y1dA", [NC * QB, D], BF16).ap()
    y1dB = nc.dram_tensor("y1dB", [NC * QB, D], BF16).ap()
    y1rA = nc.dram_tensor("y1rA", [QB, D], BF16).ap()
    y1rB = nc.dram_tensor("y1rB", [QB, D], BF16).ap()
    x1d = nc.dram_tensor("x1d", [TPC, D], F32).ap()
    h2slabA = nc.dram_tensor("h2slabA", [D, QB], BF16).ap()
    h2slabB = nc.dram_tensor("h2slabB", [D, QB], BF16).ap()
    h2gA = nc.dram_tensor("h2gA", [NC * D, QB], BF16).ap()
    h2gB = nc.dram_tensor("h2gB", [NC * D, QB], BF16).ap()
    wgbf = nc.dram_tensor("wgbf", [NNT * 128, NCH * 128], BF16).ap()
    wubf = nc.dram_tensor("wubf", [NNT * 128, NCH * 128], BF16).ap()
    wdbf = nc.dram_tensor("wdbf", [NSL, D], BF16).ap()
    zpgd = nc.dram_tensor("zpgd", [32, NSL], F32).ap()
    zpud = nc.dram_tensor("zpud", [32, NSL], F32).ap()
    y2dA = nc.dram_tensor("y2dA", [NC * QB, D], BF16).ap()
    y2dB = nc.dram_tensor("y2dB", [NC * QB, D], BF16).ap()
    y2rA = nc.dram_tensor("y2rA", [QB, D], BF16).ap()
    y2rB = nc.dram_tensor("y2rB", [QB, D], BF16).ap()

    dbg = {}
    for name, src_shape, dt_ in (
        ("h0", [NC * D, TPC], BF16), ("qkv", [12 * HD, S], BF16),
        ("oa", [NSLOT * HD, S], BF16), ("y1a", [QB, D], BF16),
        ("x1", [TPC, D], F32), ("h2a", [NC * D, QB], BF16),
        ("h2b", [NC * D, QB], BF16), ("y2a", [QB, D], BF16),
        ("y2b", [QB, D], BF16),
    ):
        if name in debug or (name[:2] in ("h2", "y2") and name[:2] in debug) or \
           (name in ("y1a",) and "y1" in debug):
            dbg[name] = nc.dram_tensor("dbg_" + name, src_shape, dt_, kind="ExternalOutput").ap()

    RG = [list(range(NC))]

    ctx = contextlib.ExitStack()
    with ctx:
        e = ctx.enter_context

        shifts = e(nc.sbuf_tensor("s_shifts", [128, 1], I32))
        epsv = e(nc.sbuf_tensor("s_eps", [128, 1], F32))
        sbq = e(nc.sbuf_tensor("s_bqkv", [128, 12], F32))
        stat6 = e(nc.sbuf_tensor("s_stat6", [128, 8 * 6], F32))
        stat2 = e(nc.sbuf_tensor("s_stat2", [128, 2], F32))
        lnt1 = e(nc.sbuf_tensor("s_lnt1", [128, 1], F32))
        lnt2 = e(nc.sbuf_tensor("s_lnt2", [128, 1], F32))
        fscr = e(nc.sbuf_tensor("s_fscr", [128, 1], F32))
        fsc2 = e(nc.sbuf_tensor("s_fsc2", [128, 1], F32))
        sM = contextlib.ExitStack()
        eM = sM.enter_context
        identb = eM(nc.sbuf_tensor("s_identb", [128, 128], BF16))
        identf = eM(nc.sbuf_tensor("s_identf", [128, 128], F32))
        onesb = eM(nc.sbuf_tensor("s_onesb", [128, 128], BF16))
        sebias = eM(nc.sbuf_tensor("s_ebias", [128, NSLOT * 64], F32))

        pf = [e(nc.psum_tensor(f"pf{i}", [128, 512], F32)) for i in range(6)]
        pb = [e(nc.psum_tensor(f"pb{i}", [128, 1024], BF16)) for i in range(2)]

        def sem(name):
            return Sem(e(nc.semaphore(name)))

        s_init = sem("init")
        s_dql = [sem("dql0"), sem("dql1")]; s_dqv = sem("dqv")
        s_dqs = [sem("dqs0"), sem("dqs1")]
        s_p0l = [sem("p0l0"), sem("p0l1")]; s_p0v = sem("p0v"); s_rst = sem("rst")
        s_p0t = sem("p0t"); s_p0e = sem("p0e"); s_p0s = sem("p0s")
        s_ag0 = sem("ag0")
        s_p1w = [sem("p1w0"), sem("p1w1")]; s_p1wc = sem("p1wc")
        s_p1h = [sem("p1h0"), sem("p1h1"), sem("p1h2"), sem("p1h3")]
        s_p1j = sem("p1j"); s_p1e = sem("p1e")
        s_p1s = [sem("p1s0"), sem("p1s1")]
        s_patt = sem("patt")
        s_kf = [sem("skf0"), sem("skf1")]
        s_vf = [sem("svf0"), sem("svf1")]
        s_qf = [sem("sqf0"), sem("sqf1")]
        s_vt = sem("vt"); s_vte = sem("vte")
        s_sS = sem("sS"); s_sX = sem("sX"); s_sM = sem("sM"); s_sAV = sem("sAV")
        s_sR = sem("sR"); s_sE = sem("sE"); s_sON = sem("sON"); s_oas = sem("oas")
        s_bp = sem("sbp")
        s_p3l = [sem("p3l0"), sem("p3l1")]
        s_p3w = [sem("p3w0"), sem("p3w1")]; s_p3wc = sem("p3wc")
        s_p3m = sem("p3m"); s_p3e = sem("p3e")
        s_p3s = [sem("p3s0"), sem("p3s1")]
        s_rsa1 = sem("rsa1"); s_rsb1 = sem("rsb1")
        s_r5l = sem("r5l"); s_r5v = sem("r5v"); s_r5r = sem("r5r")
        s_h2t = sem("h2t"); s_h2e = sem("h2e")
        s_x1s = sem("x1s"); s_h2sA = sem("h2sA"); s_h2sB = sem("h2sB")
        s_agh2a = sem("agh2a"); s_agh2b = sem("agh2b")
        s_fh = sem("fh"); s_fwg = sem("fwg"); s_fwu = sem("fwu"); s_fwd = sem("fwd")
        s_fg = sem("fg"); s_fsil = sem("fsil"); s_fint = sem("fint")
        s_fy = sem("fy"); s_fye = sem("fye")
        s_fys = [sem("fys0"), sem("fys1")]
        s_rsa2 = sem("rsa2"); s_rsb2 = sem("rsb2")
        s_p5l = [sem("p5l0"), sem("p5l1")]
        s_p5v = sem("p5v"); s_p5r = sem("p5r"); s_p5s = sem("p5s")
        s_dv = sem("dvself"); s_sc = sem("scself")
        s_fab = sem("fab"); s_fbc = sem("fbc"); s_fcd = sem("fcd")
        s_fde = sem("fde"); s_fef = sem("fef")

        def vedge(vector, inst):
            inst.then_inc(s_dv.h, 1)
            s_dv.v += 1
            vector.wait_ge(s_dv.h, s_dv.v)

        def sedge(scalar, inst):
            inst.then_inc(s_sc.h, 1)
            s_sc.v += 1
            scalar.wait_ge(s_sc.h, s_sc.v)

        block = e(nc.Block())

        @block.sync
        def _(sync):
            sync.dma_start(out=shifts[:], in_=shifts_in).then_inc(s_init.h, 16)
            sync.dma_start(out=identf[:], in_=identf_in).then_inc(s_init.h, 16)
            sync.dma_start(out=onesb[:], in_=onesb_in).then_inc(s_init.h, 16)
            sync.dma_start(out=sebias[:], in_=ebias).then_inc(s_init.h, 16)
            sync.dma_start(out=sbq[:], in_=bqkvT).then_inc(s_init.h, 16)
            s_init.v += 80

        @block.vector
        def _(vector):
            vector.wait_ge(s_init.h, 80)
            vector.memset(epsv[:], 1e-5)
            vector.tensor_copy(out=identb[:], in_=identf[:]).then_inc(s_init.h, 1)
            s_init.v += 1
        INIT_DONE = s_init.v

        sDQ = contextlib.ExitStack()
        eQ = sDQ.enter_context
        dqr = [eQ(nc.sbuf_tensor(f"dq_qr{i}", [128, NSL], I32)) for i in range(2)]
        dsc = [eQ(nc.sbuf_tensor(f"dq_sc{i}", [128, NSL], F16)) for i in range(2)]
        dzp = [eQ(nc.sbuf_tensor(f"dq_zp{i}", [128, NSL], F32)) for i in range(2)]
        dnib = eQ(nc.sbuf_tensor("dq_nib", [128, NSL], I32))
        dt1 = eQ(nc.sbuf_tensor("dq_t1", [128, NSL], F32))
        dwb = [eQ(nc.sbuf_tensor(f"dq_wb{i}", [128, NSL], BF16)) for i in range(2)]
        dzrow = [eQ(nc.sbuf_tensor(f"dq_zrow{i}", [128, 128], I32)) for i in range(2)]
        dzaux = eQ(nc.sbuf_tensor("dq_zaux", [128, 1024], I32))
        zpz = eQ(nc.sbuf_tensor("dq_zpz", [32, NSL // 8], I32))
        zps = eQ(nc.sbuf_tensor("dq_zps", [32, NSL], F16))

        gu_src = [(gq, gs, zpgd, wgbf), (uq, us, zpud, wubf)]

        def dq_loads(gpsimd, lo, hi):
            for u in range(lo, hi):
                kind = DQ_UNITS[u]
                sl_ = s_dql[u % 2]
                if kind[0] == "zprep":
                    m_ = kind[1]
                    if u == 1:
                        gpsimd.wait_ge(s_dqv.h, 1)
                    zsrc = gz if m_ == 0 else uz
                    ssrc = gs if m_ == 0 else us
                    gpsimd.dma_start(out=zpz[:], in_=zsrc).then_inc(sl_.h, 16)
                    gpsimd.dma_start(out=zps[:], in_=ssrc).then_inc(sl_.h, 16)
                    sl_.v += 32
                elif kind[0] == "gu":
                    m_, j_ = kind[1], kind[2]
                    qsrc, ssrc, zpd_, _w = gu_src[m_]
                    if u >= 2:
                        gpsimd.wait_ge(s_dqv.h, u - 1)
                    gpsimd.wait_ge(s_dqs[m_].h, 16)
                    rep = bass.AP(qsrc.tensor, 16 * j_ * NSL, [[NSL, 16], [0, 8], [1, NSL]])
                    gpsimd.dma_start(out=dqr[u % 2][:, 0:NSL], in_=rep).then_inc(sl_.h, 16)
                    srep = bass.AP(ssrc.tensor, j_ * NSL, [[0, 128], [1, NSL]])
                    gpsimd.dma_start(out=dsc[u % 2][:, 0:NSL], in_=srep).then_inc(sl_.h, 16)
                    zrep = bass.AP(zpd_.tensor, j_ * NSL, [[0, 128], [1, NSL]])
                    gpsimd.dma_start(out=dzp[u % 2][:, 0:NSL], in_=zrep).then_inc(sl_.h, 16)
                    sl_.v += 48
                else:
                    _, mt_, ec_ = kind
                    nw = NT_SIZES[mt_]
                    if u >= 2:
                        gpsimd.wait_ge(s_dqv.h, u - 1)
                    rep = bass.AP(dq.tensor, 16 * mt_ * D + 1024 * ec_, [[D, nw // 8], [0, 8], [1, 1024]])
                    gpsimd.dma_start(out=dqr[u % 2][0:nw, 0:1024], in_=rep).then_inc(sl_.h, 16)
                    gpsimd.dma_start(out=dsc[u % 2][0:nw, 0:1024],
                                     in_=dsx[128 * mt_:128 * mt_ + nw, 1024 * ec_:1024 * (ec_ + 1)]).then_inc(sl_.h, 16)
                    gpsimd.dma_start(out=dzrow[u % 2][0:nw, :],
                                     in_=dzx[128 * mt_:128 * mt_ + nw, 128 * ec_:128 * (ec_ + 1)]).then_inc(sl_.h, 16)
                    sl_.v += 48

        def dq_vec(vector, lo, hi):
            for u in range(lo, hi):
                kind = DQ_UNITS[u]
                vector.wait_ge(s_dql[u % 2].h, DQL2_AT[u])
                if u == 1:
                    vector.wait_ge(s_dqs[0].h, 16)
                if u == 2:
                    vector.wait_ge(s_dqs[1].h, 16)
                if u >= 2:
                    vector.wait_ge(s_dqs[u % 2].h, DQS2_AT[u - 2])
                if kind[0] == "zprep":
                    zpe_ = dnib[0:32, 0:NSL]
                    zpf_ = dt1[0:32, 0:NSL]
                    zpg_ = dt1[32:64, 0:NSL]
                    for jn in range(8):
                        inst = vector.tensor_scalar(out=zpe_[:, jn::8], in0=zpz[:], scalar1=4 * jn,
                                                    scalar2=15, op0=ALU.logical_shift_right, op1=ALU.bitwise_and)
                    vedge(vector, inst)
                    vedge(vector, vector.tensor_scalar(out=zpf_[:], in0=zpe_[:], scalar1=1.0, scalar2=None, op0=ALU.add))
                    vector.tensor_tensor(out=zpg_[:], in0=zpf_[:], in1=zps[:], op=ALU.mult).then_inc(s_dqv.h, 1)
                elif kind[0] == "gu":
                    u2 = u % 2
                    vedge(vector, vector.tensor_scalar(out=dnib[:, 0:NSL], in0=dqr[u2][:, 0:NSL], scalar1=shifts[:],
                                                       scalar2=15, op0=ALU.logical_shift_right, op1=ALU.bitwise_and))
                    vedge(vector, vector.tensor_tensor(out=dt1[:, 0:NSL], in0=dnib[:, 0:NSL],
                                                       in1=dsc[u2][:, 0:NSL], op=ALU.mult))
                    vector.tensor_tensor(out=dwb[u2][:, 0:NSL], in0=dt1[:, 0:NSL], in1=dzp[u2][:, 0:NSL],
                                         op=ALU.subtract).then_inc(s_dqv.h, 1)
                else:
                    _, mt_, ec_ = kind
                    u2 = u % 2
                    for jn in range(8):
                        inst = vector.tensor_scalar(out=dzaux[:, jn::8], in0=dzrow[u % 2][:], scalar1=4 * jn,
                                                    scalar2=15, op0=ALU.logical_shift_right, op1=ALU.bitwise_and)
                    vedge(vector, inst)
                    vedge(vector, vector.tensor_scalar(out=dt1[:, 0:1024], in0=dzaux[:], scalar1=1.0,
                                                       scalar2=None, op0=ALU.add))
                    vedge(vector, vector.tensor_tensor(out=dzp[u2][:, 0:1024], in0=dt1[:, 0:1024],
                                                       in1=dsc[u2][:, 0:1024], op=ALU.mult))
                    vedge(vector, vector.tensor_scalar(out=dnib[:, 0:1024], in0=dqr[u2][:, 0:1024], scalar1=shifts[:],
                                                       scalar2=15, op0=ALU.logical_shift_right, op1=ALU.bitwise_and))
                    vedge(vector, vector.tensor_tensor(out=dt1[:, 0:1024], in0=dnib[:, 0:1024],
                                                       in1=dsc[u2][:, 0:1024], op=ALU.mult))
                    vector.tensor_tensor(out=dwb[u2][:, 0:1024], in0=dt1[:, 0:1024], in1=dzp[u2][:, 0:1024],
                                         op=ALU.subtract).then_inc(s_dqv.h, 1)
                s_dqv.v += 1

        def dq_stores(gpsimd, lo, hi):
            for u in range(lo, hi):
                kind = DQ_UNITS[u]
                ss_ = s_dqs[u % 2]
                gpsimd.wait_ge(s_dqv.h, u + 1)
                if kind[0] == "zprep":
                    dst = zpgd if kind[1] == 0 else zpud
                    gpsimd.dma_start(out=dst, in_=dt1[32:64, 0:NSL]).then_inc(ss_.h, 16)
                elif kind[0] == "gu":
                    m_, j_ = kind[1], kind[2]
                    wdst = gu_src[m_][3]
                    for nt in range(NNT):
                        nw = NT_SIZES[nt]
                        gpsimd.dma_start(out=wdst[128 * nt:128 * (nt + 1), j_ * nw:(j_ + 1) * nw],
                                         in_=dwb[u % 2][:, NT_OFF[nt]:NT_OFF[nt] + nw]).then_inc(ss_.h, 16)
                        ss_.v += 16
                    ss_.v -= 16
                else:
                    _, mt_, ec_ = kind
                    nw = NT_SIZES[mt_]
                    gpsimd.dma_start(out=wdbf[128 * mt_:128 * mt_ + nw, 1024 * ec_:1024 * (ec_ + 1)],
                                     in_=dwb[u % 2][0:nw, 0:1024]).then_inc(ss_.h, 16)
                ss_.v += 16

        dq_ptr = [0]

        def dq_chunk(gpsimd, lo, hi, flush=False):
            for u in range(lo, hi):
                while dq_ptr[0] <= u - 2:
                    dq_stores(gpsimd, dq_ptr[0], dq_ptr[0] + 1)
                    dq_ptr[0] += 1
                dq_loads(gpsimd, u, u + 1)
            if flush:
                while dq_ptr[0] < hi:
                    dq_stores(gpsimd, dq_ptr[0], dq_ptr[0] + 1)
                    dq_ptr[0] += 1

        sA = contextlib.ExitStack()
        eA = sA.enter_context
        x0t = [eA(nc.sbuf_tensor(f"p0_x{i}", [128, D], F32)) for i in range(2)]
        h0t = eA(nc.sbuf_tensor("p0_h0", [128, D], BF16))
        h0stg = eA(nc.sbuf_tensor("p0_stg", [128, D], BF16))

        NT0 = TPC // 128

        @block.sync
        def _(sync):
            for i in range(NT0):
                if i >= 2:
                    sync.wait_ge(s_p0v.h, 2 * (i - 1))
                sp = s_p0l[i % 2]
                sync.dma_start(out=x0t[i % 2][:], in_=xrows[128 * i:128 * (i + 1), :]).then_inc(sp.h, 16)
                sp.v += 16

        @block.vector
        def _(vector):
            vector.wait_ge(s_init.h, INIT_DONE)
            for i in range(NT0):
                vector.wait_ge(s_p0l[i % 2].h, 16 * (i // 2 + 1))
                if i >= 1:
                    vector.wait_ge(s_rst.h, i)
                xt = x0t[i % 2]
                for ch in range(8):
                    inst = vector.bn_stats(out=stat6[:, 6 * ch:6 * ch + 6], in_=xt[:, 512 * ch:512 * (ch + 1)])
                vedge(vector, inst)
                vector.bn_aggr(out=stat2[:], in_=stat6[:]).then_inc(s_p0v.h, 1)
                s_p0v.v += 1
                vector.wait_ge(s_rst.h, i + 1)
                if i >= 1:
                    vector.wait_ge(s_p0t.h, 4 * i)
                vector.tensor_scalar(out=h0t[:], in0=xt[:], scalar1=stat2[:, 0:1],
                                     scalar2=lnt2[:], op0=ALU.subtract, op1=ALU.mult).then_inc(s_p0v.h, 1)
                s_p0v.v += 1
                for g in range(4):
                    vector.wait_ge(s_p0t.h, 4 * i + g + 1)
                    if g == 0 and i >= 1:
                        vector.wait_ge(s_p0s.h, 16 * i)
                    vector.tensor_copy(out=h0stg[:, 1024 * g:1024 * (g + 1)],
                                       in_=pb[g % 2][:]).then_inc(s_p0e.h, 1)
                    s_p0e.v += 1

        @block.scalar
        def _(scalar):
            for i in range(NT0):
                scalar.wait_ge(s_p0v.h, 2 * i + 1)
                sedge(scalar, scalar.activation(out=lnt1[:], in_=stat2[:, 1:2], func=AF.Ln, bias=epsv[:]))
                scalar.activation(out=lnt2[:], in_=lnt1[:], func=AF.Exp, scale=-0.5).then_inc(s_rst.h, 1)
                s_rst.v += 1

        @block.tensor
        def _(tensor):
            tensor.wait_ge(s_init.h, INIT_DONE)
            for i in range(NT0):
                tensor.wait_ge(s_p0v.h, 2 * i + 2)
                for g in range(4):
                    gg = 4 * i + g
                    if gg >= 2:
                        tensor.wait_ge(s_p0e.h, gg - 1)
                    for k in range(8):
                        ch = 8 * g + k
                        inst = tensor.transpose(pb[g % 2][:, 128 * k:128 * (k + 1)],
                                                h0t[:, 128 * ch:128 * (ch + 1)], identb[:])
                    inst.then_inc(s_p0t.h, 1)
                    s_p0t.v += 1

        @block.gpsimd
        def _(gpsimd):
            for i in range(NT0):
                gpsimd.wait_ge(s_p0e.h, 4 * (i + 1))
                dstap = bass.AP(h0slab.tensor, 128 * i, [[TPC, 128], [128 * TPC, 32], [1, 128]])
                gpsimd.dma_start(out=dstap,
                                 in_=h0stg.ap().rearrange("p (j t) -> p j t", t=128)).then_inc(s_p0s.h, 16)
                s_p0s.v += 16
            gpsimd.wait_ge(s_p0s.h, s_p0s.v)
            gpsimd.collective_compute("AllGather", ALU.bypass, RG, [h0slab.opt()], [h0g.opt()]).then_inc(s_ag0.h, 1)
            s_ag0.v += 1

        @block.gpsimd
        def _(gpsimd):
            gpsimd.wait_ge(s_init.h, 80)
            dq_chunk(gpsimd, *DQ_W_A)

        @block.vector
        def _(vector):
            dq_vec(vector, *DQ_W_A)

        FAB = 3

        @block.vector
        def _(vector):
            vector.tensor_copy(out=fscr[:], in_=epsv[:]).then_inc(s_fab.h, 1)

        @block.scalar
        def _(scalar):
            scalar.activation(out=fsc2[:], in_=epsv[:], func=AF.Copy).then_inc(s_fab.h, 1)

        @block.gpsimd
        def _(gpsimd):
            gpsimd.wait_ge(s_p0s.h, s_p0s.v)
            gpsimd.wait_ge(s_dqs[0].h, s_dqs[0].v)
            gpsimd.wait_ge(s_dqs[1].h, s_dqs[1].v)
            gpsimd.sem_inc(s_fab.h, 1)

        sA.close()
        sB = contextlib.ExitStack()
        eB = sB.enter_context
        wqsb = eB(nc.sbuf_tensor("p1_w", [128, NCH * 768], BF16))
        wqld = [eB(nc.sbuf_tensor(f"p1_wl{i}", [128, 768], F32)) for i in range(2)]
        h0rhs = [eB(nc.sbuf_tensor(f"p1_h{i}", [128, 512], BF16)) for i in range(4)]
        qkev = [eB(nc.sbuf_tensor(f"p1_e{i}", [128, 6 * 512], BF16)) for i in range(2)]

        @block.sync
        def _(sync):
            sync.wait_ge(s_fab.h, FAB)
            for sw in range(2):
                for j in range(NCH):
                    cidx = sw * NCH + j
                    if cidx >= 2:
                        sync.wait_ge(s_p1wc.h, cidx - 1)
                    sp = s_p1w[cidx % 2]
                    sync.dma_start(out=wqld[cidx % 2][:],
                                   in_=wqkv[128 * j:128 * (j + 1), 768 * sw:768 * (sw + 1)]).then_inc(sp.h, 16)
                    sp.v += 16
                if sw == 0:
                    sync.wait_ge(s_ag0.h, 1)
                for m in range(NQB):
                    for j in range(NCH):
                        idx = (sw * NQB + m) * NCH + j
                        if idx >= 4:
                            sync.wait_ge(s_p1j.h, idx - 3)
                        rk = m // 2
                        tl = 512 * (m % 2)
                        src = h0g[D * rk + 128 * j:D * rk + 128 * (j + 1), tl:tl + 512]
                        sp = s_p1h[idx % 4]
                        sync.dma_start(out=h0rhs[idx % 4][:], in_=src).then_inc(sp.h, 16)
                        sp.v += 16

        @block.vector
        def _(vector):
            vector.wait_ge(s_fab.h, FAB)
            for sw in range(2):
                if sw == 1:
                    vector.wait_ge(s_p1j.h, NCH * NQB)
                for j in range(NCH):
                    cidx = sw * NCH + j
                    vector.wait_ge(s_p1w[cidx % 2].h, 16 * (cidx // 2 + 1))
                    vector.tensor_copy(out=wqsb[:, 768 * j:768 * (j + 1)],
                                       in_=wqld[cidx % 2][:]).then_inc(s_p1wc.h, 1)
                    s_p1wc.v += 1

        @block.tensor
        def _(tensor):
            tensor.wait_ge(s_fab.h, FAB)
            for sw in range(2):
                tensor.wait_ge(s_p1wc.h, NCH * (sw + 1))
                for m in range(NQB):
                    gidx = sw * NQB + m
                    if gidx >= 1:
                        tensor.wait_ge(s_p1e.h, 6 * gidx)
                    for j in range(NCH):
                        idx = gidx * NCH + j
                        tensor.wait_ge(s_p1h[idx % 4].h, 16 * (idx // 4 + 1))
                        for q6 in range(6):
                            inst = tensor.matmul(pf[q6][:],
                                                 wqsb[:, 768 * j + 128 * q6:768 * j + 128 * (q6 + 1)],
                                                 h0rhs[idx % 4][:], start=(j == 0), stop=(j == NCH - 1))
                        inst.then_inc(s_p1j.h, 1)
                        s_p1j.v += 1

        @block.scalar
        def _(scalar):
            for sw in range(2):
                for m in range(NQB):
                    gidx = sw * NQB + m
                    scalar.wait_ge(s_p1j.h, NCH * (gidx + 1))
                    if gidx >= 2:
                        scalar.wait_ge(s_p1s[gidx % 2].h, 96 * (gidx // 2))
                    for q6 in range(6):
                        sq = 3 * (2 * sw + q6 // 3) + (q6 % 3)
                        scalar.activation(out=qkev[gidx % 2][:, 512 * q6:512 * (q6 + 1)], in_=pf[q6][:],
                                          func=AF.Identity, bias=sbq[:, sq:sq + 1],
                                          scale=1.0).then_inc(s_p1e.h, 1)
                        s_p1e.v += 1

        @block.gpsimd
        def _(gpsimd):
            for sw in range(2):
                for m in range(NQB):
                    gidx = sw * NQB + m
                    gpsimd.wait_ge(s_p1e.h, 6 * (gidx + 1))
                    sp = s_p1s[gidx % 2]
                    for q6 in range(6):
                        sq = 3 * (2 * sw + q6 // 3) + (q6 % 3)
                        gpsimd.dma_start(out=qkvd[sq, :, 512 * m:512 * (m + 1)],
                                         in_=qkev[gidx % 2][:, 512 * q6:512 * (q6 + 1)]).then_inc(sp.h, 16)
                        sp.v += 16

        @block.gpsimd
        def _(gpsimd):
            dq_chunk(gpsimd, *DQ_W_B)

        @block.vector
        def _(vector):
            dq_vec(vector, *DQ_W_B)

        FBC = 3

        @block.vector
        def _(vector):
            vector.tensor_copy(out=fscr[:], in_=epsv[:]).then_inc(s_fbc.h, 1)

        @block.scalar
        def _(scalar):
            scalar.activation(out=fsc2[:], in_=epsv[:], func=AF.Copy).then_inc(s_fbc.h, 1)

        @block.gpsimd
        def _(gpsimd):
            gpsimd.wait_ge(s_p1s[0].h, s_p1s[0].v)
            gpsimd.wait_ge(s_p1s[1].h, s_p1s[1].v)
            gpsimd.wait_ge(s_dqs[0].h, s_dqs[0].v)
            gpsimd.wait_ge(s_dqs[1].h, s_dqs[1].v)
            gpsimd.sem_inc(s_fbc.h, 1)

        sB.close()
        sC = contextlib.ExitStack()
        eC = sC.enter_context
        spatt = eC(nc.sbuf_tensor("p2_patt", [128, 20 * 512], BF16))
        kf2 = [eC(nc.sbuf_tensor(f"p2_k{i}", [128, S], BF16)) for i in range(2)]
        vf2 = [eC(nc.sbuf_tensor(f"p2_v{i}", [128, S], BF16)) for i in range(2)]
        qf2 = [eC(nc.sbuf_tensor(f"p2_q{i}", [128, S], BF16)) for i in range(2)]
        vt = eC(nc.sbuf_tensor("p2_vt", [128, S], BF16))
        ptx = [eC(nc.sbuf_tensor(f"p2_px{i}", [128, 512], BF16)) for i in range(4)]
        ptm = [eC(nc.sbuf_tensor(f"p2_pm{i}", [128, 512], BF16)) for i in range(4)]
        osb = eC(nc.sbuf_tensor("p2_osb", [128, 512], BF16))
        rsb = eC(nc.sbuf_tensor("p2_rsb", [128, 512], F32))
        onb = eC(nc.sbuf_tensor("p2_onb", [128, 512], BF16))

        @block.sync
        def _(sync):
            sync.wait_ge(s_fbc.h, FBC)
            sync.dma_start(out=spatt[:],
                           in_=patt.rearrange("(v p) q -> p v q", p=128)).then_inc(s_patt.h, 16)
            s_patt.v += 16
            for s_ in range(NSLOT):
                if s_ >= 2:
                    gend = BLOCK_RANGE[(s_ - 2, NQB - 1)][1]
                    sync.wait_ge(s_sAV.h, gend)
                    sync.wait_ge(s_vt.h, 8 * (s_ - 1))
                sync.dma_start(out=kf2[s_ % 2][:], in_=qkvd[3 * s_ + 1, :, :]).then_inc(s_kf[s_ % 2].h, 16)
                sync.dma_start(out=vf2[s_ % 2][:], in_=qkvd[3 * s_ + 2, :, :]).then_inc(s_vf[s_ % 2].h, 16)
                sync.dma_start(out=qf2[s_ % 2][:], in_=qkvd[3 * s_ + 0, :, :]).then_inc(s_qf[s_ % 2].h, 16)
                s_kf[s_ % 2].v += 16; s_vf[s_ % 2].v += 16; s_qf[s_ % 2].v += 16

        for s_ in range(NSLOT):
            def make_tensor(s_=s_):
                def emit(tensor):
                    kf = kf2[s_ % 2]
                    vf = vf2[s_ % 2]
                    qf = qf2[s_ % 2]
                    if s_ == 0:
                        tensor.wait_ge(s_fbc.h, FBC)
                    tensor.wait_ge(s_vf[s_ % 2].h, 16 * (s_ // 2 + 1))
                    for gvt in range(8):
                        gg = 8 * s_ + gvt
                        if gg >= 2:
                            tensor.wait_ge(s_vte.h, gg - 1)
                        for k in range(8):
                            ch = 8 * gvt + k
                            inst = tensor.transpose(pb[gvt % 2][:, 128 * k:128 * (k + 1)],
                                                    vf[:, 128 * ch:128 * (ch + 1)], identb[:])
                        inst.then_inc(s_vt.h, 1)
                        s_vt.v += 1
                    tensor.wait_ge(s_kf[s_ % 2].h, 16 * (s_ // 2 + 1))
                    tensor.wait_ge(s_qf[s_ % 2].h, 16 * (s_ // 2 + 1))
                    tensor.wait_ge(s_patt.h, 16)
                    tensor.wait_ge(s_vte.h, 8 * (s_ + 1))
                    LA = 3
                    gs0 = BLOCK_RANGE[(s_, 0)][0]
                    gs1 = BLOCK_RANGE[(s_, NQB - 1)][1]

                    def emit_S(gi):
                        sx, bx, ux = STEP_U[gi]
                        k0x = QB * bx + 128 * ux
                        sbank = [pf[0], pf[1], pf[4], pf[5]][gi % 4]
                        if gi >= 4:
                            tensor.wait_ge(s_sX.h, gi - 3)
                        inst = tensor.matmul(sbank[:], kf[:, k0x:k0x + 128],
                                             qf[:, QB * bx:QB * (bx + 1)], start=True, stop=True)
                        inst.then_inc(s_sS.h, 1)
                        s_sS.v += 1

                    for gpre in range(gs0, min(gs0 + LA, gs1)):
                        emit_S(gpre)
                    for b_ in range(NQB):
                        g0, g1 = BLOCK_RANGE[(s_, b_)]
                        blkidx = s_ * NQB + b_
                        if blkidx >= 1:
                            tensor.wait_ge(s_sR.h, blkidx)
                            tensor.wait_ge(s_sE.h, blkidx)
                        for i, gi in enumerate(range(g0, g1)):
                            if gi + LA < gs1:
                                emit_S(gi + LA)
                            u_ = STEP_U[gi][2]
                            k0 = QB * b_ + 128 * u_
                            tensor.wait_ge(s_sM.h, gi + 1)
                            tensor.matmul(pf[2][:], onesb[:], ptm[gi % 4][:],
                                          start=(i == 0), stop=(i == g1 - g0 - 1))
                            ct = k0 // 128
                            inst = tensor.matmul(pf[3][:], vt[:, 128 * ct:128 * (ct + 1)],
                                                 ptm[gi % 4][:],
                                                 start=(i == 0), stop=(i == g1 - g0 - 1))
                            inst.then_inc(s_sAV.h, 1)
                            s_sAV.v += 1
                return emit
            block.tensor(make_tensor())

            def make_scalar(s_=s_):
                def emit(scalar):
                    for b_ in range(NQB):
                        g0, g1 = BLOCK_RANGE[(s_, b_)]
                        blkidx = s_ * NQB + b_
                        for i, gi in enumerate(range(g0, g1)):
                            u_ = STEP_U[gi][2]
                            scalar.wait_ge(s_sS.h, gi + 1)
                            if gi >= 4:
                                scalar.wait_ge(s_sM.h, gi - 3)
                            iu = 3 - u_
                            sbank = [pf[0], pf[1], pf[4], pf[5]][gi % 4]
                            scalar.activation(out=ptx[gi % 4][:], in_=sbank[:], func=AF.Exp,
                                              bias=sebias[:, 64 * s_ + iu:64 * s_ + iu + 1],
                                              scale=1.0 / 128).then_inc(s_sX.h, 1)
                            s_sX.v += 1
                        scalar.wait_ge(s_sAV.h, g1)
                        if blkidx >= 1:
                            scalar.wait_ge(s_sON.h, blkidx)
                        scalar.activation(out=osb[:], in_=pf[3][:], func=AF.Copy).then_inc(s_sE.h, 1)
                        s_sE.v += 1
                return emit
            block.scalar(make_scalar())

            def make_vector(s_=s_):
                def emit(vector):
                    if s_ == 0:
                        vector.wait_ge(s_patt.h, 16)
                    for gvt in range(8):
                        vector.wait_ge(s_vt.h, 8 * s_ + gvt + 1)
                        vector.tensor_copy(out=vt[:, 1024 * gvt:1024 * (gvt + 1)],
                                           in_=pb[gvt % 2][:]).then_inc(s_vte.h, 1)
                        s_vte.v += 1
                    for b_ in range(NQB):
                        g0, g1 = BLOCK_RANGE[(s_, b_)]
                        blkidx = s_ * NQB + b_
                        for i, gi in enumerate(range(g0, g1)):
                            u_ = STEP_U[gi][2]
                            vector.wait_ge(s_sX.h, gi + 1)
                            if gi >= 4:
                                vector.wait_ge(s_sAV.h, gi - 3)
                            v_ = u_ if 0 <= u_ else 4
                            pslice = spatt[:, 512 * (5 * s_ + v_):512 * (5 * s_ + v_ + 1)]
                            vector.tensor_tensor(out=ptm[gi % 4][:], in0=ptx[gi % 4][:],
                                                 in1=pslice, op=ALU.mult).then_inc(s_sM.h, 1)
                            s_sM.v += 1
                        vector.wait_ge(s_sAV.h, g1)
                        vector.reciprocal(out=rsb[:], in_=pf[2][:]).then_inc(s_sR.h, 1)
                        s_sR.v += 1
                        vector.wait_ge(s_sR.h, s_sR.v)
                        vector.wait_ge(s_sE.h, blkidx + 1)
                        if blkidx >= 1:
                            vector.wait_ge(s_oas.h, 16 * blkidx)
                        vector.tensor_tensor(out=onb[:], in0=osb[:], in1=rsb[:],
                                             op=ALU.mult).then_inc(s_sON.h, 1)
                        s_sON.v += 1
                return emit
            block.vector(make_vector())

            def make_gp(s_=s_):
                def emit(gpsimd):
                    for b_ in range(NQB):
                        blkidx = s_ * NQB + b_
                        gpsimd.wait_ge(s_sON.h, blkidx + 1)
                        gpsimd.dma_start(out=oaslab[128 * s_:128 * (s_ + 1), QB * b_:QB * (b_ + 1)],
                                         in_=onb[:]).then_inc(s_oas.h, 16)
                        s_oas.v += 16
                return emit
            block.gpsimd(make_gp())

            lo, hi = DQ_W_S[s_]

            def mk_dql(lo=lo, hi=hi, last=(s_ == NSLOT - 1)):
                def emit(gpsimd):
                    dq_chunk(gpsimd, lo, hi, flush=last)
                return emit
            block.gpsimd(mk_dql())

            def mk_dqv(lo=lo, hi=hi):
                def emit(vector):
                    dq_vec(vector, lo, hi)
                return emit
            block.vector(mk_dqv())



        FCD = 3

        @block.vector
        def _(vector):
            vector.tensor_copy(out=fscr[:], in_=epsv[:]).then_inc(s_fcd.h, 1)

        @block.scalar
        def _(scalar):
            scalar.activation(out=fsc2[:], in_=epsv[:], func=AF.Copy).then_inc(s_fcd.h, 1)

        @block.gpsimd
        def _(gpsimd):
            gpsimd.wait_ge(s_oas.h, s_oas.v)
            gpsimd.wait_ge(s_dqs[0].h, s_dqs[0].v)
            gpsimd.wait_ge(s_dqs[1].h, s_dqs[1].v)
            gpsimd.sem_inc(s_fcd.h, 1)

        sC.close()
        sDQ.close()
        sD = contextlib.ExitStack()
        eD = sD.enter_context
        wpld = [eD(nc.sbuf_tensor(f"p3_wl{i}", [128, D], F32)) for i in range(2)]
        wpcb = eD(nc.sbuf_tensor("p3_w", [128, NSLOT * D], BF16))
        oalhs = [eD(nc.sbuf_tensor(f"p3_o{i}", [128, NSLOT * 512], BF16)) for i in range(2)]
        yev = [eD(nc.sbuf_tensor(f"p3_y{i}", [128, D], BF16)) for i in range(2)]
        bprep = eD(nc.sbuf_tensor("p3_bp", [128, D], F32))
        ylt = eD(nc.sbuf_tensor("p3_yl", [128, D], BF16))
        ybt = eD(nc.sbuf_tensor("p3_yb", [128, D], F32))
        lnb = eD(nc.sbuf_tensor("p3_ln", [128, D], F32))
        xt3 = eD(nc.sbuf_tensor("p3_x", [128, D], F32))
        x1b = eD(nc.sbuf_tensor("p3_x1", [128, D], F32))
        h2t = eD(nc.sbuf_tensor("p3_h2", [128, D], BF16))
        h2stg = eD(nc.sbuf_tensor("p3_h2s", [128, D], BF16))

        P3_TILES = [(r, q, 0) for r in range(NC) for q in range(4)] + \
                   [(r, q, 1) for r in range(NC) for q in range(4)]

        @block.sync
        def _(sync):
            sync.wait_ge(s_fcd.h, FCD)
            brep = bass.AP(bproj.tensor, 0, [[0, 128], [1, D]])
            sync.dma_start(out=bprep[:], in_=brep).then_inc(s_bp.h, 16)
            s_bp.v += 16
            for sl in range(NSLOT):
                if sl >= 2:
                    sync.wait_ge(s_p3wc.h, sl - 1)
                sp = s_p3w[sl % 2]
                sync.dma_start(out=wpld[sl % 2][:],
                               in_=wprojc[128 * sl:128 * (sl + 1), :]).then_inc(sp.h, 16)
                sp.v += 16
            for lg in range(16):
                r, hf = lg % 8, lg // 8
                if lg >= 2:
                    sync.wait_ge(s_p3m.h, 8 * 4 * (lg - 1))
                src = bass.AP(oaslab.tensor, 1024 * r + 512 * hf,
                              [[S, 128], [128 * S, NSLOT], [1, 512]])
                sp = s_p3l[lg % 2]
                sync.dma_start(out=oalhs[lg % 2].ap().rearrange("p (s t) -> p s t", t=512),
                               in_=src).then_inc(sp.h, 16)
                sp.v += 16

        @block.vector
        def _(vector):
            vector.wait_ge(s_fcd.h, FCD)
            for sl in range(NSLOT):
                vector.wait_ge(s_p3w[sl % 2].h, 16 * (sl // 2 + 1))
                vector.tensor_copy(out=wpcb[:, D * sl:D * (sl + 1)],
                                   in_=wpld[sl % 2][:]).then_inc(s_p3wc.h, 1)
                s_p3wc.v += 1

        @block.tensor
        def _(tensor):
            tensor.wait_ge(s_fcd.h, FCD)
            tensor.wait_ge(s_p3wc.h, NSLOT)
            for ti, (r, q, hf) in enumerate(P3_TILES):
                lg = 8 * hf + r
                tensor.wait_ge(s_p3l[lg % 2].h, 16 * (lg // 2 + 1))
                for ec in range(8):
                    eidx = 8 * ti + ec
                    if eidx >= 4:
                        tensor.wait_ge(s_p3e.h, eidx - 3)
                    for sl in range(NSLOT):
                        inst = tensor.matmul(pf[ec % 4][:],
                                             oalhs[lg % 2][:, 512 * sl + 128 * q:512 * sl + 128 * (q + 1)],
                                             wpcb[:, D * sl + 512 * ec:D * sl + 512 * (ec + 1)],
                                             start=(sl == 0), stop=(sl == NSLOT - 1))
                    inst.then_inc(s_p3m.h, 1)
                    s_p3m.v += 1

        @block.scalar
        def _(scalar):
            for ti in range(len(P3_TILES)):
                if ti >= 2:
                    scalar.wait_ge(s_p3s[ti % 2].h, 16 * (ti // 2))
                for ec in range(8):
                    eidx = 8 * ti + ec
                    scalar.wait_ge(s_p3m.h, eidx + 1)
                    scalar.activation(out=yev[ti % 2][:, 512 * ec:512 * (ec + 1)], in_=pf[ec % 4][:],
                                      func=AF.Copy).then_inc(s_p3e.h, 1)
                    s_p3e.v += 1

        @block.gpsimd
        def _(gpsimd):
            for ti, (r, q, hf) in enumerate(P3_TILES):
                ydst = y1dA if hf == 0 else y1dB
                gpsimd.wait_ge(s_p3e.h, 8 * (ti + 1))
                sp = s_p3s[ti % 2]
                gpsimd.dma_start(out=ydst[QB * r + 128 * q:QB * r + 128 * (q + 1), :],
                                 in_=yev[ti % 2][:]).then_inc(sp.h, 16)
                sp.v += 16
                if ti == NC * 4 - 1:
                    gpsimd.wait_ge(s_p3s[0].h, 16 * NC * 2)
                    gpsimd.wait_ge(s_p3s[1].h, 16 * NC * 2)
                    gpsimd.collective_compute("ReduceScatter", ALU.add, RG,
                                              [y1dA.opt()], [y1rA.opt()]).then_inc(s_rsa1.h, 1)
                    s_rsa1.v += 1
            gpsimd.wait_ge(s_p3s[0].h, 16 * NC * 4)
            gpsimd.wait_ge(s_p3s[1].h, 16 * NC * 4)
            gpsimd.collective_compute("ReduceScatter", ALU.add, RG,
                                      [y1dB.opt()], [y1rB.opt()]).then_inc(s_rsb1.h, 1)
            s_rsb1.v += 1

        @block.sync
        def _(sync):
            for hf in range(2):
                sync.wait_ge((s_rsa1 if hf == 0 else s_rsb1).h, 1)
                ysrc = y1rA if hf == 0 else y1rB
                for q in range(4):
                    rowi = 4 * hf + q
                    if rowi >= 1:
                        sync.wait_ge(s_r5v.h, 5 * rowi)
                    sync.dma_start(out=ylt[:], in_=ysrc[128 * q:128 * (q + 1), :]).then_inc(s_r5l.h, 16)
                    sync.dma_start(out=xt3[:],
                                   in_=xrows[QB * hf + 128 * q:QB * hf + 128 * (q + 1), :]).then_inc(s_r5l.h, 16)
                    s_r5l.v += 32

        @block.vector
        def _(vector):
            for hf in range(2):
                for q in range(4):
                    rowi = 4 * hf + q
                    vector.wait_ge(s_r5l.h, 32 * (rowi + 1))
                    if rowi == 0:
                        vector.wait_ge(s_bp.h, 16)
                    if rowi >= 1:
                        vector.wait_ge(s_r5r.h, 2 * rowi)
                    vedge(vector, vector.tensor_tensor(out=ybt[:], in0=ylt[:], in1=bprep[:], op=ALU.add))
                    for ch in range(8):
                        inst = vector.bn_stats(out=stat6[:, 6 * ch:6 * ch + 6], in_=ybt[:, 512 * ch:512 * (ch + 1)])
                    vedge(vector, inst)
                    vector.bn_aggr(out=stat2[:], in_=stat6[:]).then_inc(s_r5v.h, 1)
                    vector.wait_ge(s_r5r.h, 2 * rowi + 1)
                    vector.tensor_scalar(out=lnb[:], in0=ybt[:], scalar1=stat2[:, 0:1],
                                         scalar2=lnt2[:], op0=ALU.subtract, op1=ALU.mult).then_inc(s_r5v.h, 1)
                    vector.wait_ge(s_r5v.h, 5 * rowi + 2)
                    if rowi >= 1:
                        vector.wait_ge(s_x1s.h, 16 * rowi)
                    vector.tensor_tensor(out=x1b[:], in0=xt3[:], in1=lnb[:], op=ALU.add).then_inc(s_r5v.h, 1)
                    vector.wait_ge(s_r5v.h, 5 * rowi + 3)
                    for ch in range(8):
                        inst = vector.bn_stats(out=stat6[:, 6 * ch:6 * ch + 6], in_=x1b[:, 512 * ch:512 * (ch + 1)])
                    vedge(vector, inst)
                    vector.bn_aggr(out=stat2[:], in_=stat6[:]).then_inc(s_r5v.h, 1)
                    vector.wait_ge(s_r5r.h, 2 * rowi + 2)
                    if rowi >= 1:
                        vector.wait_ge(s_h2t.h, 4 * rowi)
                    vector.tensor_scalar(out=h2t[:], in0=x1b[:], scalar1=stat2[:, 0:1],
                                         scalar2=lnt2[:], op0=ALU.subtract, op1=ALU.mult).then_inc(s_r5v.h, 1)
                    s_r5v.v += 5
                    for g in range(4):
                        gg = 4 * rowi + g
                        vector.wait_ge(s_h2t.h, gg + 1)
                        if g == 0 and rowi >= 1:
                            if rowi <= 4:
                                vector.wait_ge(s_h2sA.h, 16 * min(rowi, 4))
                            if rowi > 4:
                                vector.wait_ge(s_h2sB.h, 16 * (rowi - 4))
                        vector.tensor_copy(out=h2stg[:, 1024 * g:1024 * (g + 1)],
                                           in_=pb[g % 2][:]).then_inc(s_h2e.h, 1)
                        s_h2e.v += 1

        @block.scalar
        def _(scalar):
            for rowi in range(8):
                for k2 in range(2):
                    scalar.wait_ge(s_r5v.h, 5 * rowi + (1 if k2 == 0 else 4))
                    sedge(scalar, scalar.activation(out=lnt1[:], in_=stat2[:, 1:2], func=AF.Ln, bias=epsv[:]))
                    scalar.activation(out=lnt2[:], in_=lnt1[:], func=AF.Exp, scale=-0.5).then_inc(s_r5r.h, 1)
                    s_r5r.v += 1

        @block.tensor
        def _(tensor):
            for rowi in range(8):
                tensor.wait_ge(s_r5v.h, 5 * rowi + 5)
                for g in range(4):
                    gg = 4 * rowi + g
                    if gg >= 2:
                        tensor.wait_ge(s_h2e.h, gg - 1)
                    for k in range(8):
                        ch = 8 * g + k
                        inst = tensor.transpose(pb[g % 2][:, 128 * k:128 * (k + 1)],
                                                h2t[:, 128 * ch:128 * (ch + 1)], identb[:])
                    inst.then_inc(s_h2t.h, 1)
                    s_h2t.v += 1

        @block.gpsimd
        def _(gpsimd):
            for hf in range(2):
                h2dst = h2slabA if hf == 0 else h2slabB
                sh2 = s_h2sA if hf == 0 else s_h2sB
                for q in range(4):
                    rowi = 4 * hf + q
                    gpsimd.wait_ge(s_r5v.h, 5 * rowi + 3)
                    gpsimd.dma_start(out=x1d[QB * hf + 128 * q:QB * hf + 128 * (q + 1), :],
                                     in_=x1b[:]).then_inc(s_x1s.h, 16)
                    s_x1s.v += 16
                    gpsimd.wait_ge(s_h2e.h, 4 * (rowi + 1))
                    dstap = bass.AP(h2dst.tensor, 128 * q, [[QB, 128], [128 * QB, 32], [1, 128]])
                    gpsimd.dma_start(out=dstap,
                                     in_=h2stg.ap().rearrange("p (j t) -> p j t", t=128)).then_inc(sh2.h, 16)
                    sh2.v += 16

        @block.gpsimd
        def _(gpsimd):
            gpsimd.wait_ge(s_h2sA.h, 16 * 4)
            gpsimd.collective_compute("AllGather", ALU.bypass, RG,
                                      [h2slabA.opt()], [h2gA.opt()]).then_inc(s_agh2a.h, 1)
            s_agh2a.v += 1
            gpsimd.wait_ge(s_h2sB.h, 16 * 4)
            gpsimd.collective_compute("AllGather", ALU.bypass, RG,
                                      [h2slabB.opt()], [h2gB.opt()]).then_inc(s_agh2b.h, 1)
            s_agh2b.v += 1

        FDE = 3

        @block.vector
        def _(vector):
            vector.tensor_copy(out=fscr[:], in_=epsv[:]).then_inc(s_fde.h, 1)

        @block.scalar
        def _(scalar):
            scalar.activation(out=fsc2[:], in_=epsv[:], func=AF.Copy).then_inc(s_fde.h, 1)

        @block.gpsimd
        def _(gpsimd):
            gpsimd.wait_ge(s_x1s.h, s_x1s.v)
            gpsimd.wait_ge(s_h2sA.h, s_h2sA.v)
            gpsimd.wait_ge(s_h2sB.h, s_h2sB.v)
            gpsimd.sem_inc(s_fde.h, 1)

        sD.close()
        sM.close()
        sE = contextlib.ExitStack()
        eE = sE.enter_context
        h2sb = eE(nc.sbuf_tensor("f_h2", [128, NCH * 1024], BF16))
        wgsb0 = eE(nc.sbuf_tensor("f_wg", [128, NCH * 128], BF16))
        wusb0 = eE(nc.sbuf_tensor("f_wu", [128, NCH * 128], BF16))
        wgsb = [wgsb0, wgsb0]
        wusb = [wusb0, wusb0]
        wdsb = eE(nc.sbuf_tensor("f_wd", [128, NNT * D], BF16))
        intr = eE(nc.sbuf_tensor("f_int", [128, NNT * 1024], BF16))
        silb = [eE(nc.sbuf_tensor(f"f_sil{i}", [128, 512], BF16)) for i in range(2)]
        y2ev = silb

        NGG = NNT * 4
        NDG = 64

        @block.sync
        def _(sync):
            sync.wait_ge(s_fde.h, FDE)
            for i in range(8):
                hf = i // 4
                rp = i % 4
                h2src = h2gA if hf == 0 else h2gB
                sync.wait_ge((s_agh2a if hf == 0 else s_agh2b).h, 1)
                if i >= 1:
                    sync.wait_ge(s_fg.h, NGG * i)
                for rr in range(2):
                    srcap = bass.AP(h2src.tensor, D * QB * (2 * rp + rr),
                                    [[QB, 128], [128 * QB, NCH], [1, QB]])
                    dstap = h2sb.ap().rearrange("p (j rt) -> p j rt", rt=1024)[:, :, 512 * rr:512 * (rr + 1)]
                    sync.dma_start(out=dstap, in_=srcap).then_inc(s_fh.h, 16)
                s_fh.v += 32
                for nt in range(NNT):
                    nw = NT_SIZES[nt]
                    widx = i * NNT + nt
                    if widx >= 1:
                        sync.wait_ge(s_fg.h, 4 * widx - 2)
                    sync.dma_start(out=wgsb[widx % 2][:, 0:NCH * nw],
                                   in_=wgbf[128 * nt:128 * (nt + 1), 0:NCH * nw]).then_inc(s_fwg.h, 16)
                    s_fwg.v += 16
                    if widx >= 1:
                        sync.wait_ge(s_fg.h, 4 * widx)
                    sync.dma_start(out=wusb[widx % 2][:, 0:NCH * nw],
                                   in_=wubf[128 * nt:128 * (nt + 1), 0:NCH * nw]).then_inc(s_fwu.h, 16)
                    s_fwu.v += 16
                    if i >= 1 and nt == 0:
                        sync.wait_ge(s_fy.h, NDG * i)
                    sync.dma_start(out=wdsb[0:nw, D * nt:D * (nt + 1)],
                                   in_=wdbf[128 * nt:128 * nt + nw, :]).then_inc(s_fwd.h, 16)
                    s_fwd.v += 16

        @block.tensor
        def _(tensor):
            tensor.wait_ge(s_fde.h, FDE)
            for i in range(8):
                tensor.wait_ge(s_fh.h, 32 * (i + 1))
                for nt in range(NNT):
                    nw = NT_SIZES[nt]
                    widx = i * NNT + nt
                    tensor.wait_ge(s_fwg.h, 16 * (widx + 1))
                    tensor.wait_ge(s_fwu.h, 16 * (widx + 1))
                    for mat in range(2):
                        wsb = (wgsb if mat == 0 else wusb)[widx % 2]
                        for rr in range(2):
                            bank = pf[2 * mat + rr]
                            if widx >= 1:
                                if mat == 0:
                                    tensor.wait_ge(s_fsil.h, 2 * widx - 2 + rr + 1)
                                else:
                                    tensor.wait_ge(s_fint.h, 2 * widx - 2 + rr + 1)
                            for j in range(NCH):
                                inst = tensor.matmul(bank[0:nw, 0:512],
                                                     wsb[:, nw * j:nw * j + nw],
                                                     h2sb[:, 1024 * j + 512 * rr:1024 * j + 512 * (rr + 1)],
                                                     start=(j == 0), stop=(j == NCH - 1))
                            inst.then_inc(s_fg.h, 1)
                            s_fg.v += 1
                tensor.wait_ge(s_fint.h, 2 * NNT * (i + 1))
                tensor.wait_ge(s_fwd.h, 16 * NNT * (i + 1))
                for ts in range(8):
                    for ec in range(8):
                        didx = NDG * i + 8 * ts + ec
                        if didx >= 2:
                            tensor.wait_ge(s_fye.h, didx - 1)
                        for nt in range(NNT):
                            nw = NT_SIZES[nt]
                            inst = tensor.matmul(pf[4 + didx % 2][:],
                                                 intr[0:nw, 1024 * nt + 128 * ts:1024 * nt + 128 * (ts + 1)],
                                                 wdsb[0:nw, D * nt + 512 * ec:D * nt + 512 * (ec + 1)],
                                                 start=(nt == 0), stop=(nt == NNT - 1))
                        inst.then_inc(s_fy.h, 1)
                        s_fy.v += 1

        @block.scalar
        def _(scalar):
            for i in range(8):
                if i >= 1:
                    scalar.wait_ge(s_fys[0].h, 16 * NDG * i // 2)
                    scalar.wait_ge(s_fys[1].h, 16 * NDG * i // 2)
                for nt in range(NNT):
                    widx = i * NNT + nt
                    nw = NT_SIZES[nt]
                    for rr in range(2):
                        sidx = 2 * widx + rr
                        scalar.wait_ge(s_fg.h, 4 * widx + rr + 1)
                        if sidx >= 2:
                            scalar.wait_ge(s_fint.h, sidx - 1)
                        scalar.activation(out=silb[sidx % 2][0:nw, :], in_=pf[rr][0:nw, 0:512],
                                          func=AF.Silu).then_inc(s_fsil.h, 1)
                        s_fsil.v += 1
                for ts in range(8):
                    for ec in range(8):
                        didx = NDG * i + 8 * ts + ec
                        scalar.wait_ge(s_fy.h, didx + 1)
                        if didx >= 2:
                            scalar.wait_ge(s_fys[didx % 2].h, 16 * (didx // 2))
                        scalar.activation(out=y2ev[didx % 2][:], in_=pf[4 + didx % 2][:],
                                          func=AF.Copy).then_inc(s_fye.h, 1)
                        s_fye.v += 1

        @block.vector
        def _(vector):
            vector.wait_ge(s_fde.h, FDE)
            for i in range(8):
                if i >= 1:
                    vector.wait_ge(s_fy.h, NDG * i)
                for nt in range(NNT):
                    widx = i * NNT + nt
                    nw = NT_SIZES[nt]
                    for rr in range(2):
                        sidx = 2 * widx + rr
                        vector.wait_ge(s_fsil.h, sidx + 1)
                        vector.wait_ge(s_fg.h, 4 * widx + 2 + rr + 1)
                        vector.tensor_tensor(out=intr[0:nw, 1024 * nt + 512 * rr:1024 * nt + 512 * (rr + 1)],
                                             in0=silb[sidx % 2][0:nw, :],
                                             in1=pf[2 + rr][0:nw, 0:512],
                                             op=ALU.mult).then_inc(s_fint.h, 1)
                        s_fint.v += 1

        @block.gpsimd
        def _(gpsimd):
            for i in range(8):
                hf = i // 4
                rp = i % 4
                ydst = y2dA if hf == 0 else y2dB
                for ts in range(8):
                    for ec in range(8):
                        didx = NDG * i + 8 * ts + ec
                        gpsimd.wait_ge(s_fye.h, didx + 1)
                        r_ = 2 * rp + ts // 4
                        tq = ts % 4
                        sp = s_fys[didx % 2]
                        gpsimd.dma_start(out=ydst[QB * r_ + 128 * tq:QB * r_ + 128 * (tq + 1),
                                                  512 * ec:512 * (ec + 1)],
                                         in_=y2ev[didx % 2][:]).then_inc(sp.h, 16)
                        sp.v += 16
                if i == 3:
                    gpsimd.wait_ge(s_fys[0].h, 16 * NDG * 2)
                    gpsimd.wait_ge(s_fys[1].h, 16 * NDG * 2)
                    gpsimd.collective_compute("ReduceScatter", ALU.add, RG,
                                              [y2dA.opt()], [y2rA.opt()]).then_inc(s_rsa2.h, 1)
                    s_rsa2.v += 1
            gpsimd.wait_ge(s_fys[0].h, 16 * NDG * 4)
            gpsimd.wait_ge(s_fys[1].h, 16 * NDG * 4)
            gpsimd.collective_compute("ReduceScatter", ALU.add, RG,
                                      [y2dB.opt()], [y2rB.opt()]).then_inc(s_rsb2.h, 1)
            s_rsb2.v += 1

        FEF = 3

        @block.vector
        def _(vector):
            vector.tensor_copy(out=fscr[:], in_=epsv[:]).then_inc(s_fef.h, 1)

        @block.scalar
        def _(scalar):
            scalar.activation(out=fsc2[:], in_=epsv[:], func=AF.Copy).then_inc(s_fef.h, 1)

        @block.gpsimd
        def _(gpsimd):
            gpsimd.wait_ge(s_fys[0].h, s_fys[0].v)
            gpsimd.wait_ge(s_fys[1].h, s_fys[1].v)
            gpsimd.sem_inc(s_fef.h, 1)

        sE.close()
        sF = contextlib.ExitStack()
        eF = sF.enter_context
        y5l2 = [eF(nc.sbuf_tensor(f"p5_y{i}", [128, D], BF16)) for i in range(2)]
        x5l2 = [eF(nc.sbuf_tensor(f"p5_x{i}", [128, D], F32)) for i in range(2)]
        ln5 = eF(nc.sbuf_tensor("p5_ln", [128, D], F32))
        dln = eF(nc.sbuf_tensor("p5_d", [128, D], F32))
        o5 = eF(nc.sbuf_tensor("p5_o", [128, D], F32))
        amx = eF(nc.sbuf_tensor("p5_amx", [128, 1], F32))
        msk = eF(nc.sbuf_tensor("p5_msk", [128, 1], F32))
        rst2 = eF(nc.sbuf_tensor("p5_rst2", [128, 1], F32))

        @block.sync
        def _(sync):
            sync.wait_ge(s_fef.h, FEF)
            for hf in range(2):
                sync.wait_ge((s_rsa2 if hf == 0 else s_rsb2).h, 1)
                ysrc = y2rA if hf == 0 else y2rB
                for q in range(4):
                    rowi = 4 * hf + q
                    if rowi >= 2:
                        sync.wait_ge(s_p5v.h, 3 * (rowi - 1))
                    sp = s_p5l[rowi % 2]
                    sync.dma_start(out=y5l2[rowi % 2][:], in_=ysrc[128 * q:128 * (q + 1), :]).then_inc(sp.h, 16)
                    sync.dma_start(out=x5l2[rowi % 2][:],
                                   in_=x1d[QB * hf + 128 * q:QB * hf + 128 * (q + 1), :]).then_inc(sp.h, 16)
                    sp.v += 32

        @block.vector
        def _(vector):
            vector.wait_ge(s_fef.h, FEF)
            for rowi in range(8):
                y5l = y5l2[rowi % 2]
                x5l = x5l2[rowi % 2]
                vector.wait_ge(s_p5l[rowi % 2].h, 32 * (rowi // 2 + 1))
                if rowi >= 1:
                    vector.wait_ge(s_p5r.h, rowi)
                for ch in range(8):
                    inst = vector.bn_stats(out=stat6[:, 6 * ch:6 * ch + 6], in_=y5l[:, 512 * ch:512 * (ch + 1)])
                vedge(vector, inst)
                vector.bn_aggr(out=stat2[:], in_=stat6[:]).then_inc(s_p5v.h, 1)
                vector.wait_ge(s_p5v.h, 3 * rowi + 1)
                vedge(vector, vector.tensor_scalar(out=msk[:], in0=stat2[:, 1:2], scalar1=65520.0,
                                                   scalar2=None, op0=ALU.is_lt))
                vector.wait_ge(s_p5r.h, rowi + 1)
                vedge(vector, vector.tensor_tensor(out=rst2[:], in0=msk[:], in1=lnt2[:], op=ALU.mult))
                vector.tensor_scalar(out=ln5[:], in0=y5l[:], scalar1=stat2[:, 0:1],
                                     scalar2=rst2[:], op0=ALU.subtract, op1=ALU.mult).then_inc(s_p5v.h, 1)
                vector.wait_ge(s_p5v.h, 3 * rowi + 2)
                if rowi >= 1:
                    vector.wait_ge(s_p5s.h, 16 * rowi)
                vector.tensor_tensor(out=o5[:], in0=x5l[:], in1=ln5[:], op=ALU.add).then_inc(s_p5v.h, 1)
                s_p5v.v += 3

        @block.scalar
        def _(scalar):
            for rowi in range(8):
                scalar.wait_ge(s_p5v.h, 3 * rowi + 1)
                sedge(scalar, scalar.activation(out=lnt1[:], in_=stat2[:, 1:2], func=AF.Ln, bias=epsv[:]))
                scalar.activation(out=lnt2[:], in_=lnt1[:], func=AF.Exp, scale=-0.5).then_inc(s_p5r.h, 1)
                s_p5r.v += 1

        @block.gpsimd
        def _(gpsimd):
            for hf in range(2):
                for q in range(4):
                    rowi = 4 * hf + q
                    gpsimd.wait_ge(s_p5v.h, 3 * rowi + 3)
                    gpsimd.dma_start(out=out[QB * hf + 128 * q:QB * hf + 128 * (q + 1), :],
                                     in_=o5[:]).then_inc(s_p5s.h, 16)
                    s_p5s.v += 16
            if dbg:
                srcs = {"h0": h0g, "qkv": qkvd.rearrange("a p t -> (a p) t"), "oa": oaslab,
                        "y1a": y1rA, "x1": x1d, "h2a": h2gA, "h2b": h2gB,
                        "y2a": y2rA, "y2b": y2rB}
                for name, ap_ in dbg.items():
                    gpsimd.dma_start(out=ap_, in_=srcs[name]).then_inc(s_p5s.h, 16)
                    s_p5s.v += 16
            gpsimd.wait_ge(s_p5s.h, s_p5s.v)

        sF.close()

    return nc



def stage_inputs(inputs):
    x = np.asarray(inputs["L0_attn_hidden_states"]).reshape(S, D)
    w_attn = np.asarray(inputs["L0_attn_w_attn"])
    b_attn = np.asarray(inputs["L0_attn_b_attn"])
    w_proj = np.asarray(inputs["L0_attn_w_proj"])
    b_proj = np.asarray(inputs["L0_attn_b_proj"])
    gqw = np.asarray(inputs["L0_ffn_gate_qweight"])
    gsc = np.asarray(inputs["L0_ffn_gate_scales"])
    gzr = np.asarray(inputs["L0_ffn_gate_zeros"])
    uqw = np.asarray(inputs["L0_ffn_up_qweight"])
    usc = np.asarray(inputs["L0_ffn_up_scales"])
    uzr = np.asarray(inputs["L0_ffn_up_zeros"])
    dqw = np.asarray(inputs["L0_ffn_down_qweight"])
    dsc_ = np.asarray(inputs["L0_ffn_down_scales"])
    dzr = np.asarray(inputs["L0_ffn_down_zeros"])

    bf = ml_dtypes.bfloat16
    p = np.arange(128)
    shifts = (4 * (p % 8)).astype(np.int32).reshape(128, 1)
    identf = np.eye(128, dtype=np.float32)
    onesb = np.ones((128, 128), dtype=bf)
    g_of_m = None

    in_maps = []
    for c in range(NC):
        d = {}
        d["xrows"] = np.ascontiguousarray(x[TPC * c:TPC * (c + 1), :])
        cols = []
        bcols = []
        for s in range(NSLOT):
            h = 8 * s + c
            for g0, base in ((0, 0), (1, D), (2, 2 * D)):
                cols.append(w_attn[:, base + HD * h:base + HD * (h + 1)])
                bcols.append(b_attn[base + HD * h:base + HD * (h + 1)])
        d["wqkv"] = np.ascontiguousarray(np.concatenate(cols, axis=1))
        d["bqkvT"] = np.ascontiguousarray(np.stack(bcols, axis=1))
        d["wprojc"] = np.ascontiguousarray(
            np.concatenate([w_proj[HD * (8 * s + c):HD * (8 * s + c + 1), :] for s in range(NSLOT)], axis=0))
        d["bproj"] = np.ascontiguousarray(b_proj.reshape(1, D))

        pt = np.zeros((NSLOT * 5 * 128, QB), dtype=bf)
        eb = np.zeros((128, NSLOT * 64), dtype=np.float32)
        jj = np.arange(QB)
        for s in range(NSLOT):
            sig = float(SLOPES[8 * s + c])
            for v in range(4):
                arg = sig * (128.0 * v + p[:, None] - jj[None, :])
                m = (jj[None, :] >= 128 * v + p[:, None])
                with np.errstate(over="ignore", under="ignore"):
                    tile = np.where(m, np.exp(np.minimum(arg, 0.0)), 0.0)
                pt[(5 * s + v) * 128:(5 * s + v + 1) * 128, :] = tile.astype(bf)
            arg = sig * (p[:, None] - jj[None, :] - 128.0)
            with np.errstate(under="ignore"):
                pt[(5 * s + 4) * 128:(5 * s + 5) * 128, :] = np.exp(arg).astype(bf)
            for iu in range(64):
                u = 3 - iu
                if u <= -1:
                    eb[:, 64 * s + iu] = np.float32(max(128.0 * sig * (u + 1), -1e30))
        d["patt"] = pt
        d["ebias"] = eb
        d["shifts"] = shifts
        d["identf"] = identf
        d["onesb"] = onesb

        nsl0 = NSL * c
        d["gq"] = np.ascontiguousarray(gqw[:, nsl0:nsl0 + NSL])
        d["gs"] = np.ascontiguousarray(gsc[:, nsl0:nsl0 + NSL])
        d["gz"] = np.ascontiguousarray(gzr[:, nsl0 // 8:(nsl0 + NSL) // 8])
        d["uq"] = np.ascontiguousarray(uqw[:, nsl0:nsl0 + NSL])
        d["us"] = np.ascontiguousarray(usc[:, nsl0:nsl0 + NSL])
        d["uz"] = np.ascontiguousarray(uzr[:, nsl0 // 8:(nsl0 + NSL) // 8])
        d["dq"] = np.ascontiguousarray(dqw[nsl0 // 8:(nsl0 + NSL) // 8, :])
        if g_of_m is None:
            pass
        gm = (nsl0 + np.arange(NSL)) // 128
        d["dsx"] = np.ascontiguousarray(dsc_[gm, :])
        d["dzx"] = np.ascontiguousarray(dzr[gm, :])
        in_maps.append(d)
    return in_maps


def kernel(**inputs):
    in_maps = stage_inputs(inputs)
    nc = build_nc(debug=DEBUG_DUMPS)
    res = run_bass_kernel_spmd(nc, in_maps, core_ids=list(range(NC)))
    outs = [np.asarray(res.results[c]["out"]) for c in range(NC)]
    full = np.concatenate(outs, axis=0).reshape(1, S, D).astype(np.float32)
    if DEBUG_DUMPS:
        kernel._dbg = res.results
    return full


# version 09
# speedup vs baseline: 1.0745x; 1.0121x over previous
import contextlib
import math

import numpy as np
import ml_dtypes

import concourse.bass as bass
import concourse.mybir as mybir
from concourse.bass_utils import run_bass_kernel_spmd

F32 = mybir.dt.float32
F16 = mybir.dt.float16
BF16 = mybir.dt.bfloat16
I32 = mybir.dt.int32
AF = mybir.ActivationFunctionType
ALU = mybir.AluOpType

S = 8192
D = 4096
NH = 32
HD = 128
INTER = 11008
NC = 8
TPC = S // NC
NSLOT = 4
NSL = INTER // NC
NQB = 16
QB = 512
NCH = D // 128
WIN_THRESH = 100.0

DEBUG_DUMPS = ()


def _alibi_slopes(n_heads):
    cp2 = 2 ** math.floor(math.log2(n_heads))
    base = 2 ** (-(2 ** (-(math.log2(cp2) - 3))))
    slopes = base ** np.arange(1, cp2 + 1, dtype=np.float64)
    if cp2 != n_heads:
        extra_base = 2 ** (-(2 ** (-(math.log2(2 * cp2) - 3))))
        n_rem = min(cp2, n_heads - cp2)
        extra = extra_base ** np.arange(1, 1 + 2 * n_rem, 2, dtype=np.float64)
        slopes = np.concatenate([slopes, extra])
    return slopes


SLOPES = _alibi_slopes(NH)

B_SLOT = []
for _s in range(NSLOT):
    sig_min = SLOPES[8 * _s + 7]
    B_SLOT.append(min(60, int(math.floor((127 + WIN_THRESH / sig_min) / 128.0))))

N_STEPS = [[4 + min(4 * b, B_SLOT[s]) for b in range(NQB)] for s in range(NSLOT)]
TOTAL_STEPS = sum(sum(r) for r in N_STEPS)

NT_SIZES = [128] * 10 + [96]
NNT = len(NT_SIZES)
NT_OFF = [sum(NT_SIZES[:i]) for i in range(NNT)]

STEP_U = []
BLOCK_RANGE = {}
_g = 0
for _s in range(NSLOT):
    for _b in range(NQB):
        n = N_STEPS[_s][_b]
        BLOCK_RANGE[(_s, _b)] = (_g, _g + n)
        for _u in range(-(n - 4), 4):
            STEP_U.append((_s, _b, _u))
        _g += n
assert _g == TOTAL_STEPS

DQ_UNITS = [("zprep", 0), ("zprep", 1)]
for _m in range(2):
    for _j in range(NCH):
        DQ_UNITS.append(("gu", _m, _j))
for _mt in range(NNT):
    for _ec in range(4):
        DQ_UNITS.append(("down", _mt, _ec))
N_DQ = len(DQ_UNITS)
DQL2_AT = []
_v2 = [0, 0]
for _u in range(N_DQ):
    _v2[_u % 2] += 32 if DQ_UNITS[_u][0] == "zprep" else 48
    DQL2_AT.append(_v2[_u % 2])
DQS2_AT = []
_v3 = [0, 0]
for _u in range(N_DQ):
    _v3[_u % 2] += 16 * (NNT if DQ_UNITS[_u][0] == "gu" else 1)
    DQS2_AT.append(_v3[_u % 2])
DQ_W_A = (0, 15)
DQ_W_B = (15, 15)
DQ_W_S = [(15, 33), (33, 55), (55, 81), (81, N_DQ)]


def _split_range(lo, hi, n):
    total = hi - lo
    out = []
    a = lo
    for k in range(n):
        b = lo + (total * (k + 1)) // n
        out.append((a, b))
        a = b
    return out


class Sem:
    __slots__ = ("h", "v")

    def __init__(self, h):
        self.h = h
        self.v = 0

    def inc(self, inst, n=1):
        inst.then_inc(self.h, n)
        self.v += n
        return self.v


def build_nc(debug=(), detect_races=True):
    nc = bass.Bass("TRN2", target_bir_lowering=False, detect_race_conditions=detect_races)

    xrows = nc.dram_tensor("xrows", [TPC, D], F32, kind="ExternalInput").ap()
    wqkv = nc.dram_tensor("wqkv", [D, NSLOT * 3 * HD], F32, kind="ExternalInput").ap()
    bqkvT = nc.dram_tensor("bqkvT", [HD, 12], F32, kind="ExternalInput").ap()
    wprojc = nc.dram_tensor("wprojc", [NSLOT * HD, D], F32, kind="ExternalInput").ap()
    bproj = nc.dram_tensor("bproj", [1, D], F32, kind="ExternalInput").ap()
    patt = nc.dram_tensor("patt", [NSLOT * 5 * 128, QB], BF16, kind="ExternalInput").ap()
    ebias = nc.dram_tensor("ebias", [128, NSLOT * 64], F32, kind="ExternalInput").ap()
    shifts_in = nc.dram_tensor("shifts", [128, 1], I32, kind="ExternalInput").ap()
    identf_in = nc.dram_tensor("identf", [128, 128], F32, kind="ExternalInput").ap()
    onesb_in = nc.dram_tensor("onesb", [128, 128], BF16, kind="ExternalInput").ap()
    gq = nc.dram_tensor("gq", [512, NSL], I32, kind="ExternalInput").ap()
    gs = nc.dram_tensor("gs", [32, NSL], F16, kind="ExternalInput").ap()
    gz = nc.dram_tensor("gz", [32, NSL // 8], I32, kind="ExternalInput").ap()
    uq = nc.dram_tensor("uq", [512, NSL], I32, kind="ExternalInput").ap()
    us = nc.dram_tensor("us", [32, NSL], F16, kind="ExternalInput").ap()
    uz = nc.dram_tensor("uz", [32, NSL // 8], I32, kind="ExternalInput").ap()
    dq = nc.dram_tensor("dq", [NSL // 8, D], I32, kind="ExternalInput").ap()
    dsx = nc.dram_tensor("dsx", [NSL, D], F16, kind="ExternalInput").ap()
    dzx = nc.dram_tensor("dzx", [NSL, D // 8], I32, kind="ExternalInput").ap()
    out = nc.dram_tensor("out", [TPC, D], F32, kind="ExternalOutput").ap()

    h0slab = nc.dram_tensor("h0slab", [D, TPC], BF16).ap()
    h0g = nc.dram_tensor("h0g", [NC * D, TPC], BF16).ap()
    qkvd = nc.dram_tensor("qkvd", [12, HD, S], BF16).ap()
    oaslab = nc.dram_tensor("oaslab", [NSLOT * HD, S], BF16).ap()
    y1dA = nc.dram_tensor("y1dA", [NC * QB, D], BF16).ap()
    y1dB = nc.dram_tensor("y1dB", [NC * QB, D], BF16).ap()
    y1rA = nc.dram_tensor("y1rA", [QB, D], BF16).ap()
    y1rB = nc.dram_tensor("y1rB", [QB, D], BF16).ap()
    x1d = nc.dram_tensor("x1d", [TPC, D], F32).ap()
    h2slabA = nc.dram_tensor("h2slabA", [D, QB], BF16).ap()
    h2slabB = nc.dram_tensor("h2slabB", [D, QB], BF16).ap()
    h2gA = nc.dram_tensor("h2gA", [NC * D, QB], BF16).ap()
    h2gB = nc.dram_tensor("h2gB", [NC * D, QB], BF16).ap()
    wgbf = nc.dram_tensor("wgbf", [NNT * 128, NCH * 128], BF16).ap()
    wubf = nc.dram_tensor("wubf", [NNT * 128, NCH * 128], BF16).ap()
    wdbf = nc.dram_tensor("wdbf", [NSL, D], BF16).ap()
    zpgd = nc.dram_tensor("zpgd", [32, NSL], F32).ap()
    zpud = nc.dram_tensor("zpud", [32, NSL], F32).ap()
    y2dA = nc.dram_tensor("y2dA", [NC * QB, D], BF16).ap()
    y2dB = nc.dram_tensor("y2dB", [NC * QB, D], BF16).ap()
    y2rA = nc.dram_tensor("y2rA", [QB, D], BF16).ap()
    y2rB = nc.dram_tensor("y2rB", [QB, D], BF16).ap()

    dbg = {}
    for name, src_shape, dt_ in (
        ("h0", [NC * D, TPC], BF16), ("qkv", [12 * HD, S], BF16),
        ("oa", [NSLOT * HD, S], BF16), ("y1a", [QB, D], BF16),
        ("x1", [TPC, D], F32), ("h2a", [NC * D, QB], BF16),
        ("h2b", [NC * D, QB], BF16), ("y2a", [QB, D], BF16),
        ("y2b", [QB, D], BF16),
    ):
        if name in debug or (name[:2] in ("h2", "y2") and name[:2] in debug) or \
           (name in ("y1a",) and "y1" in debug):
            dbg[name] = nc.dram_tensor("dbg_" + name, src_shape, dt_, kind="ExternalOutput").ap()

    RG = [list(range(NC))]

    ctx = contextlib.ExitStack()
    with ctx:
        e = ctx.enter_context

        shifts = e(nc.sbuf_tensor("s_shifts", [128, 1], I32))
        epsv = e(nc.sbuf_tensor("s_eps", [128, 1], F32))
        sbq = e(nc.sbuf_tensor("s_bqkv", [128, 12], F32))
        stat6 = e(nc.sbuf_tensor("s_stat6", [128, 8 * 6], F32))
        stat2 = e(nc.sbuf_tensor("s_stat2", [128, 2], F32))
        lnt1 = e(nc.sbuf_tensor("s_lnt1", [128, 1], F32))
        lnt2 = e(nc.sbuf_tensor("s_lnt2", [128, 1], F32))
        fscr = e(nc.sbuf_tensor("s_fscr", [128, 1], F32))
        fsc2 = e(nc.sbuf_tensor("s_fsc2", [128, 1], F32))
        sM = contextlib.ExitStack()
        eM = sM.enter_context
        identb = eM(nc.sbuf_tensor("s_identb", [128, 128], BF16))
        identf = eM(nc.sbuf_tensor("s_identf", [128, 128], F32))
        onesb = eM(nc.sbuf_tensor("s_onesb", [128, 128], BF16))
        sebias = eM(nc.sbuf_tensor("s_ebias", [128, NSLOT * 64], F32))

        pf = [e(nc.psum_tensor(f"pf{i}", [128, 512], F32)) for i in range(6)]
        pb = [e(nc.psum_tensor(f"pb{i}", [128, 1024], BF16)) for i in range(2)]

        def sem(name):
            return Sem(e(nc.semaphore(name)))

        s_init = sem("init")
        s_dql = [sem("dql0"), sem("dql1")]; s_dqv = sem("dqv")
        s_dqs = [sem("dqs0"), sem("dqs1")]
        s_p0l = [sem("p0l0"), sem("p0l1")]; s_p0v = sem("p0v"); s_rst = sem("rst")
        s_p0t = sem("p0t"); s_p0e = sem("p0e"); s_p0s = sem("p0s")
        s_ag0 = sem("ag0")
        s_p1w = [sem("p1w0"), sem("p1w1")]; s_p1wc = sem("p1wc")
        s_p1h = [sem("p1h0"), sem("p1h1"), sem("p1h2"), sem("p1h3")]
        s_p1j = sem("p1j"); s_p1e = sem("p1e")
        s_p1s = [sem("p1s0"), sem("p1s1")]
        s_patt = sem("patt")
        s_kf = [sem("skf0"), sem("skf1")]
        s_vf = [sem("svf0"), sem("svf1")]
        s_qf = [sem("sqf0"), sem("sqf1")]
        s_vt = sem("vt"); s_vte = sem("vte")
        s_sS = sem("sS"); s_sX = sem("sX"); s_sM = sem("sM"); s_sAV = sem("sAV")
        s_sR = sem("sR"); s_sE = sem("sE"); s_sON = sem("sON"); s_oas = sem("oas")
        s_bp = sem("sbp")
        s_p3l = [sem("p3l0"), sem("p3l1")]
        s_p3w = [sem("p3w0"), sem("p3w1")]; s_p3wc = sem("p3wc")
        s_p3m = sem("p3m"); s_p3e = sem("p3e")
        s_p3s = [sem("p3s0"), sem("p3s1")]
        s_rsa1 = sem("rsa1"); s_rsb1 = sem("rsb1")
        s_r5l = sem("r5l"); s_r5v = sem("r5v"); s_r5r = sem("r5r")
        s_h2t = sem("h2t"); s_h2e = sem("h2e")
        s_x1s = sem("x1s"); s_h2sA = sem("h2sA"); s_h2sB = sem("h2sB")
        s_agh2a = sem("agh2a"); s_agh2b = sem("agh2b")
        s_fh = sem("fh"); s_fwg = sem("fwg"); s_fwu = sem("fwu"); s_fwd = sem("fwd")
        s_fg = sem("fg"); s_fsil = sem("fsil"); s_fint = sem("fint")
        s_fy = sem("fy"); s_fye = sem("fye")
        s_fys = [sem("fys0"), sem("fys1")]
        s_rsa2 = sem("rsa2"); s_rsb2 = sem("rsb2")
        s_p5l = [sem("p5l0"), sem("p5l1")]
        s_p5v = sem("p5v"); s_p5r = sem("p5r"); s_p5s = sem("p5s")
        s_dv = sem("dvself"); s_sc = sem("scself")
        s_fab = sem("fab"); s_fbc = sem("fbc"); s_fcd = sem("fcd")
        s_fde = sem("fde"); s_fef = sem("fef")

        def vedge(vector, inst):
            inst.then_inc(s_dv.h, 1)
            s_dv.v += 1
            vector.wait_ge(s_dv.h, s_dv.v)

        def sedge(scalar, inst):
            inst.then_inc(s_sc.h, 1)
            s_sc.v += 1
            scalar.wait_ge(s_sc.h, s_sc.v)

        block = e(nc.Block())

        @block.sync
        def _(sync):
            sync.dma_start(out=shifts[:], in_=shifts_in).then_inc(s_init.h, 16)
            sync.dma_start(out=identf[:], in_=identf_in).then_inc(s_init.h, 16)
            sync.dma_start(out=onesb[:], in_=onesb_in).then_inc(s_init.h, 16)
            sync.dma_start(out=sebias[:], in_=ebias).then_inc(s_init.h, 16)
            sync.dma_start(out=sbq[:], in_=bqkvT).then_inc(s_init.h, 16)
            s_init.v += 80

        @block.vector
        def _(vector):
            vector.wait_ge(s_init.h, 80)
            vector.memset(epsv[:], 1e-5)
            vector.tensor_copy(out=identb[:], in_=identf[:]).then_inc(s_init.h, 1)
            s_init.v += 1
        INIT_DONE = s_init.v

        sDQ = contextlib.ExitStack()
        eQ = sDQ.enter_context
        dqr = [eQ(nc.sbuf_tensor(f"dq_qr{i}", [128, NSL], I32)) for i in range(2)]
        dsc = [eQ(nc.sbuf_tensor(f"dq_sc{i}", [128, NSL], F16)) for i in range(2)]
        dzp = [eQ(nc.sbuf_tensor(f"dq_zp{i}", [128, NSL], F32)) for i in range(2)]
        dnib = eQ(nc.sbuf_tensor("dq_nib", [128, NSL], I32))
        dt1 = eQ(nc.sbuf_tensor("dq_t1", [128, NSL], F32))
        dwb = [eQ(nc.sbuf_tensor(f"dq_wb{i}", [128, NSL], BF16)) for i in range(2)]
        dzrow = [eQ(nc.sbuf_tensor(f"dq_zrow{i}", [128, 128], I32)) for i in range(2)]
        dzaux = eQ(nc.sbuf_tensor("dq_zaux", [128, 1024], I32))
        zpz = eQ(nc.sbuf_tensor("dq_zpz", [32, NSL // 8], I32))
        zps = eQ(nc.sbuf_tensor("dq_zps", [32, NSL], F16))

        gu_src = [(gq, gs, zpgd, wgbf), (uq, us, zpud, wubf)]

        def dq_loads(gpsimd, lo, hi):
            for u in range(lo, hi):
                kind = DQ_UNITS[u]
                sl_ = s_dql[u % 2]
                if kind[0] == "zprep":
                    m_ = kind[1]
                    if u == 1:
                        gpsimd.wait_ge(s_dqv.h, 1)
                    zsrc = gz if m_ == 0 else uz
                    ssrc = gs if m_ == 0 else us
                    gpsimd.dma_start(out=zpz[:], in_=zsrc).then_inc(sl_.h, 16)
                    gpsimd.dma_start(out=zps[:], in_=ssrc).then_inc(sl_.h, 16)
                    sl_.v += 32
                elif kind[0] == "gu":
                    m_, j_ = kind[1], kind[2]
                    qsrc, ssrc, zpd_, _w = gu_src[m_]
                    if u >= 2:
                        gpsimd.wait_ge(s_dqv.h, u - 1)
                    gpsimd.wait_ge(s_dqs[m_].h, 16)
                    rep = bass.AP(qsrc.tensor, 16 * j_ * NSL, [[NSL, 16], [0, 8], [1, NSL]])
                    gpsimd.dma_start(out=dqr[u % 2][:, 0:NSL], in_=rep).then_inc(sl_.h, 16)
                    srep = bass.AP(ssrc.tensor, j_ * NSL, [[0, 128], [1, NSL]])
                    gpsimd.dma_start(out=dsc[u % 2][:, 0:NSL], in_=srep).then_inc(sl_.h, 16)
                    zrep = bass.AP(zpd_.tensor, j_ * NSL, [[0, 128], [1, NSL]])
                    gpsimd.dma_start(out=dzp[u % 2][:, 0:NSL], in_=zrep).then_inc(sl_.h, 16)
                    sl_.v += 48
                else:
                    _, mt_, ec_ = kind
                    nw = NT_SIZES[mt_]
                    if u >= 2:
                        gpsimd.wait_ge(s_dqv.h, u - 1)
                    rep = bass.AP(dq.tensor, 16 * mt_ * D + 1024 * ec_, [[D, nw // 8], [0, 8], [1, 1024]])
                    gpsimd.dma_start(out=dqr[u % 2][0:nw, 0:1024], in_=rep).then_inc(sl_.h, 16)
                    gpsimd.dma_start(out=dsc[u % 2][0:nw, 0:1024],
                                     in_=dsx[128 * mt_:128 * mt_ + nw, 1024 * ec_:1024 * (ec_ + 1)]).then_inc(sl_.h, 16)
                    gpsimd.dma_start(out=dzrow[u % 2][0:nw, :],
                                     in_=dzx[128 * mt_:128 * mt_ + nw, 128 * ec_:128 * (ec_ + 1)]).then_inc(sl_.h, 16)
                    sl_.v += 48

        def dq_vec(vector, lo, hi):
            for u in range(lo, hi):
                kind = DQ_UNITS[u]
                vector.wait_ge(s_dql[u % 2].h, DQL2_AT[u])
                if u == 1:
                    vector.wait_ge(s_dqs[0].h, 16)
                if u == 2:
                    vector.wait_ge(s_dqs[1].h, 16)
                if u >= 2:
                    vector.wait_ge(s_dqs[u % 2].h, DQS2_AT[u - 2])
                if kind[0] == "zprep":
                    zpe_ = dnib[0:32, 0:NSL]
                    zpf_ = dt1[0:32, 0:NSL]
                    zpg_ = dt1[32:64, 0:NSL]
                    for jn in range(8):
                        inst = vector.tensor_scalar(out=zpe_[:, jn::8], in0=zpz[:], scalar1=4 * jn,
                                                    scalar2=15, op0=ALU.logical_shift_right, op1=ALU.bitwise_and)
                    vedge(vector, inst)
                    vedge(vector, vector.tensor_scalar(out=zpf_[:], in0=zpe_[:], scalar1=1.0, scalar2=None, op0=ALU.add))
                    vector.tensor_tensor(out=zpg_[:], in0=zpf_[:], in1=zps[:], op=ALU.mult).then_inc(s_dqv.h, 1)
                elif kind[0] == "gu":
                    u2 = u % 2
                    vedge(vector, vector.tensor_scalar(out=dnib[:, 0:NSL], in0=dqr[u2][:, 0:NSL], scalar1=shifts[:],
                                                       scalar2=15, op0=ALU.logical_shift_right, op1=ALU.bitwise_and))
                    vedge(vector, vector.tensor_tensor(out=dt1[:, 0:NSL], in0=dnib[:, 0:NSL],
                                                       in1=dsc[u2][:, 0:NSL], op=ALU.mult))
                    vector.tensor_tensor(out=dwb[u2][:, 0:NSL], in0=dt1[:, 0:NSL], in1=dzp[u2][:, 0:NSL],
                                         op=ALU.subtract).then_inc(s_dqv.h, 1)
                else:
                    _, mt_, ec_ = kind
                    u2 = u % 2
                    for jn in range(8):
                        inst = vector.tensor_scalar(out=dzaux[:, jn::8], in0=dzrow[u % 2][:], scalar1=4 * jn,
                                                    scalar2=15, op0=ALU.logical_shift_right, op1=ALU.bitwise_and)
                    vedge(vector, inst)
                    vedge(vector, vector.tensor_scalar(out=dt1[:, 0:1024], in0=dzaux[:], scalar1=1.0,
                                                       scalar2=None, op0=ALU.add))
                    vedge(vector, vector.tensor_tensor(out=dzp[u2][:, 0:1024], in0=dt1[:, 0:1024],
                                                       in1=dsc[u2][:, 0:1024], op=ALU.mult))
                    vedge(vector, vector.tensor_scalar(out=dnib[:, 0:1024], in0=dqr[u2][:, 0:1024], scalar1=shifts[:],
                                                       scalar2=15, op0=ALU.logical_shift_right, op1=ALU.bitwise_and))
                    vedge(vector, vector.tensor_tensor(out=dt1[:, 0:1024], in0=dnib[:, 0:1024],
                                                       in1=dsc[u2][:, 0:1024], op=ALU.mult))
                    vector.tensor_tensor(out=dwb[u2][:, 0:1024], in0=dt1[:, 0:1024], in1=dzp[u2][:, 0:1024],
                                         op=ALU.subtract).then_inc(s_dqv.h, 1)
                s_dqv.v += 1

        def dq_stores(gpsimd, lo, hi):
            for u in range(lo, hi):
                kind = DQ_UNITS[u]
                ss_ = s_dqs[u % 2]
                gpsimd.wait_ge(s_dqv.h, u + 1)
                if kind[0] == "zprep":
                    dst = zpgd if kind[1] == 0 else zpud
                    gpsimd.dma_start(out=dst, in_=dt1[32:64, 0:NSL]).then_inc(ss_.h, 16)
                elif kind[0] == "gu":
                    m_, j_ = kind[1], kind[2]
                    wdst = gu_src[m_][3]
                    for nt in range(NNT):
                        nw = NT_SIZES[nt]
                        gpsimd.dma_start(out=wdst[128 * nt:128 * (nt + 1), j_ * nw:(j_ + 1) * nw],
                                         in_=dwb[u % 2][:, NT_OFF[nt]:NT_OFF[nt] + nw]).then_inc(ss_.h, 16)
                        ss_.v += 16
                    ss_.v -= 16
                else:
                    _, mt_, ec_ = kind
                    nw = NT_SIZES[mt_]
                    gpsimd.dma_start(out=wdbf[128 * mt_:128 * mt_ + nw, 1024 * ec_:1024 * (ec_ + 1)],
                                     in_=dwb[u % 2][0:nw, 0:1024]).then_inc(ss_.h, 16)
                ss_.v += 16

        dq_ptr = [0]

        def dq_chunk(gpsimd, lo, hi, flush=False):
            for u in range(lo, hi):
                while dq_ptr[0] <= u - 2:
                    dq_stores(gpsimd, dq_ptr[0], dq_ptr[0] + 1)
                    dq_ptr[0] += 1
                dq_loads(gpsimd, u, u + 1)
            if flush:
                while dq_ptr[0] < hi:
                    dq_stores(gpsimd, dq_ptr[0], dq_ptr[0] + 1)
                    dq_ptr[0] += 1

        sA = contextlib.ExitStack()
        eA = sA.enter_context
        x0t = [eA(nc.sbuf_tensor(f"p0_x{i}", [128, D], F32)) for i in range(2)]
        h0t = eA(nc.sbuf_tensor("p0_h0", [128, D], BF16))
        h0stg = eA(nc.sbuf_tensor("p0_stg", [128, D], BF16))

        NT0 = TPC // 128

        @block.sync
        def _(sync):
            for i in range(NT0):
                if i >= 2:
                    sync.wait_ge(s_p0v.h, 2 * (i - 1))
                sp = s_p0l[i % 2]
                sync.dma_start(out=x0t[i % 2][:], in_=xrows[128 * i:128 * (i + 1), :]).then_inc(sp.h, 16)
                sp.v += 16

        @block.vector
        def _(vector):
            vector.wait_ge(s_init.h, INIT_DONE)
            for i in range(NT0):
                vector.wait_ge(s_p0l[i % 2].h, 16 * (i // 2 + 1))
                if i >= 1:
                    vector.wait_ge(s_rst.h, i)
                xt = x0t[i % 2]
                for ch in range(8):
                    inst = vector.bn_stats(out=stat6[:, 6 * ch:6 * ch + 6], in_=xt[:, 512 * ch:512 * (ch + 1)])
                vedge(vector, inst)
                vector.bn_aggr(out=stat2[:], in_=stat6[:]).then_inc(s_p0v.h, 1)
                s_p0v.v += 1
                vector.wait_ge(s_rst.h, i + 1)
                if i >= 1:
                    vector.wait_ge(s_p0t.h, 4 * i)
                vector.tensor_scalar(out=h0t[:], in0=xt[:], scalar1=stat2[:, 0:1],
                                     scalar2=lnt2[:], op0=ALU.subtract, op1=ALU.mult).then_inc(s_p0v.h, 1)
                s_p0v.v += 1
                for g in range(4):
                    vector.wait_ge(s_p0t.h, 4 * i + g + 1)
                    if g == 0 and i >= 1:
                        vector.wait_ge(s_p0s.h, 16 * i)
                    vector.tensor_copy(out=h0stg[:, 1024 * g:1024 * (g + 1)],
                                       in_=pb[g % 2][:]).then_inc(s_p0e.h, 1)
                    s_p0e.v += 1

        @block.scalar
        def _(scalar):
            for i in range(NT0):
                scalar.wait_ge(s_p0v.h, 2 * i + 1)
                sedge(scalar, scalar.activation(out=lnt1[:], in_=stat2[:, 1:2], func=AF.Ln, bias=epsv[:]))
                scalar.activation(out=lnt2[:], in_=lnt1[:], func=AF.Exp, scale=-0.5).then_inc(s_rst.h, 1)
                s_rst.v += 1

        @block.tensor
        def _(tensor):
            tensor.wait_ge(s_init.h, INIT_DONE)
            for i in range(NT0):
                tensor.wait_ge(s_p0v.h, 2 * i + 2)
                for g in range(4):
                    gg = 4 * i + g
                    if gg >= 2:
                        tensor.wait_ge(s_p0e.h, gg - 1)
                    for k in range(8):
                        ch = 8 * g + k
                        inst = tensor.transpose(pb[g % 2][:, 128 * k:128 * (k + 1)],
                                                h0t[:, 128 * ch:128 * (ch + 1)], identb[:])
                    inst.then_inc(s_p0t.h, 1)
                    s_p0t.v += 1

        @block.gpsimd
        def _(gpsimd):
            for i in range(NT0):
                gpsimd.wait_ge(s_p0e.h, 4 * (i + 1))
                dstap = bass.AP(h0slab.tensor, 128 * i, [[TPC, 128], [128 * TPC, 32], [1, 128]])
                gpsimd.dma_start(out=dstap,
                                 in_=h0stg.ap().rearrange("p (j t) -> p j t", t=128)).then_inc(s_p0s.h, 16)
                s_p0s.v += 16
            gpsimd.wait_ge(s_p0s.h, s_p0s.v)
            gpsimd.collective_compute("AllGather", ALU.bypass, RG, [h0slab.opt()], [h0g.opt()]).then_inc(s_ag0.h, 1)
            s_ag0.v += 1

        @block.gpsimd
        def _(gpsimd):
            gpsimd.wait_ge(s_init.h, 80)
            dq_chunk(gpsimd, *DQ_W_A)

        @block.vector
        def _(vector):
            dq_vec(vector, *DQ_W_A)

        FAB = 3

        @block.vector
        def _(vector):
            vector.tensor_copy(out=fscr[:], in_=epsv[:]).then_inc(s_fab.h, 1)

        @block.scalar
        def _(scalar):
            scalar.activation(out=fsc2[:], in_=epsv[:], func=AF.Copy).then_inc(s_fab.h, 1)

        @block.gpsimd
        def _(gpsimd):
            gpsimd.wait_ge(s_p0s.h, s_p0s.v)
            gpsimd.wait_ge(s_dqs[0].h, s_dqs[0].v)
            gpsimd.wait_ge(s_dqs[1].h, s_dqs[1].v)
            gpsimd.sem_inc(s_fab.h, 1)

        sA.close()
        sB = contextlib.ExitStack()
        eB = sB.enter_context
        wqsb = eB(nc.sbuf_tensor("p1_w", [128, NCH * 768], BF16))
        wqld = [eB(nc.sbuf_tensor(f"p1_wl{i}", [128, 768], F32)) for i in range(2)]
        h0rhs = [eB(nc.sbuf_tensor(f"p1_h{i}", [128, 512], BF16)) for i in range(4)]
        qkev = [eB(nc.sbuf_tensor(f"p1_e{i}", [128, 6 * 512], BF16)) for i in range(2)]

        @block.sync
        def _(sync):
            sync.wait_ge(s_fab.h, FAB)
            for sw in range(2):
                for j in range(NCH):
                    cidx = sw * NCH + j
                    if cidx >= 2:
                        sync.wait_ge(s_p1wc.h, cidx - 1)
                    sp = s_p1w[cidx % 2]
                    sync.dma_start(out=wqld[cidx % 2][:],
                                   in_=wqkv[128 * j:128 * (j + 1), 768 * sw:768 * (sw + 1)]).then_inc(sp.h, 16)
                    sp.v += 16
                if sw == 0:
                    sync.wait_ge(s_ag0.h, 1)
                for m in range(NQB):
                    for j in range(NCH):
                        idx = (sw * NQB + m) * NCH + j
                        if idx >= 4:
                            sync.wait_ge(s_p1j.h, idx - 3)
                        rk = m // 2
                        tl = 512 * (m % 2)
                        src = h0g[D * rk + 128 * j:D * rk + 128 * (j + 1), tl:tl + 512]
                        sp = s_p1h[idx % 4]
                        sync.dma_start(out=h0rhs[idx % 4][:], in_=src).then_inc(sp.h, 16)
                        sp.v += 16

        @block.vector
        def _(vector):
            vector.wait_ge(s_fab.h, FAB)
            for sw in range(2):
                if sw == 1:
                    vector.wait_ge(s_p1j.h, NCH * NQB)
                for j in range(NCH):
                    cidx = sw * NCH + j
                    vector.wait_ge(s_p1w[cidx % 2].h, 16 * (cidx // 2 + 1))
                    vector.tensor_copy(out=wqsb[:, 768 * j:768 * (j + 1)],
                                       in_=wqld[cidx % 2][:]).then_inc(s_p1wc.h, 1)
                    s_p1wc.v += 1

        @block.tensor
        def _(tensor):
            tensor.wait_ge(s_fab.h, FAB)
            for sw in range(2):
                tensor.wait_ge(s_p1wc.h, NCH * (sw + 1))
                for m in range(NQB):
                    gidx = sw * NQB + m
                    if gidx >= 1:
                        tensor.wait_ge(s_p1e.h, 6 * gidx)
                    for j in range(NCH):
                        idx = gidx * NCH + j
                        tensor.wait_ge(s_p1h[idx % 4].h, 16 * (idx // 4 + 1))
                        for q6 in range(6):
                            inst = tensor.matmul(pf[q6][:],
                                                 wqsb[:, 768 * j + 128 * q6:768 * j + 128 * (q6 + 1)],
                                                 h0rhs[idx % 4][:], start=(j == 0), stop=(j == NCH - 1))
                        inst.then_inc(s_p1j.h, 1)
                        s_p1j.v += 1

        @block.scalar
        def _(scalar):
            for sw in range(2):
                for m in range(NQB):
                    gidx = sw * NQB + m
                    scalar.wait_ge(s_p1j.h, NCH * (gidx + 1))
                    if gidx >= 2:
                        scalar.wait_ge(s_p1s[gidx % 2].h, 96 * (gidx // 2))
                    for q6 in range(6):
                        sq = 3 * (2 * sw + q6 // 3) + (q6 % 3)
                        scalar.activation(out=qkev[gidx % 2][:, 512 * q6:512 * (q6 + 1)], in_=pf[q6][:],
                                          func=AF.Identity, bias=sbq[:, sq:sq + 1],
                                          scale=1.0).then_inc(s_p1e.h, 1)
                        s_p1e.v += 1

        P1_PIECES = _split_range(DQ_W_B[0], DQ_W_B[1], 32)

        @block.gpsimd
        def _(gpsimd):
            for sw in range(2):
                for m in range(NQB):
                    gidx = sw * NQB + m
                    gpsimd.wait_ge(s_p1e.h, 6 * (gidx + 1))
                    sp = s_p1s[gidx % 2]
                    for q6 in range(6):
                        sq = 3 * (2 * sw + q6 // 3) + (q6 % 3)
                        gpsimd.dma_start(out=qkvd[sq, :, 512 * m:512 * (m + 1)],
                                         in_=qkev[gidx % 2][:, 512 * q6:512 * (q6 + 1)]).then_inc(sp.h, 16)
                        sp.v += 16
                    dq_chunk(gpsimd, *P1_PIECES[gidx])

        @block.vector
        def _(vector):
            dq_vec(vector, *DQ_W_B)

        FBC = 3

        @block.vector
        def _(vector):
            vector.tensor_copy(out=fscr[:], in_=epsv[:]).then_inc(s_fbc.h, 1)

        @block.scalar
        def _(scalar):
            scalar.activation(out=fsc2[:], in_=epsv[:], func=AF.Copy).then_inc(s_fbc.h, 1)

        @block.gpsimd
        def _(gpsimd):
            gpsimd.wait_ge(s_p1s[0].h, s_p1s[0].v)
            gpsimd.wait_ge(s_p1s[1].h, s_p1s[1].v)
            gpsimd.wait_ge(s_dqs[0].h, s_dqs[0].v)
            gpsimd.wait_ge(s_dqs[1].h, s_dqs[1].v)
            gpsimd.sem_inc(s_fbc.h, 1)

        sB.close()
        sC = contextlib.ExitStack()
        eC = sC.enter_context
        spatt = eC(nc.sbuf_tensor("p2_patt", [128, 20 * 512], BF16))
        kf2 = [eC(nc.sbuf_tensor(f"p2_k{i}", [128, S], BF16)) for i in range(2)]
        vf2 = [eC(nc.sbuf_tensor(f"p2_v{i}", [128, S], BF16)) for i in range(2)]
        qf2 = [eC(nc.sbuf_tensor(f"p2_q{i}", [128, S], BF16)) for i in range(2)]
        vt = eC(nc.sbuf_tensor("p2_vt", [128, S], BF16))
        ptx = [eC(nc.sbuf_tensor(f"p2_px{i}", [128, 512], BF16)) for i in range(4)]
        ptm = [eC(nc.sbuf_tensor(f"p2_pm{i}", [128, 512], BF16)) for i in range(4)]
        osb = eC(nc.sbuf_tensor("p2_osb", [128, 512], BF16))
        rsb = eC(nc.sbuf_tensor("p2_rsb", [128, 512], F32))
        onb = eC(nc.sbuf_tensor("p2_onb", [128, 512], BF16))

        @block.sync
        def _(sync):
            sync.wait_ge(s_fbc.h, FBC)
            sync.dma_start(out=spatt[:],
                           in_=patt.rearrange("(v p) q -> p v q", p=128)).then_inc(s_patt.h, 16)
            s_patt.v += 16
            for s_ in range(NSLOT):
                if s_ >= 2:
                    gend = BLOCK_RANGE[(s_ - 2, NQB - 1)][1]
                    sync.wait_ge(s_sAV.h, gend)
                    sync.wait_ge(s_vt.h, 8 * (s_ - 1))
                sync.dma_start(out=kf2[s_ % 2][:], in_=qkvd[3 * s_ + 1, :, :]).then_inc(s_kf[s_ % 2].h, 16)
                sync.dma_start(out=vf2[s_ % 2][:], in_=qkvd[3 * s_ + 2, :, :]).then_inc(s_vf[s_ % 2].h, 16)
                sync.dma_start(out=qf2[s_ % 2][:], in_=qkvd[3 * s_ + 0, :, :]).then_inc(s_qf[s_ % 2].h, 16)
                s_kf[s_ % 2].v += 16; s_vf[s_ % 2].v += 16; s_qf[s_ % 2].v += 16

        for s_ in range(NSLOT):
            lo, hi = DQ_W_S[s_]
            P2_PIECES = _split_range(lo, hi, NQB)

            def make_tensor(s_=s_):
                def emit(tensor):
                    kf = kf2[s_ % 2]
                    vf = vf2[s_ % 2]
                    qf = qf2[s_ % 2]
                    if s_ == 0:
                        tensor.wait_ge(s_fbc.h, FBC)
                    tensor.wait_ge(s_vf[s_ % 2].h, 16 * (s_ // 2 + 1))
                    for gvt in range(8):
                        gg = 8 * s_ + gvt
                        if gg >= 2:
                            tensor.wait_ge(s_vte.h, gg - 1)
                        for k in range(8):
                            ch = 8 * gvt + k
                            inst = tensor.transpose(pb[gvt % 2][:, 128 * k:128 * (k + 1)],
                                                    vf[:, 128 * ch:128 * (ch + 1)], identb[:])
                        inst.then_inc(s_vt.h, 1)
                        s_vt.v += 1
                    tensor.wait_ge(s_kf[s_ % 2].h, 16 * (s_ // 2 + 1))
                    tensor.wait_ge(s_qf[s_ % 2].h, 16 * (s_ // 2 + 1))
                    tensor.wait_ge(s_patt.h, 16)
                    tensor.wait_ge(s_vte.h, 8 * (s_ + 1))
                    LA = 3
                    gs0 = BLOCK_RANGE[(s_, 0)][0]
                    gs1 = BLOCK_RANGE[(s_, NQB - 1)][1]

                    def emit_S(gi):
                        sx, bx, ux = STEP_U[gi]
                        k0x = QB * bx + 128 * ux
                        sbank = [pf[0], pf[1], pf[4], pf[5]][gi % 4]
                        if gi >= 4:
                            tensor.wait_ge(s_sX.h, gi - 3)
                        inst = tensor.matmul(sbank[:], kf[:, k0x:k0x + 128],
                                             qf[:, QB * bx:QB * (bx + 1)], start=True, stop=True)
                        inst.then_inc(s_sS.h, 1)
                        s_sS.v += 1

                    for gpre in range(gs0, min(gs0 + LA, gs1)):
                        emit_S(gpre)
                    for b_ in range(NQB):
                        g0, g1 = BLOCK_RANGE[(s_, b_)]
                        blkidx = s_ * NQB + b_
                        if blkidx >= 1:
                            tensor.wait_ge(s_sR.h, blkidx)
                            tensor.wait_ge(s_sE.h, blkidx)
                        for i, gi in enumerate(range(g0, g1)):
                            if gi + LA < gs1:
                                emit_S(gi + LA)
                            u_ = STEP_U[gi][2]
                            k0 = QB * b_ + 128 * u_
                            tensor.wait_ge(s_sM.h, gi + 1)
                            tensor.matmul(pf[2][:], onesb[:], ptm[gi % 4][:],
                                          start=(i == 0), stop=(i == g1 - g0 - 1))
                            ct = k0 // 128
                            inst = tensor.matmul(pf[3][:], vt[:, 128 * ct:128 * (ct + 1)],
                                                 ptm[gi % 4][:],
                                                 start=(i == 0), stop=(i == g1 - g0 - 1))
                            inst.then_inc(s_sAV.h, 1)
                            s_sAV.v += 1
                return emit
            block.tensor(make_tensor())

            def make_scalar(s_=s_):
                def emit(scalar):
                    for b_ in range(NQB):
                        g0, g1 = BLOCK_RANGE[(s_, b_)]
                        blkidx = s_ * NQB + b_
                        for i, gi in enumerate(range(g0, g1)):
                            u_ = STEP_U[gi][2]
                            scalar.wait_ge(s_sS.h, gi + 1)
                            if gi >= 4:
                                scalar.wait_ge(s_sM.h, gi - 3)
                            iu = 3 - u_
                            sbank = [pf[0], pf[1], pf[4], pf[5]][gi % 4]
                            scalar.activation(out=ptx[gi % 4][:], in_=sbank[:], func=AF.Exp,
                                              bias=sebias[:, 64 * s_ + iu:64 * s_ + iu + 1],
                                              scale=1.0 / 128).then_inc(s_sX.h, 1)
                            s_sX.v += 1
                        scalar.wait_ge(s_sAV.h, g1)
                        if blkidx >= 1:
                            scalar.wait_ge(s_sON.h, blkidx)
                        scalar.activation(out=osb[:], in_=pf[3][:], func=AF.Copy).then_inc(s_sE.h, 1)
                        s_sE.v += 1
                return emit
            block.scalar(make_scalar())

            def make_vector(s_=s_, P2_PIECES=P2_PIECES):
                def emit(vector):
                    if s_ == 0:
                        vector.wait_ge(s_patt.h, 16)
                    for gvt in range(8):
                        vector.wait_ge(s_vt.h, 8 * s_ + gvt + 1)
                        vector.tensor_copy(out=vt[:, 1024 * gvt:1024 * (gvt + 1)],
                                           in_=pb[gvt % 2][:]).then_inc(s_vte.h, 1)
                        s_vte.v += 1
                    for b_ in range(NQB):
                        g0, g1 = BLOCK_RANGE[(s_, b_)]
                        blkidx = s_ * NQB + b_
                        for i, gi in enumerate(range(g0, g1)):
                            u_ = STEP_U[gi][2]
                            vector.wait_ge(s_sX.h, gi + 1)
                            if gi >= 4:
                                vector.wait_ge(s_sAV.h, gi - 3)
                            v_ = u_ if 0 <= u_ else 4
                            pslice = spatt[:, 512 * (5 * s_ + v_):512 * (5 * s_ + v_ + 1)]
                            vector.tensor_tensor(out=ptm[gi % 4][:], in0=ptx[gi % 4][:],
                                                 in1=pslice, op=ALU.mult).then_inc(s_sM.h, 1)
                            s_sM.v += 1
                        vector.wait_ge(s_sAV.h, g1)
                        vector.reciprocal(out=rsb[:], in_=pf[2][:]).then_inc(s_sR.h, 1)
                        s_sR.v += 1
                        vector.wait_ge(s_sR.h, s_sR.v)
                        vector.wait_ge(s_sE.h, blkidx + 1)
                        if blkidx >= 1:
                            vector.wait_ge(s_oas.h, 16 * blkidx)
                        vector.tensor_tensor(out=onb[:], in0=osb[:], in1=rsb[:],
                                             op=ALU.mult).then_inc(s_sON.h, 1)
                        s_sON.v += 1
                        dq_vec(vector, *P2_PIECES[b_])
                return emit
            block.vector(make_vector())

            def make_gp(s_=s_, P2_PIECES=P2_PIECES, last=(s_ == NSLOT - 1)):
                def emit(gpsimd):
                    for b_ in range(NQB):
                        blkidx = s_ * NQB + b_
                        gpsimd.wait_ge(s_sON.h, blkidx + 1)
                        gpsimd.dma_start(out=oaslab[128 * s_:128 * (s_ + 1), QB * b_:QB * (b_ + 1)],
                                         in_=onb[:]).then_inc(s_oas.h, 16)
                        s_oas.v += 16
                        dq_chunk(gpsimd, *P2_PIECES[b_],
                                 flush=(last and b_ == NQB - 1))
                return emit
            block.gpsimd(make_gp())



        FCD = 3

        @block.vector
        def _(vector):
            vector.tensor_copy(out=fscr[:], in_=epsv[:]).then_inc(s_fcd.h, 1)

        @block.scalar
        def _(scalar):
            scalar.activation(out=fsc2[:], in_=epsv[:], func=AF.Copy).then_inc(s_fcd.h, 1)

        @block.gpsimd
        def _(gpsimd):
            gpsimd.wait_ge(s_oas.h, s_oas.v)
            gpsimd.wait_ge(s_dqs[0].h, s_dqs[0].v)
            gpsimd.wait_ge(s_dqs[1].h, s_dqs[1].v)
            gpsimd.sem_inc(s_fcd.h, 1)

        sC.close()
        sDQ.close()
        sD = contextlib.ExitStack()
        eD = sD.enter_context
        wpld = [eD(nc.sbuf_tensor(f"p3_wl{i}", [128, D], F32)) for i in range(2)]
        wpcb = eD(nc.sbuf_tensor("p3_w", [128, NSLOT * D], BF16))
        oalhs = [eD(nc.sbuf_tensor(f"p3_o{i}", [128, NSLOT * 512], BF16)) for i in range(2)]
        yev = [eD(nc.sbuf_tensor(f"p3_y{i}", [128, D], BF16)) for i in range(2)]
        bprep = eD(nc.sbuf_tensor("p3_bp", [128, D], F32))
        ylt = eD(nc.sbuf_tensor("p3_yl", [128, D], BF16))
        ybt = eD(nc.sbuf_tensor("p3_yb", [128, D], F32))
        lnb = eD(nc.sbuf_tensor("p3_ln", [128, D], F32))
        xt3 = eD(nc.sbuf_tensor("p3_x", [128, D], F32))
        x1b = eD(nc.sbuf_tensor("p3_x1", [128, D], F32))
        h2t = eD(nc.sbuf_tensor("p3_h2", [128, D], BF16))
        h2stg = eD(nc.sbuf_tensor("p3_h2s", [128, D], BF16))

        P3_TILES = [(r, q, 0) for r in range(NC) for q in range(4)] + \
                   [(r, q, 1) for r in range(NC) for q in range(4)]

        @block.sync
        def _(sync):
            sync.wait_ge(s_fcd.h, FCD)
            brep = bass.AP(bproj.tensor, 0, [[0, 128], [1, D]])
            sync.dma_start(out=bprep[:], in_=brep).then_inc(s_bp.h, 16)
            s_bp.v += 16
            for sl in range(NSLOT):
                if sl >= 2:
                    sync.wait_ge(s_p3wc.h, sl - 1)
                sp = s_p3w[sl % 2]
                sync.dma_start(out=wpld[sl % 2][:],
                               in_=wprojc[128 * sl:128 * (sl + 1), :]).then_inc(sp.h, 16)
                sp.v += 16
            for lg in range(16):
                r, hf = lg % 8, lg // 8
                if lg >= 2:
                    sync.wait_ge(s_p3m.h, 8 * 4 * (lg - 1))
                src = bass.AP(oaslab.tensor, 1024 * r + 512 * hf,
                              [[S, 128], [128 * S, NSLOT], [1, 512]])
                sp = s_p3l[lg % 2]
                sync.dma_start(out=oalhs[lg % 2].ap().rearrange("p (s t) -> p s t", t=512),
                               in_=src).then_inc(sp.h, 16)
                sp.v += 16

        @block.vector
        def _(vector):
            vector.wait_ge(s_fcd.h, FCD)
            for sl in range(NSLOT):
                vector.wait_ge(s_p3w[sl % 2].h, 16 * (sl // 2 + 1))
                vector.tensor_copy(out=wpcb[:, D * sl:D * (sl + 1)],
                                   in_=wpld[sl % 2][:]).then_inc(s_p3wc.h, 1)
                s_p3wc.v += 1

        @block.tensor
        def _(tensor):
            tensor.wait_ge(s_fcd.h, FCD)
            tensor.wait_ge(s_p3wc.h, NSLOT)
            for ti, (r, q, hf) in enumerate(P3_TILES):
                lg = 8 * hf + r
                tensor.wait_ge(s_p3l[lg % 2].h, 16 * (lg // 2 + 1))
                for ec in range(8):
                    eidx = 8 * ti + ec
                    if eidx >= 4:
                        tensor.wait_ge(s_p3e.h, eidx - 3)
                    for sl in range(NSLOT):
                        inst = tensor.matmul(pf[ec % 4][:],
                                             oalhs[lg % 2][:, 512 * sl + 128 * q:512 * sl + 128 * (q + 1)],
                                             wpcb[:, D * sl + 512 * ec:D * sl + 512 * (ec + 1)],
                                             start=(sl == 0), stop=(sl == NSLOT - 1))
                    inst.then_inc(s_p3m.h, 1)
                    s_p3m.v += 1

        @block.scalar
        def _(scalar):
            for ti in range(len(P3_TILES)):
                if ti >= 2:
                    scalar.wait_ge(s_p3s[ti % 2].h, 16 * (ti // 2))
                for ec in range(8):
                    eidx = 8 * ti + ec
                    scalar.wait_ge(s_p3m.h, eidx + 1)
                    scalar.activation(out=yev[ti % 2][:, 512 * ec:512 * (ec + 1)], in_=pf[ec % 4][:],
                                      func=AF.Copy).then_inc(s_p3e.h, 1)
                    s_p3e.v += 1

        @block.gpsimd
        def _(gpsimd):
            for ti, (r, q, hf) in enumerate(P3_TILES):
                ydst = y1dA if hf == 0 else y1dB
                gpsimd.wait_ge(s_p3e.h, 8 * (ti + 1))
                sp = s_p3s[ti % 2]
                gpsimd.dma_start(out=ydst[QB * r + 128 * q:QB * r + 128 * (q + 1), :],
                                 in_=yev[ti % 2][:]).then_inc(sp.h, 16)
                sp.v += 16
                if ti == NC * 4 - 1:
                    gpsimd.wait_ge(s_p3s[0].h, 16 * NC * 2)
                    gpsimd.wait_ge(s_p3s[1].h, 16 * NC * 2)
                    gpsimd.collective_compute("ReduceScatter", ALU.add, RG,
                                              [y1dA.opt()], [y1rA.opt()]).then_inc(s_rsa1.h, 1)
                    s_rsa1.v += 1
            gpsimd.wait_ge(s_p3s[0].h, 16 * NC * 4)
            gpsimd.wait_ge(s_p3s[1].h, 16 * NC * 4)
            gpsimd.collective_compute("ReduceScatter", ALU.add, RG,
                                      [y1dB.opt()], [y1rB.opt()]).then_inc(s_rsb1.h, 1)
            s_rsb1.v += 1

        @block.sync
        def _(sync):
            for hf in range(2):
                sync.wait_ge((s_rsa1 if hf == 0 else s_rsb1).h, 1)
                ysrc = y1rA if hf == 0 else y1rB
                for q in range(4):
                    rowi = 4 * hf + q
                    if rowi >= 1:
                        sync.wait_ge(s_r5v.h, 5 * rowi)
                    sync.dma_start(out=ylt[:], in_=ysrc[128 * q:128 * (q + 1), :]).then_inc(s_r5l.h, 16)
                    sync.dma_start(out=xt3[:],
                                   in_=xrows[QB * hf + 128 * q:QB * hf + 128 * (q + 1), :]).then_inc(s_r5l.h, 16)
                    s_r5l.v += 32

        @block.vector
        def _(vector):
            for hf in range(2):
                for q in range(4):
                    rowi = 4 * hf + q
                    vector.wait_ge(s_r5l.h, 32 * (rowi + 1))
                    if rowi == 0:
                        vector.wait_ge(s_bp.h, 16)
                    if rowi >= 1:
                        vector.wait_ge(s_r5r.h, 2 * rowi)
                    vedge(vector, vector.tensor_tensor(out=ybt[:], in0=ylt[:], in1=bprep[:], op=ALU.add))
                    for ch in range(8):
                        inst = vector.bn_stats(out=stat6[:, 6 * ch:6 * ch + 6], in_=ybt[:, 512 * ch:512 * (ch + 1)])
                    vedge(vector, inst)
                    vector.bn_aggr(out=stat2[:], in_=stat6[:]).then_inc(s_r5v.h, 1)
                    vector.wait_ge(s_r5r.h, 2 * rowi + 1)
                    vector.tensor_scalar(out=lnb[:], in0=ybt[:], scalar1=stat2[:, 0:1],
                                         scalar2=lnt2[:], op0=ALU.subtract, op1=ALU.mult).then_inc(s_r5v.h, 1)
                    vector.wait_ge(s_r5v.h, 5 * rowi + 2)
                    if rowi >= 1:
                        vector.wait_ge(s_x1s.h, 16 * rowi)
                    vector.tensor_tensor(out=x1b[:], in0=xt3[:], in1=lnb[:], op=ALU.add).then_inc(s_r5v.h, 1)
                    vector.wait_ge(s_r5v.h, 5 * rowi + 3)
                    for ch in range(8):
                        inst = vector.bn_stats(out=stat6[:, 6 * ch:6 * ch + 6], in_=x1b[:, 512 * ch:512 * (ch + 1)])
                    vedge(vector, inst)
                    vector.bn_aggr(out=stat2[:], in_=stat6[:]).then_inc(s_r5v.h, 1)
                    vector.wait_ge(s_r5r.h, 2 * rowi + 2)
                    if rowi >= 1:
                        vector.wait_ge(s_h2t.h, 4 * rowi)
                    vector.tensor_scalar(out=h2t[:], in0=x1b[:], scalar1=stat2[:, 0:1],
                                         scalar2=lnt2[:], op0=ALU.subtract, op1=ALU.mult).then_inc(s_r5v.h, 1)
                    s_r5v.v += 5
                    for g in range(4):
                        gg = 4 * rowi + g
                        vector.wait_ge(s_h2t.h, gg + 1)
                        if g == 0 and rowi >= 1:
                            if rowi <= 4:
                                vector.wait_ge(s_h2sA.h, 16 * min(rowi, 4))
                            if rowi > 4:
                                vector.wait_ge(s_h2sB.h, 16 * (rowi - 4))
                        vector.tensor_copy(out=h2stg[:, 1024 * g:1024 * (g + 1)],
                                           in_=pb[g % 2][:]).then_inc(s_h2e.h, 1)
                        s_h2e.v += 1

        @block.scalar
        def _(scalar):
            for rowi in range(8):
                for k2 in range(2):
                    scalar.wait_ge(s_r5v.h, 5 * rowi + (1 if k2 == 0 else 4))
                    sedge(scalar, scalar.activation(out=lnt1[:], in_=stat2[:, 1:2], func=AF.Ln, bias=epsv[:]))
                    scalar.activation(out=lnt2[:], in_=lnt1[:], func=AF.Exp, scale=-0.5).then_inc(s_r5r.h, 1)
                    s_r5r.v += 1

        @block.tensor
        def _(tensor):
            for rowi in range(8):
                tensor.wait_ge(s_r5v.h, 5 * rowi + 5)
                for g in range(4):
                    gg = 4 * rowi + g
                    if gg >= 2:
                        tensor.wait_ge(s_h2e.h, gg - 1)
                    for k in range(8):
                        ch = 8 * g + k
                        inst = tensor.transpose(pb[g % 2][:, 128 * k:128 * (k + 1)],
                                                h2t[:, 128 * ch:128 * (ch + 1)], identb[:])
                    inst.then_inc(s_h2t.h, 1)
                    s_h2t.v += 1

        @block.gpsimd
        def _(gpsimd):
            for hf in range(2):
                h2dst = h2slabA if hf == 0 else h2slabB
                sh2 = s_h2sA if hf == 0 else s_h2sB
                for q in range(4):
                    rowi = 4 * hf + q
                    gpsimd.wait_ge(s_r5v.h, 5 * rowi + 3)
                    gpsimd.dma_start(out=x1d[QB * hf + 128 * q:QB * hf + 128 * (q + 1), :],
                                     in_=x1b[:]).then_inc(s_x1s.h, 16)
                    s_x1s.v += 16
                    gpsimd.wait_ge(s_h2e.h, 4 * (rowi + 1))
                    dstap = bass.AP(h2dst.tensor, 128 * q, [[QB, 128], [128 * QB, 32], [1, 128]])
                    gpsimd.dma_start(out=dstap,
                                     in_=h2stg.ap().rearrange("p (j t) -> p j t", t=128)).then_inc(sh2.h, 16)
                    sh2.v += 16

        @block.gpsimd
        def _(gpsimd):
            gpsimd.wait_ge(s_h2sA.h, 16 * 4)
            gpsimd.collective_compute("AllGather", ALU.bypass, RG,
                                      [h2slabA.opt()], [h2gA.opt()]).then_inc(s_agh2a.h, 1)
            s_agh2a.v += 1
            gpsimd.wait_ge(s_h2sB.h, 16 * 4)
            gpsimd.collective_compute("AllGather", ALU.bypass, RG,
                                      [h2slabB.opt()], [h2gB.opt()]).then_inc(s_agh2b.h, 1)
            s_agh2b.v += 1

        FDE = 3

        @block.vector
        def _(vector):
            vector.tensor_copy(out=fscr[:], in_=epsv[:]).then_inc(s_fde.h, 1)

        @block.scalar
        def _(scalar):
            scalar.activation(out=fsc2[:], in_=epsv[:], func=AF.Copy).then_inc(s_fde.h, 1)

        @block.gpsimd
        def _(gpsimd):
            gpsimd.wait_ge(s_x1s.h, s_x1s.v)
            gpsimd.wait_ge(s_h2sA.h, s_h2sA.v)
            gpsimd.wait_ge(s_h2sB.h, s_h2sB.v)
            gpsimd.sem_inc(s_fde.h, 1)

        sD.close()
        sM.close()
        sE = contextlib.ExitStack()
        eE = sE.enter_context
        h2sb = eE(nc.sbuf_tensor("f_h2", [128, NCH * 1024], BF16))
        wgsb0 = eE(nc.sbuf_tensor("f_wg", [128, NCH * 128], BF16))
        wusb0 = eE(nc.sbuf_tensor("f_wu", [128, NCH * 128], BF16))
        wgsb = [wgsb0, wgsb0]
        wusb = [wusb0, wusb0]
        wdsb = eE(nc.sbuf_tensor("f_wd", [128, NNT * D], BF16))
        intr = eE(nc.sbuf_tensor("f_int", [128, NNT * 1024], BF16))
        silb = [eE(nc.sbuf_tensor(f"f_sil{i}", [128, 512], BF16)) for i in range(2)]
        y2ev = silb

        NGG = NNT * 4
        NDG = 64

        @block.sync
        def _(sync):
            sync.wait_ge(s_fde.h, FDE)
            for i in range(8):
                hf = i // 4
                rp = i % 4
                h2src = h2gA if hf == 0 else h2gB
                sync.wait_ge((s_agh2a if hf == 0 else s_agh2b).h, 1)
                if i >= 1:
                    sync.wait_ge(s_fg.h, NGG * i)
                for rr in range(2):
                    srcap = bass.AP(h2src.tensor, D * QB * (2 * rp + rr),
                                    [[QB, 128], [128 * QB, NCH], [1, QB]])
                    dstap = h2sb.ap().rearrange("p (j rt) -> p j rt", rt=1024)[:, :, 512 * rr:512 * (rr + 1)]
                    sync.dma_start(out=dstap, in_=srcap).then_inc(s_fh.h, 16)
                s_fh.v += 32
                for nt in range(NNT):
                    nw = NT_SIZES[nt]
                    widx = i * NNT + nt
                    if widx >= 1:
                        sync.wait_ge(s_fg.h, 4 * widx - 2)
                    sync.dma_start(out=wgsb[widx % 2][:, 0:NCH * nw],
                                   in_=wgbf[128 * nt:128 * (nt + 1), 0:NCH * nw]).then_inc(s_fwg.h, 16)
                    s_fwg.v += 16
                    if widx >= 1:
                        sync.wait_ge(s_fg.h, 4 * widx)
                    sync.dma_start(out=wusb[widx % 2][:, 0:NCH * nw],
                                   in_=wubf[128 * nt:128 * (nt + 1), 0:NCH * nw]).then_inc(s_fwu.h, 16)
                    s_fwu.v += 16
                    if i >= 1 and nt == 0:
                        sync.wait_ge(s_fy.h, NDG * i)
                    sync.dma_start(out=wdsb[0:nw, D * nt:D * (nt + 1)],
                                   in_=wdbf[128 * nt:128 * nt + nw, :]).then_inc(s_fwd.h, 16)
                    s_fwd.v += 16

        @block.tensor
        def _(tensor):
            tensor.wait_ge(s_fde.h, FDE)
            for i in range(8):
                tensor.wait_ge(s_fh.h, 32 * (i + 1))
                for nt in range(NNT):
                    nw = NT_SIZES[nt]
                    widx = i * NNT + nt
                    tensor.wait_ge(s_fwg.h, 16 * (widx + 1))
                    for mat in range(2):
                        if mat == 1:
                            tensor.wait_ge(s_fwu.h, 16 * (widx + 1))
                        wsb = (wgsb if mat == 0 else wusb)[widx % 2]
                        for rr in range(2):
                            bank = pf[2 * mat + rr]
                            if widx >= 1:
                                if mat == 0:
                                    tensor.wait_ge(s_fsil.h, 2 * widx - 2 + rr + 1)
                                else:
                                    tensor.wait_ge(s_fint.h, 2 * widx - 2 + rr + 1)
                            for j in range(NCH):
                                inst = tensor.matmul(bank[0:nw, 0:512],
                                                     wsb[:, nw * j:nw * j + nw],
                                                     h2sb[:, 1024 * j + 512 * rr:1024 * j + 512 * (rr + 1)],
                                                     start=(j == 0), stop=(j == NCH - 1))
                            inst.then_inc(s_fg.h, 1)
                            s_fg.v += 1
                tensor.wait_ge(s_fint.h, 2 * NNT * (i + 1))
                tensor.wait_ge(s_fwd.h, 16 * NNT * (i + 1))
                for ts in range(8):
                    for ec in range(8):
                        didx = NDG * i + 8 * ts + ec
                        if didx >= 2:
                            tensor.wait_ge(s_fye.h, didx - 1)
                        for nt in range(NNT):
                            nw = NT_SIZES[nt]
                            inst = tensor.matmul(pf[4 + didx % 2][:],
                                                 intr[0:nw, 1024 * nt + 128 * ts:1024 * nt + 128 * (ts + 1)],
                                                 wdsb[0:nw, D * nt + 512 * ec:D * nt + 512 * (ec + 1)],
                                                 start=(nt == 0), stop=(nt == NNT - 1))
                        inst.then_inc(s_fy.h, 1)
                        s_fy.v += 1

        @block.scalar
        def _(scalar):
            for i in range(8):
                if i >= 1:
                    scalar.wait_ge(s_fys[0].h, 16 * NDG * i // 2)
                    scalar.wait_ge(s_fys[1].h, 16 * NDG * i // 2)
                for nt in range(NNT):
                    widx = i * NNT + nt
                    nw = NT_SIZES[nt]
                    for rr in range(2):
                        sidx = 2 * widx + rr
                        scalar.wait_ge(s_fg.h, 4 * widx + rr + 1)
                        if sidx >= 2:
                            scalar.wait_ge(s_fint.h, sidx - 1)
                        scalar.activation(out=silb[sidx % 2][0:nw, :], in_=pf[rr][0:nw, 0:512],
                                          func=AF.Silu).then_inc(s_fsil.h, 1)
                        s_fsil.v += 1
                for ts in range(8):
                    for ec in range(8):
                        didx = NDG * i + 8 * ts + ec
                        scalar.wait_ge(s_fy.h, didx + 1)
                        if didx >= 2:
                            scalar.wait_ge(s_fys[didx % 2].h, 16 * (didx // 2))
                        scalar.activation(out=y2ev[didx % 2][:], in_=pf[4 + didx % 2][:],
                                          func=AF.Copy).then_inc(s_fye.h, 1)
                        s_fye.v += 1

        @block.vector
        def _(vector):
            vector.wait_ge(s_fde.h, FDE)
            for i in range(8):
                if i >= 1:
                    vector.wait_ge(s_fy.h, NDG * i)
                for nt in range(NNT):
                    widx = i * NNT + nt
                    nw = NT_SIZES[nt]
                    for rr in range(2):
                        sidx = 2 * widx + rr
                        vector.wait_ge(s_fsil.h, sidx + 1)
                        vector.wait_ge(s_fg.h, 4 * widx + 2 + rr + 1)
                        vector.tensor_tensor(out=intr[0:nw, 1024 * nt + 512 * rr:1024 * nt + 512 * (rr + 1)],
                                             in0=silb[sidx % 2][0:nw, :],
                                             in1=pf[2 + rr][0:nw, 0:512],
                                             op=ALU.mult).then_inc(s_fint.h, 1)
                        s_fint.v += 1

        @block.gpsimd
        def _(gpsimd):
            for i in range(8):
                hf = i // 4
                rp = i % 4
                ydst = y2dA if hf == 0 else y2dB
                for ts in range(8):
                    for ec in range(8):
                        didx = NDG * i + 8 * ts + ec
                        gpsimd.wait_ge(s_fye.h, didx + 1)
                        r_ = 2 * rp + ts // 4
                        tq = ts % 4
                        sp = s_fys[didx % 2]
                        gpsimd.dma_start(out=ydst[QB * r_ + 128 * tq:QB * r_ + 128 * (tq + 1),
                                                  512 * ec:512 * (ec + 1)],
                                         in_=y2ev[didx % 2][:]).then_inc(sp.h, 16)
                        sp.v += 16
                if i == 3:
                    gpsimd.wait_ge(s_fys[0].h, 16 * NDG * 2)
                    gpsimd.wait_ge(s_fys[1].h, 16 * NDG * 2)
                    gpsimd.collective_compute("ReduceScatter", ALU.add, RG,
                                              [y2dA.opt()], [y2rA.opt()]).then_inc(s_rsa2.h, 1)
                    s_rsa2.v += 1
            gpsimd.wait_ge(s_fys[0].h, 16 * NDG * 4)
            gpsimd.wait_ge(s_fys[1].h, 16 * NDG * 4)
            gpsimd.collective_compute("ReduceScatter", ALU.add, RG,
                                      [y2dB.opt()], [y2rB.opt()]).then_inc(s_rsb2.h, 1)
            s_rsb2.v += 1

        FEF = 3

        @block.vector
        def _(vector):
            vector.tensor_copy(out=fscr[:], in_=epsv[:]).then_inc(s_fef.h, 1)

        @block.scalar
        def _(scalar):
            scalar.activation(out=fsc2[:], in_=epsv[:], func=AF.Copy).then_inc(s_fef.h, 1)

        @block.gpsimd
        def _(gpsimd):
            gpsimd.wait_ge(s_fys[0].h, s_fys[0].v)
            gpsimd.wait_ge(s_fys[1].h, s_fys[1].v)
            gpsimd.sem_inc(s_fef.h, 1)

        sE.close()
        sF = contextlib.ExitStack()
        eF = sF.enter_context
        y5l2 = [eF(nc.sbuf_tensor(f"p5_y{i}", [128, D], BF16)) for i in range(2)]
        x5l2 = [eF(nc.sbuf_tensor(f"p5_x{i}", [128, D], F32)) for i in range(2)]
        ln5 = eF(nc.sbuf_tensor("p5_ln", [128, D], F32))
        dln = eF(nc.sbuf_tensor("p5_d", [128, D], F32))
        o5 = eF(nc.sbuf_tensor("p5_o", [128, D], F32))
        amx = eF(nc.sbuf_tensor("p5_amx", [128, 1], F32))
        msk = eF(nc.sbuf_tensor("p5_msk", [128, 1], F32))
        rst2 = eF(nc.sbuf_tensor("p5_rst2", [128, 1], F32))

        @block.sync
        def _(sync):
            sync.wait_ge(s_fef.h, FEF)
            for hf in range(2):
                sync.wait_ge((s_rsa2 if hf == 0 else s_rsb2).h, 1)
                ysrc = y2rA if hf == 0 else y2rB
                for q in range(4):
                    rowi = 4 * hf + q
                    if rowi >= 2:
                        sync.wait_ge(s_p5v.h, 3 * (rowi - 1))
                    sp = s_p5l[rowi % 2]
                    sync.dma_start(out=y5l2[rowi % 2][:], in_=ysrc[128 * q:128 * (q + 1), :]).then_inc(sp.h, 16)
                    sync.dma_start(out=x5l2[rowi % 2][:],
                                   in_=x1d[QB * hf + 128 * q:QB * hf + 128 * (q + 1), :]).then_inc(sp.h, 16)
                    sp.v += 32

        @block.vector
        def _(vector):
            vector.wait_ge(s_fef.h, FEF)
            for rowi in range(8):
                y5l = y5l2[rowi % 2]
                x5l = x5l2[rowi % 2]
                vector.wait_ge(s_p5l[rowi % 2].h, 32 * (rowi // 2 + 1))
                if rowi >= 1:
                    vector.wait_ge(s_p5r.h, rowi)
                for ch in range(8):
                    inst = vector.bn_stats(out=stat6[:, 6 * ch:6 * ch + 6], in_=y5l[:, 512 * ch:512 * (ch + 1)])
                vedge(vector, inst)
                vector.bn_aggr(out=stat2[:], in_=stat6[:]).then_inc(s_p5v.h, 1)
                vector.wait_ge(s_p5v.h, 3 * rowi + 1)
                vedge(vector, vector.tensor_scalar(out=msk[:], in0=stat2[:, 1:2], scalar1=65520.0,
                                                   scalar2=None, op0=ALU.is_lt))
                vector.wait_ge(s_p5r.h, rowi + 1)
                vedge(vector, vector.tensor_tensor(out=rst2[:], in0=msk[:], in1=lnt2[:], op=ALU.mult))
                vector.tensor_scalar(out=ln5[:], in0=y5l[:], scalar1=stat2[:, 0:1],
                                     scalar2=rst2[:], op0=ALU.subtract, op1=ALU.mult).then_inc(s_p5v.h, 1)
                vector.wait_ge(s_p5v.h, 3 * rowi + 2)
                if rowi >= 1:
                    vector.wait_ge(s_p5s.h, 16 * rowi)
                vector.tensor_tensor(out=o5[:], in0=x5l[:], in1=ln5[:], op=ALU.add).then_inc(s_p5v.h, 1)
                s_p5v.v += 3

        @block.scalar
        def _(scalar):
            for rowi in range(8):
                scalar.wait_ge(s_p5v.h, 3 * rowi + 1)
                sedge(scalar, scalar.activation(out=lnt1[:], in_=stat2[:, 1:2], func=AF.Ln, bias=epsv[:]))
                scalar.activation(out=lnt2[:], in_=lnt1[:], func=AF.Exp, scale=-0.5).then_inc(s_p5r.h, 1)
                s_p5r.v += 1

        @block.gpsimd
        def _(gpsimd):
            for hf in range(2):
                for q in range(4):
                    rowi = 4 * hf + q
                    gpsimd.wait_ge(s_p5v.h, 3 * rowi + 3)
                    gpsimd.dma_start(out=out[QB * hf + 128 * q:QB * hf + 128 * (q + 1), :],
                                     in_=o5[:]).then_inc(s_p5s.h, 16)
                    s_p5s.v += 16
            if dbg:
                srcs = {"h0": h0g, "qkv": qkvd.rearrange("a p t -> (a p) t"), "oa": oaslab,
                        "y1a": y1rA, "x1": x1d, "h2a": h2gA, "h2b": h2gB,
                        "y2a": y2rA, "y2b": y2rB}
                for name, ap_ in dbg.items():
                    gpsimd.dma_start(out=ap_, in_=srcs[name]).then_inc(s_p5s.h, 16)
                    s_p5s.v += 16
            gpsimd.wait_ge(s_p5s.h, s_p5s.v)

        sF.close()

    return nc



def stage_inputs(inputs):
    x = np.asarray(inputs["L0_attn_hidden_states"]).reshape(S, D)
    w_attn = np.asarray(inputs["L0_attn_w_attn"])
    b_attn = np.asarray(inputs["L0_attn_b_attn"])
    w_proj = np.asarray(inputs["L0_attn_w_proj"])
    b_proj = np.asarray(inputs["L0_attn_b_proj"])
    gqw = np.asarray(inputs["L0_ffn_gate_qweight"])
    gsc = np.asarray(inputs["L0_ffn_gate_scales"])
    gzr = np.asarray(inputs["L0_ffn_gate_zeros"])
    uqw = np.asarray(inputs["L0_ffn_up_qweight"])
    usc = np.asarray(inputs["L0_ffn_up_scales"])
    uzr = np.asarray(inputs["L0_ffn_up_zeros"])
    dqw = np.asarray(inputs["L0_ffn_down_qweight"])
    dsc_ = np.asarray(inputs["L0_ffn_down_scales"])
    dzr = np.asarray(inputs["L0_ffn_down_zeros"])

    bf = ml_dtypes.bfloat16
    p = np.arange(128)
    shifts = (4 * (p % 8)).astype(np.int32).reshape(128, 1)
    identf = np.eye(128, dtype=np.float32)
    onesb = np.ones((128, 128), dtype=bf)
    g_of_m = None

    in_maps = []
    for c in range(NC):
        d = {}
        d["xrows"] = np.ascontiguousarray(x[TPC * c:TPC * (c + 1), :])
        cols = []
        bcols = []
        for s in range(NSLOT):
            h = 8 * s + c
            for g0, base in ((0, 0), (1, D), (2, 2 * D)):
                cols.append(w_attn[:, base + HD * h:base + HD * (h + 1)])
                bcols.append(b_attn[base + HD * h:base + HD * (h + 1)])
        d["wqkv"] = np.ascontiguousarray(np.concatenate(cols, axis=1))
        d["bqkvT"] = np.ascontiguousarray(np.stack(bcols, axis=1))
        d["wprojc"] = np.ascontiguousarray(
            np.concatenate([w_proj[HD * (8 * s + c):HD * (8 * s + c + 1), :] for s in range(NSLOT)], axis=0))
        d["bproj"] = np.ascontiguousarray(b_proj.reshape(1, D))

        pt = np.zeros((NSLOT * 5 * 128, QB), dtype=bf)
        eb = np.zeros((128, NSLOT * 64), dtype=np.float32)
        jj = np.arange(QB)
        for s in range(NSLOT):
            sig = float(SLOPES[8 * s + c])
            for v in range(4):
                arg = sig * (128.0 * v + p[:, None] - jj[None, :])
                m = (jj[None, :] >= 128 * v + p[:, None])
                with np.errstate(over="ignore", under="ignore"):
                    tile = np.where(m, np.exp(np.minimum(arg, 0.0)), 0.0)
                pt[(5 * s + v) * 128:(5 * s + v + 1) * 128, :] = tile.astype(bf)
            arg = sig * (p[:, None] - jj[None, :] - 128.0)
            with np.errstate(under="ignore"):
                pt[(5 * s + 4) * 128:(5 * s + 5) * 128, :] = np.exp(arg).astype(bf)
            for iu in range(64):
                u = 3 - iu
                if u <= -1:
                    eb[:, 64 * s + iu] = np.float32(max(128.0 * sig * (u + 1), -1e30))
        d["patt"] = pt
        d["ebias"] = eb
        d["shifts"] = shifts
        d["identf"] = identf
        d["onesb"] = onesb

        nsl0 = NSL * c
        d["gq"] = np.ascontiguousarray(gqw[:, nsl0:nsl0 + NSL])
        d["gs"] = np.ascontiguousarray(gsc[:, nsl0:nsl0 + NSL])
        d["gz"] = np.ascontiguousarray(gzr[:, nsl0 // 8:(nsl0 + NSL) // 8])
        d["uq"] = np.ascontiguousarray(uqw[:, nsl0:nsl0 + NSL])
        d["us"] = np.ascontiguousarray(usc[:, nsl0:nsl0 + NSL])
        d["uz"] = np.ascontiguousarray(uzr[:, nsl0 // 8:(nsl0 + NSL) // 8])
        d["dq"] = np.ascontiguousarray(dqw[nsl0 // 8:(nsl0 + NSL) // 8, :])
        if g_of_m is None:
            pass
        gm = (nsl0 + np.arange(NSL)) // 128
        d["dsx"] = np.ascontiguousarray(dsc_[gm, :])
        d["dzx"] = np.ascontiguousarray(dzr[gm, :])
        in_maps.append(d)
    return in_maps


def kernel(**inputs):
    in_maps = stage_inputs(inputs)
    nc = build_nc(debug=DEBUG_DUMPS)
    res = run_bass_kernel_spmd(nc, in_maps, core_ids=list(range(NC)))
    outs = [np.asarray(res.results[c]["out"]) for c in range(NC)]
    full = np.concatenate(outs, axis=0).reshape(1, S, D).astype(np.float32)
    if DEBUG_DUMPS:
        kernel._dbg = res.results
    return full


# version 10
# speedup vs baseline: 1.1246x; 1.0466x over previous
import contextlib
import math

import numpy as np
import ml_dtypes

import concourse.bass as bass
import concourse.mybir as mybir
from concourse.bass_utils import run_bass_kernel_spmd

F32 = mybir.dt.float32
F16 = mybir.dt.float16
BF16 = mybir.dt.bfloat16
I32 = mybir.dt.int32
AF = mybir.ActivationFunctionType
ALU = mybir.AluOpType

S = 8192
D = 4096
NH = 32
HD = 128
INTER = 11008
NC = 8
TPC = S // NC
NSLOT = 4
NSL = INTER // NC
NQB = 16
QB = 512
NCH = D // 128
WIN_THRESH = 100.0

DEBUG_DUMPS = ()


def _alibi_slopes(n_heads):
    cp2 = 2 ** math.floor(math.log2(n_heads))
    base = 2 ** (-(2 ** (-(math.log2(cp2) - 3))))
    slopes = base ** np.arange(1, cp2 + 1, dtype=np.float64)
    if cp2 != n_heads:
        extra_base = 2 ** (-(2 ** (-(math.log2(2 * cp2) - 3))))
        n_rem = min(cp2, n_heads - cp2)
        extra = extra_base ** np.arange(1, 1 + 2 * n_rem, 2, dtype=np.float64)
        slopes = np.concatenate([slopes, extra])
    return slopes


SLOPES = _alibi_slopes(NH)

B_SLOT = []
for _s in range(NSLOT):
    sig_min = SLOPES[8 * _s + 7]
    B_SLOT.append(min(60, int(math.floor((127 + WIN_THRESH / sig_min) / 128.0))))

N_STEPS = [[4 + min(4 * b, B_SLOT[s]) for b in range(NQB)] for s in range(NSLOT)]
TOTAL_STEPS = sum(sum(r) for r in N_STEPS)

NT_SIZES = [128] * 10 + [96]
NNT = len(NT_SIZES)
NT_OFF = [sum(NT_SIZES[:i]) for i in range(NNT)]

STEP_U = []
BLOCK_RANGE = {}
_g = 0
for _s in range(NSLOT):
    for _b in range(NQB):
        n = N_STEPS[_s][_b]
        BLOCK_RANGE[(_s, _b)] = (_g, _g + n)
        for _u in range(-(n - 4), 4):
            STEP_U.append((_s, _b, _u))
        _g += n
assert _g == TOTAL_STEPS

DQ_UNITS = [("zprep", 0), ("zprep", 1)]
for _m in range(2):
    for _j in range(NCH):
        DQ_UNITS.append(("gu", _m, _j))
for _mt in range(NNT):
    for _ec in range(4):
        DQ_UNITS.append(("down", _mt, _ec))
N_DQ = len(DQ_UNITS)
DQL2_AT = []
_v2 = [0, 0]
for _u in range(N_DQ):
    _v2[_u % 2] += 32 if DQ_UNITS[_u][0] == "zprep" else 48
    DQL2_AT.append(_v2[_u % 2])
DQS2_AT = []
_v3 = [0, 0]
for _u in range(N_DQ):
    _v3[_u % 2] += 16 * (NNT if DQ_UNITS[_u][0] == "gu" else 1)
    DQS2_AT.append(_v3[_u % 2])
DQ_W_A = (0, 36)
DQ_W_B = (36, 51)
DQ_W_S = [(51, 71), (71, 91), (91, 101), (101, N_DQ)]


class Sem:
    __slots__ = ("h", "v")

    def __init__(self, h):
        self.h = h
        self.v = 0

    def inc(self, inst, n=1):
        inst.then_inc(self.h, n)
        self.v += n
        return self.v


def build_nc(debug=(), detect_races=True):
    nc = bass.Bass("TRN2", target_bir_lowering=False, detect_race_conditions=detect_races)

    xrows = nc.dram_tensor("xrows", [TPC, D], F32, kind="ExternalInput").ap()
    wqkv = nc.dram_tensor("wqkv", [D, NSLOT * 3 * HD], F32, kind="ExternalInput").ap()
    bqkvT = nc.dram_tensor("bqkvT", [HD, 12], F32, kind="ExternalInput").ap()
    wprojc = nc.dram_tensor("wprojc", [NSLOT * HD, D], F32, kind="ExternalInput").ap()
    bproj = nc.dram_tensor("bproj", [1, D], F32, kind="ExternalInput").ap()
    patt = nc.dram_tensor("patt", [NSLOT * 5 * 128, QB], BF16, kind="ExternalInput").ap()
    ebias = nc.dram_tensor("ebias", [128, NSLOT * 64], F32, kind="ExternalInput").ap()
    shifts_in = nc.dram_tensor("shifts", [128, 1], I32, kind="ExternalInput").ap()
    identf_in = nc.dram_tensor("identf", [128, 128], F32, kind="ExternalInput").ap()
    onesb_in = nc.dram_tensor("onesb", [128, 128], BF16, kind="ExternalInput").ap()
    gq = nc.dram_tensor("gq", [512, NSL], I32, kind="ExternalInput").ap()
    gs = nc.dram_tensor("gs", [32, NSL], F16, kind="ExternalInput").ap()
    gz = nc.dram_tensor("gz", [32, NSL // 8], I32, kind="ExternalInput").ap()
    uq = nc.dram_tensor("uq", [512, NSL], I32, kind="ExternalInput").ap()
    us = nc.dram_tensor("us", [32, NSL], F16, kind="ExternalInput").ap()
    uz = nc.dram_tensor("uz", [32, NSL // 8], I32, kind="ExternalInput").ap()
    dq = nc.dram_tensor("dq", [NSL // 8, D], I32, kind="ExternalInput").ap()
    dsx = nc.dram_tensor("dsx", [NSL, D], F16, kind="ExternalInput").ap()
    dzx = nc.dram_tensor("dzx", [NSL, D // 8], I32, kind="ExternalInput").ap()
    out = nc.dram_tensor("out", [TPC, D], F32, kind="ExternalOutput").ap()

    h0slab = nc.dram_tensor("h0slab", [D, TPC], BF16).ap()
    h0g = nc.dram_tensor("h0g", [NC * D, TPC], BF16).ap()
    qkvd = nc.dram_tensor("qkvd", [12, HD, S], BF16).ap()
    oaslab = nc.dram_tensor("oaslab", [NSLOT * HD, S], BF16).ap()
    y1dA = nc.dram_tensor("y1dA", [NC * QB, D], BF16).ap()
    y1dB = nc.dram_tensor("y1dB", [NC * QB, D], BF16).ap()
    y1rA = nc.dram_tensor("y1rA", [QB, D], BF16).ap()
    y1rB = nc.dram_tensor("y1rB", [QB, D], BF16).ap()
    x1d = nc.dram_tensor("x1d", [TPC, D], F32).ap()
    h2slabA = nc.dram_tensor("h2slabA", [D, QB], BF16).ap()
    h2slabB = nc.dram_tensor("h2slabB", [D, QB], BF16).ap()
    h2gA = nc.dram_tensor("h2gA", [NC * D, QB], BF16).ap()
    h2gB = nc.dram_tensor("h2gB", [NC * D, QB], BF16).ap()
    wgbf = nc.dram_tensor("wgbf", [NNT * 128, NCH * 128], BF16).ap()
    wubf = nc.dram_tensor("wubf", [NNT * 128, NCH * 128], BF16).ap()
    wdbf = nc.dram_tensor("wdbf", [NSL, D], BF16).ap()
    zpgd = nc.dram_tensor("zpgd", [32, NSL], F32).ap()
    zpud = nc.dram_tensor("zpud", [32, NSL], F32).ap()
    y2dA = nc.dram_tensor("y2dA", [NC * QB, D], BF16).ap()
    y2dB = nc.dram_tensor("y2dB", [NC * QB, D], BF16).ap()
    y2rA = nc.dram_tensor("y2rA", [QB, D], BF16).ap()
    y2rB = nc.dram_tensor("y2rB", [QB, D], BF16).ap()

    dbg = {}
    for name, src_shape, dt_ in (
        ("h0", [NC * D, TPC], BF16), ("qkv", [12 * HD, S], BF16),
        ("oa", [NSLOT * HD, S], BF16), ("y1a", [QB, D], BF16),
        ("x1", [TPC, D], F32), ("h2a", [NC * D, QB], BF16),
        ("h2b", [NC * D, QB], BF16), ("y2a", [QB, D], BF16),
        ("y2b", [QB, D], BF16),
    ):
        if name in debug or (name[:2] in ("h2", "y2") and name[:2] in debug) or \
           (name in ("y1a",) and "y1" in debug):
            dbg[name] = nc.dram_tensor("dbg_" + name, src_shape, dt_, kind="ExternalOutput").ap()

    RG = [list(range(NC))]

    ctx = contextlib.ExitStack()
    with ctx:
        e = ctx.enter_context

        shifts = e(nc.sbuf_tensor("s_shifts", [128, 1], I32))
        epsv = e(nc.sbuf_tensor("s_eps", [128, 1], F32))
        sbq = e(nc.sbuf_tensor("s_bqkv", [128, 12], F32))
        stat6 = e(nc.sbuf_tensor("s_stat6", [128, 8 * 6], F32))
        stat2 = e(nc.sbuf_tensor("s_stat2", [128, 2], F32))
        lnt1 = e(nc.sbuf_tensor("s_lnt1", [128, 1], F32))
        lnt2 = e(nc.sbuf_tensor("s_lnt2", [128, 1], F32))
        fscr = e(nc.sbuf_tensor("s_fscr", [128, 1], F32))
        fsc2 = e(nc.sbuf_tensor("s_fsc2", [128, 1], F32))
        sM = contextlib.ExitStack()
        eM = sM.enter_context
        identb = eM(nc.sbuf_tensor("s_identb", [128, 128], BF16))
        identf = eM(nc.sbuf_tensor("s_identf", [128, 128], F32))
        onesb = eM(nc.sbuf_tensor("s_onesb", [128, 128], BF16))
        sebias = eM(nc.sbuf_tensor("s_ebias", [128, NSLOT * 64], F32))

        pf = [e(nc.psum_tensor(f"pf{i}", [128, 512], F32)) for i in range(6)]
        pb = [e(nc.psum_tensor(f"pb{i}", [128, 1024], BF16)) for i in range(2)]

        def sem(name):
            return Sem(e(nc.semaphore(name)))

        s_init = sem("init")
        s_dql = [sem("dql0"), sem("dql1")]; s_dqv = sem("dqv")
        s_dqs = [sem("dqs0"), sem("dqs1")]
        s_p0l = [sem("p0l0"), sem("p0l1")]; s_p0v = sem("p0v"); s_rst = sem("rst")
        s_p0t = sem("p0t"); s_p0e = sem("p0e"); s_p0s = sem("p0s")
        s_ag0 = sem("ag0")
        s_p1w = [sem("p1w0"), sem("p1w1")]; s_p1wc = sem("p1wc")
        s_p1h = [sem("p1h0"), sem("p1h1"), sem("p1h2"), sem("p1h3")]
        s_p1j = sem("p1j"); s_p1e = sem("p1e")
        s_p1s = [sem("p1s0"), sem("p1s1")]
        s_patt = sem("patt")
        s_kf = [sem("skf0"), sem("skf1")]
        s_vf = [sem("svf0"), sem("svf1")]
        s_qf = [sem("sqf0"), sem("sqf1")]
        s_vt = sem("vt"); s_vte = sem("vte")
        s_sS = sem("sS"); s_sX = sem("sX"); s_sM = sem("sM"); s_sAV = sem("sAV")
        s_sR = sem("sR"); s_sE = sem("sE"); s_sON = sem("sON"); s_oas = sem("oas")
        s_bp = sem("sbp")
        s_p3l = [sem("p3l0"), sem("p3l1")]
        s_p3w = [sem("p3w0"), sem("p3w1")]; s_p3wc = sem("p3wc")
        s_p3m = sem("p3m"); s_p3e = sem("p3e")
        s_p3s = [sem("p3s0"), sem("p3s1")]
        s_rsa1 = sem("rsa1"); s_rsb1 = sem("rsb1")
        s_r5l = sem("r5l"); s_r5v = sem("r5v"); s_r5r = sem("r5r")
        s_h2t = sem("h2t"); s_h2e = sem("h2e")
        s_x1s = sem("x1s"); s_h2sA = sem("h2sA"); s_h2sB = sem("h2sB")
        s_agh2a = sem("agh2a"); s_agh2b = sem("agh2b")
        s_fh = sem("fh"); s_fwg = sem("fwg"); s_fwu = sem("fwu"); s_fwd = sem("fwd")
        s_fg = sem("fg"); s_fsil = sem("fsil"); s_fint = sem("fint")
        s_fy = sem("fy"); s_fye = sem("fye")
        s_fys = [sem("fys0"), sem("fys1")]
        s_rsa2 = sem("rsa2"); s_rsb2 = sem("rsb2")
        s_p5l = [sem("p5l0"), sem("p5l1")]
        s_p5v = sem("p5v"); s_p5r = sem("p5r"); s_p5s = sem("p5s")
        s_dv = sem("dvself"); s_sc = sem("scself")
        s_fab = sem("fab"); s_fbc = sem("fbc"); s_fcd = sem("fcd")
        s_fde = sem("fde"); s_fef = sem("fef")

        def vedge(vector, inst):
            inst.then_inc(s_dv.h, 1)
            s_dv.v += 1
            vector.wait_ge(s_dv.h, s_dv.v)

        def sedge(scalar, inst):
            inst.then_inc(s_sc.h, 1)
            s_sc.v += 1
            scalar.wait_ge(s_sc.h, s_sc.v)

        block = e(nc.Block())

        @block.sync
        def _(sync):
            sync.dma_start(out=shifts[:], in_=shifts_in).then_inc(s_init.h, 16)
            sync.dma_start(out=identf[:], in_=identf_in).then_inc(s_init.h, 16)
            sync.dma_start(out=onesb[:], in_=onesb_in).then_inc(s_init.h, 16)
            sync.dma_start(out=sebias[:], in_=ebias).then_inc(s_init.h, 16)
            sync.dma_start(out=sbq[:], in_=bqkvT).then_inc(s_init.h, 16)
            s_init.v += 80

        @block.vector
        def _(vector):
            vector.wait_ge(s_init.h, 80)
            vector.memset(epsv[:], 1e-5)
            vector.tensor_copy(out=identb[:], in_=identf[:]).then_inc(s_init.h, 1)
            s_init.v += 1
        INIT_DONE = s_init.v

        sDQ = contextlib.ExitStack()
        eQ = sDQ.enter_context
        dqr = [eQ(nc.sbuf_tensor(f"dq_qr{i}", [128, NSL], I32)) for i in range(2)]
        dsc = [eQ(nc.sbuf_tensor(f"dq_sc{i}", [128, NSL], F16)) for i in range(2)]
        dzp = [eQ(nc.sbuf_tensor(f"dq_zp{i}", [128, NSL], F32)) for i in range(2)]
        dnib = eQ(nc.sbuf_tensor("dq_nib", [128, NSL], I32))
        dt1 = eQ(nc.sbuf_tensor("dq_t1", [128, NSL], F32))
        dwb = [eQ(nc.sbuf_tensor(f"dq_wb{i}", [128, NSL], BF16)) for i in range(2)]
        dzrow = [eQ(nc.sbuf_tensor(f"dq_zrow{i}", [128, 128], I32)) for i in range(2)]
        dzaux = eQ(nc.sbuf_tensor("dq_zaux", [128, 1024], I32))
        zpz = eQ(nc.sbuf_tensor("dq_zpz", [32, NSL // 8], I32))
        zps = eQ(nc.sbuf_tensor("dq_zps", [32, NSL], F16))

        gu_src = [(gq, gs, zpgd, wgbf), (uq, us, zpud, wubf)]

        def dq_loads(gpsimd, lo, hi):
            for u in range(lo, hi):
                kind = DQ_UNITS[u]
                sl_ = s_dql[u % 2]
                if kind[0] == "zprep":
                    m_ = kind[1]
                    if u == 1:
                        gpsimd.wait_ge(s_dqv.h, 1)
                    zsrc = gz if m_ == 0 else uz
                    ssrc = gs if m_ == 0 else us
                    gpsimd.dma_start(out=zpz[:], in_=zsrc).then_inc(sl_.h, 16)
                    gpsimd.dma_start(out=zps[:], in_=ssrc).then_inc(sl_.h, 16)
                    sl_.v += 32
                elif kind[0] == "gu":
                    m_, j_ = kind[1], kind[2]
                    qsrc, ssrc, zpd_, _w = gu_src[m_]
                    if u >= 2:
                        gpsimd.wait_ge(s_dqv.h, u - 1)
                    gpsimd.wait_ge(s_dqs[m_].h, 16)
                    rep = bass.AP(qsrc.tensor, 16 * j_ * NSL, [[NSL, 16], [0, 8], [1, NSL]])
                    gpsimd.dma_start(out=dqr[u % 2][:, 0:NSL], in_=rep).then_inc(sl_.h, 16)
                    srep = bass.AP(ssrc.tensor, j_ * NSL, [[0, 128], [1, NSL]])
                    gpsimd.dma_start(out=dsc[u % 2][:, 0:NSL], in_=srep).then_inc(sl_.h, 16)
                    zrep = bass.AP(zpd_.tensor, j_ * NSL, [[0, 128], [1, NSL]])
                    gpsimd.dma_start(out=dzp[u % 2][:, 0:NSL], in_=zrep).then_inc(sl_.h, 16)
                    sl_.v += 48
                else:
                    _, mt_, ec_ = kind
                    nw = NT_SIZES[mt_]
                    if u >= 2:
                        gpsimd.wait_ge(s_dqv.h, u - 1)
                    rep = bass.AP(dq.tensor, 16 * mt_ * D + 1024 * ec_, [[D, nw // 8], [0, 8], [1, 1024]])
                    gpsimd.dma_start(out=dqr[u % 2][0:nw, 0:1024], in_=rep).then_inc(sl_.h, 16)
                    gpsimd.dma_start(out=dsc[u % 2][0:nw, 0:1024],
                                     in_=dsx[128 * mt_:128 * mt_ + nw, 1024 * ec_:1024 * (ec_ + 1)]).then_inc(sl_.h, 16)
                    gpsimd.dma_start(out=dzrow[u % 2][0:nw, :],
                                     in_=dzx[128 * mt_:128 * mt_ + nw, 128 * ec_:128 * (ec_ + 1)]).then_inc(sl_.h, 16)
                    sl_.v += 48

        def dq_vec(vector, lo, hi):
            for u in range(lo, hi):
                kind = DQ_UNITS[u]
                vector.wait_ge(s_dql[u % 2].h, DQL2_AT[u])
                if u == 1:
                    vector.wait_ge(s_dqs[0].h, 16)
                if u == 2:
                    vector.wait_ge(s_dqs[1].h, 16)
                if u >= 2:
                    vector.wait_ge(s_dqs[u % 2].h, DQS2_AT[u - 2])
                if kind[0] == "zprep":
                    zpe_ = dnib[0:32, 0:NSL]
                    zpf_ = dt1[0:32, 0:NSL]
                    zpg_ = dt1[32:64, 0:NSL]
                    for jn in range(8):
                        inst = vector.tensor_scalar(out=zpe_[:, jn::8], in0=zpz[:], scalar1=4 * jn,
                                                    scalar2=15, op0=ALU.logical_shift_right, op1=ALU.bitwise_and)
                    vedge(vector, inst)
                    vedge(vector, vector.tensor_scalar(out=zpf_[:], in0=zpe_[:], scalar1=1.0, scalar2=None, op0=ALU.add))
                    vector.tensor_tensor(out=zpg_[:], in0=zpf_[:], in1=zps[:], op=ALU.mult).then_inc(s_dqv.h, 1)
                elif kind[0] == "gu":
                    u2 = u % 2
                    vedge(vector, vector.tensor_scalar(out=dnib[:, 0:NSL], in0=dqr[u2][:, 0:NSL], scalar1=shifts[:],
                                                       scalar2=15, op0=ALU.logical_shift_right, op1=ALU.bitwise_and))
                    vedge(vector, vector.tensor_tensor(out=dt1[:, 0:NSL], in0=dnib[:, 0:NSL],
                                                       in1=dsc[u2][:, 0:NSL], op=ALU.mult))
                    vector.tensor_tensor(out=dwb[u2][:, 0:NSL], in0=dt1[:, 0:NSL], in1=dzp[u2][:, 0:NSL],
                                         op=ALU.subtract).then_inc(s_dqv.h, 1)
                else:
                    _, mt_, ec_ = kind
                    u2 = u % 2
                    for jn in range(8):
                        inst = vector.tensor_scalar(out=dzaux[:, jn::8], in0=dzrow[u % 2][:], scalar1=4 * jn,
                                                    scalar2=15, op0=ALU.logical_shift_right, op1=ALU.bitwise_and)
                    vedge(vector, inst)
                    vedge(vector, vector.tensor_scalar(out=dt1[:, 0:1024], in0=dzaux[:], scalar1=1.0,
                                                       scalar2=None, op0=ALU.add))
                    vedge(vector, vector.tensor_tensor(out=dzp[u2][:, 0:1024], in0=dt1[:, 0:1024],
                                                       in1=dsc[u2][:, 0:1024], op=ALU.mult))
                    vedge(vector, vector.tensor_scalar(out=dnib[:, 0:1024], in0=dqr[u2][:, 0:1024], scalar1=shifts[:],
                                                       scalar2=15, op0=ALU.logical_shift_right, op1=ALU.bitwise_and))
                    vedge(vector, vector.tensor_tensor(out=dt1[:, 0:1024], in0=dnib[:, 0:1024],
                                                       in1=dsc[u2][:, 0:1024], op=ALU.mult))
                    vector.tensor_tensor(out=dwb[u2][:, 0:1024], in0=dt1[:, 0:1024], in1=dzp[u2][:, 0:1024],
                                         op=ALU.subtract).then_inc(s_dqv.h, 1)
                s_dqv.v += 1

        def dq_stores(gpsimd, lo, hi):
            for u in range(lo, hi):
                kind = DQ_UNITS[u]
                ss_ = s_dqs[u % 2]
                gpsimd.wait_ge(s_dqv.h, u + 1)
                if kind[0] == "zprep":
                    dst = zpgd if kind[1] == 0 else zpud
                    gpsimd.dma_start(out=dst, in_=dt1[32:64, 0:NSL]).then_inc(ss_.h, 16)
                elif kind[0] == "gu":
                    m_, j_ = kind[1], kind[2]
                    wdst = gu_src[m_][3]
                    for nt in range(NNT):
                        nw = NT_SIZES[nt]
                        gpsimd.dma_start(out=wdst[128 * nt:128 * (nt + 1), j_ * nw:(j_ + 1) * nw],
                                         in_=dwb[u % 2][:, NT_OFF[nt]:NT_OFF[nt] + nw]).then_inc(ss_.h, 16)
                        ss_.v += 16
                    ss_.v -= 16
                else:
                    _, mt_, ec_ = kind
                    nw = NT_SIZES[mt_]
                    gpsimd.dma_start(out=wdbf[128 * mt_:128 * mt_ + nw, 1024 * ec_:1024 * (ec_ + 1)],
                                     in_=dwb[u % 2][0:nw, 0:1024]).then_inc(ss_.h, 16)
                ss_.v += 16

        dq_ptr = [0]

        def dq_chunk(gpsimd, lo, hi, flush=False):
            for u in range(lo, hi):
                while dq_ptr[0] <= u - 2:
                    dq_stores(gpsimd, dq_ptr[0], dq_ptr[0] + 1)
                    dq_ptr[0] += 1
                dq_loads(gpsimd, u, u + 1)
            if flush:
                while dq_ptr[0] < hi:
                    dq_stores(gpsimd, dq_ptr[0], dq_ptr[0] + 1)
                    dq_ptr[0] += 1

        sA = contextlib.ExitStack()
        eA = sA.enter_context
        x0t = [eA(nc.sbuf_tensor(f"p0_x{i}", [128, D], F32)) for i in range(2)]
        h0t = eA(nc.sbuf_tensor("p0_h0", [128, D], BF16))
        h0stg = eA(nc.sbuf_tensor("p0_stg", [128, D], BF16))

        NT0 = TPC // 128

        @block.sync
        def _(sync):
            for i in range(NT0):
                if i >= 2:
                    sync.wait_ge(s_p0v.h, 2 * (i - 1))
                sp = s_p0l[i % 2]
                sync.dma_start(out=x0t[i % 2][:], in_=xrows[128 * i:128 * (i + 1), :]).then_inc(sp.h, 16)
                sp.v += 16

        @block.vector
        def _(vector):
            vector.wait_ge(s_init.h, INIT_DONE)
            for i in range(NT0):
                vector.wait_ge(s_p0l[i % 2].h, 16 * (i // 2 + 1))
                if i >= 1:
                    vector.wait_ge(s_rst.h, i)
                xt = x0t[i % 2]
                for ch in range(8):
                    inst = vector.bn_stats(out=stat6[:, 6 * ch:6 * ch + 6], in_=xt[:, 512 * ch:512 * (ch + 1)])
                vedge(vector, inst)
                vector.bn_aggr(out=stat2[:], in_=stat6[:]).then_inc(s_p0v.h, 1)
                s_p0v.v += 1
                vector.wait_ge(s_rst.h, i + 1)
                if i >= 1:
                    vector.wait_ge(s_p0t.h, 4 * i)
                vector.tensor_scalar(out=h0t[:], in0=xt[:], scalar1=stat2[:, 0:1],
                                     scalar2=lnt2[:], op0=ALU.subtract, op1=ALU.mult).then_inc(s_p0v.h, 1)
                s_p0v.v += 1
                for g in range(4):
                    vector.wait_ge(s_p0t.h, 4 * i + g + 1)
                    if g == 0 and i >= 1:
                        vector.wait_ge(s_p0s.h, 16 * i)
                    vector.tensor_copy(out=h0stg[:, 1024 * g:1024 * (g + 1)],
                                       in_=pb[g % 2][:]).then_inc(s_p0e.h, 1)
                    s_p0e.v += 1

        @block.scalar
        def _(scalar):
            for i in range(NT0):
                scalar.wait_ge(s_p0v.h, 2 * i + 1)
                sedge(scalar, scalar.activation(out=lnt1[:], in_=stat2[:, 1:2], func=AF.Ln, bias=epsv[:]))
                scalar.activation(out=lnt2[:], in_=lnt1[:], func=AF.Exp, scale=-0.5).then_inc(s_rst.h, 1)
                s_rst.v += 1

        @block.tensor
        def _(tensor):
            tensor.wait_ge(s_init.h, INIT_DONE)
            for i in range(NT0):
                tensor.wait_ge(s_p0v.h, 2 * i + 2)
                for g in range(4):
                    gg = 4 * i + g
                    if gg >= 2:
                        tensor.wait_ge(s_p0e.h, gg - 1)
                    for k in range(8):
                        ch = 8 * g + k
                        inst = tensor.transpose(pb[g % 2][:, 128 * k:128 * (k + 1)],
                                                h0t[:, 128 * ch:128 * (ch + 1)], identb[:])
                    inst.then_inc(s_p0t.h, 1)
                    s_p0t.v += 1

        @block.gpsimd
        def _(gpsimd):
            for i in range(NT0):
                gpsimd.wait_ge(s_p0e.h, 4 * (i + 1))
                dstap = bass.AP(h0slab.tensor, 128 * i, [[TPC, 128], [128 * TPC, 32], [1, 128]])
                gpsimd.dma_start(out=dstap,
                                 in_=h0stg.ap().rearrange("p (j t) -> p j t", t=128)).then_inc(s_p0s.h, 16)
                s_p0s.v += 16
            gpsimd.wait_ge(s_p0s.h, s_p0s.v)
            gpsimd.collective_compute("AllGather", ALU.bypass, RG, [h0slab.opt()], [h0g.opt()]).then_inc(s_ag0.h, 1)
            s_ag0.v += 1

        @block.gpsimd
        def _(gpsimd):
            gpsimd.wait_ge(s_init.h, 80)
            dq_chunk(gpsimd, *DQ_W_A)

        @block.vector
        def _(vector):
            dq_vec(vector, *DQ_W_A)

        FAB = 3

        @block.vector
        def _(vector):
            vector.tensor_copy(out=fscr[:], in_=epsv[:]).then_inc(s_fab.h, 1)

        @block.scalar
        def _(scalar):
            scalar.activation(out=fsc2[:], in_=epsv[:], func=AF.Copy).then_inc(s_fab.h, 1)

        @block.gpsimd
        def _(gpsimd):
            gpsimd.wait_ge(s_p0s.h, s_p0s.v)
            gpsimd.wait_ge(s_dqs[0].h, s_dqs[0].v)
            gpsimd.wait_ge(s_dqs[1].h, s_dqs[1].v)
            gpsimd.sem_inc(s_fab.h, 1)

        sA.close()
        sB = contextlib.ExitStack()
        eB = sB.enter_context
        wqsb = eB(nc.sbuf_tensor("p1_w", [128, NCH * 768], BF16))
        wqld = [eB(nc.sbuf_tensor(f"p1_wl{i}", [128, 768], F32)) for i in range(2)]
        h0rhs = [eB(nc.sbuf_tensor(f"p1_h{i}", [128, 512], BF16)) for i in range(4)]
        qkev = [eB(nc.sbuf_tensor(f"p1_e{i}", [128, 6 * 512], BF16)) for i in range(2)]

        @block.sync
        def _(sync):
            sync.wait_ge(s_fab.h, FAB)
            for sw in range(2):
                for j in range(NCH):
                    cidx = sw * NCH + j
                    if cidx >= 2:
                        sync.wait_ge(s_p1wc.h, cidx - 1)
                    sp = s_p1w[cidx % 2]
                    sync.dma_start(out=wqld[cidx % 2][:],
                                   in_=wqkv[128 * j:128 * (j + 1), 768 * sw:768 * (sw + 1)]).then_inc(sp.h, 16)
                    sp.v += 16
                if sw == 0:
                    sync.wait_ge(s_ag0.h, 1)
                for m in range(NQB):
                    for j in range(NCH):
                        idx = (sw * NQB + m) * NCH + j
                        if idx >= 4:
                            sync.wait_ge(s_p1j.h, idx - 3)
                        rk = m // 2
                        tl = 512 * (m % 2)
                        src = h0g[D * rk + 128 * j:D * rk + 128 * (j + 1), tl:tl + 512]
                        sp = s_p1h[idx % 4]
                        sync.dma_start(out=h0rhs[idx % 4][:], in_=src).then_inc(sp.h, 16)
                        sp.v += 16

        @block.vector
        def _(vector):
            vector.wait_ge(s_fab.h, FAB)
            for sw in range(2):
                if sw == 1:
                    vector.wait_ge(s_p1j.h, NCH * NQB)
                for j in range(NCH):
                    cidx = sw * NCH + j
                    vector.wait_ge(s_p1w[cidx % 2].h, 16 * (cidx // 2 + 1))
                    vector.tensor_copy(out=wqsb[:, 768 * j:768 * (j + 1)],
                                       in_=wqld[cidx % 2][:]).then_inc(s_p1wc.h, 1)
                    s_p1wc.v += 1
                for m in range(NQB):
                    gidx = sw * NQB + m
                    vector.wait_ge(s_p1j.h, NCH * (gidx + 1))
                    if gidx >= 2:
                        vector.wait_ge(s_p1s[gidx % 2].h, 96 * (gidx // 2))
                    for q6 in range(3, 6):
                        sq = 3 * (2 * sw + q6 // 3) + (q6 % 3)
                        vector.tensor_scalar(out=qkev[gidx % 2][:, 512 * q6:512 * (q6 + 1)],
                                             in0=pf[q6][:], scalar1=sbq[:, sq:sq + 1],
                                             scalar2=None, op0=ALU.add).then_inc(s_p1e.h, 1)
                        s_p1e.v += 1

        @block.tensor
        def _(tensor):
            tensor.wait_ge(s_fab.h, FAB)
            for sw in range(2):
                tensor.wait_ge(s_p1wc.h, NCH * (sw + 1))
                for m in range(NQB):
                    gidx = sw * NQB + m
                    if gidx >= 1:
                        tensor.wait_ge(s_p1e.h, 6 * gidx)
                    for j in range(NCH):
                        idx = gidx * NCH + j
                        tensor.wait_ge(s_p1h[idx % 4].h, 16 * (idx // 4 + 1))
                        for q6 in range(6):
                            inst = tensor.matmul(pf[q6][:],
                                                 wqsb[:, 768 * j + 128 * q6:768 * j + 128 * (q6 + 1)],
                                                 h0rhs[idx % 4][:], start=(j == 0), stop=(j == NCH - 1))
                        inst.then_inc(s_p1j.h, 1)
                        s_p1j.v += 1

        @block.scalar
        def _(scalar):
            for sw in range(2):
                for m in range(NQB):
                    gidx = sw * NQB + m
                    scalar.wait_ge(s_p1j.h, NCH * (gidx + 1))
                    if gidx >= 2:
                        scalar.wait_ge(s_p1s[gidx % 2].h, 96 * (gidx // 2))
                    for q6 in range(3):
                        sq = 3 * (2 * sw + q6 // 3) + (q6 % 3)
                        scalar.activation(out=qkev[gidx % 2][:, 512 * q6:512 * (q6 + 1)], in_=pf[q6][:],
                                          func=AF.Identity, bias=sbq[:, sq:sq + 1],
                                          scale=1.0).then_inc(s_p1e.h, 1)
                        s_p1e.v += 1

        @block.gpsimd
        def _(gpsimd):
            for sw in range(2):
                for m in range(NQB):
                    gidx = sw * NQB + m
                    gpsimd.wait_ge(s_p1e.h, 6 * (gidx + 1))
                    sp = s_p1s[gidx % 2]
                    for q6 in range(6):
                        sq = 3 * (2 * sw + q6 // 3) + (q6 % 3)
                        gpsimd.dma_start(out=qkvd[sq, :, 512 * m:512 * (m + 1)],
                                         in_=qkev[gidx % 2][:, 512 * q6:512 * (q6 + 1)]).then_inc(sp.h, 16)
                        sp.v += 16

        @block.gpsimd
        def _(gpsimd):
            dq_chunk(gpsimd, *DQ_W_B)

        @block.vector
        def _(vector):
            dq_vec(vector, *DQ_W_B)

        FBC = 3

        @block.vector
        def _(vector):
            vector.tensor_copy(out=fscr[:], in_=epsv[:]).then_inc(s_fbc.h, 1)

        @block.scalar
        def _(scalar):
            scalar.activation(out=fsc2[:], in_=epsv[:], func=AF.Copy).then_inc(s_fbc.h, 1)

        @block.gpsimd
        def _(gpsimd):
            gpsimd.wait_ge(s_p1s[0].h, s_p1s[0].v)
            gpsimd.wait_ge(s_p1s[1].h, s_p1s[1].v)
            gpsimd.wait_ge(s_dqs[0].h, s_dqs[0].v)
            gpsimd.wait_ge(s_dqs[1].h, s_dqs[1].v)
            gpsimd.sem_inc(s_fbc.h, 1)

        sB.close()
        sC = contextlib.ExitStack()
        eC = sC.enter_context
        spatt = eC(nc.sbuf_tensor("p2_patt", [128, 20 * 512], BF16))
        kf2 = [eC(nc.sbuf_tensor(f"p2_k{i}", [128, S], BF16)) for i in range(2)]
        vf2 = [eC(nc.sbuf_tensor(f"p2_v{i}", [128, S], BF16)) for i in range(2)]
        qf2 = [eC(nc.sbuf_tensor(f"p2_q{i}", [128, S], BF16)) for i in range(2)]
        vt = eC(nc.sbuf_tensor("p2_vt", [128, S], BF16))
        ptx = [eC(nc.sbuf_tensor(f"p2_px{i}", [128, 512], BF16)) for i in range(4)]
        ptm = [eC(nc.sbuf_tensor(f"p2_pm{i}", [128, 512], BF16)) for i in range(4)]
        osb = eC(nc.sbuf_tensor("p2_osb", [128, 512], BF16))
        rsb = eC(nc.sbuf_tensor("p2_rsb", [128, 512], F32))
        onb = eC(nc.sbuf_tensor("p2_onb", [128, 512], BF16))

        @block.sync
        def _(sync):
            sync.wait_ge(s_fbc.h, FBC)
            sync.dma_start(out=spatt[:],
                           in_=patt.rearrange("(v p) q -> p v q", p=128)).then_inc(s_patt.h, 16)
            s_patt.v += 16
            for s_ in range(NSLOT):
                if s_ >= 2:
                    gend = BLOCK_RANGE[(s_ - 2, NQB - 1)][1]
                    sync.wait_ge(s_sAV.h, gend)
                    sync.wait_ge(s_vt.h, 8 * (s_ - 1))
                sync.dma_start(out=kf2[s_ % 2][:], in_=qkvd[3 * s_ + 1, :, :]).then_inc(s_kf[s_ % 2].h, 16)
                sync.dma_start(out=vf2[s_ % 2][:], in_=qkvd[3 * s_ + 2, :, :]).then_inc(s_vf[s_ % 2].h, 16)
                sync.dma_start(out=qf2[s_ % 2][:], in_=qkvd[3 * s_ + 0, :, :]).then_inc(s_qf[s_ % 2].h, 16)
                s_kf[s_ % 2].v += 16; s_vf[s_ % 2].v += 16; s_qf[s_ % 2].v += 16

        for s_ in range(NSLOT):
            def make_tensor(s_=s_):
                def emit(tensor):
                    kf = kf2[s_ % 2]
                    vf = vf2[s_ % 2]
                    qf = qf2[s_ % 2]
                    if s_ == 0:
                        tensor.wait_ge(s_fbc.h, FBC)
                    tensor.wait_ge(s_vf[s_ % 2].h, 16 * (s_ // 2 + 1))
                    for gvt in range(8):
                        gg = 8 * s_ + gvt
                        if gg >= 2:
                            tensor.wait_ge(s_vte.h, gg - 1)
                        for k in range(8):
                            ch = 8 * gvt + k
                            inst = tensor.transpose(pb[gvt % 2][:, 128 * k:128 * (k + 1)],
                                                    vf[:, 128 * ch:128 * (ch + 1)], identb[:])
                        inst.then_inc(s_vt.h, 1)
                        s_vt.v += 1
                    tensor.wait_ge(s_kf[s_ % 2].h, 16 * (s_ // 2 + 1))
                    tensor.wait_ge(s_qf[s_ % 2].h, 16 * (s_ // 2 + 1))
                    tensor.wait_ge(s_patt.h, 16)
                    tensor.wait_ge(s_vte.h, 8 * (s_ + 1))
                    LA = 3
                    gs0 = BLOCK_RANGE[(s_, 0)][0]
                    gs1 = BLOCK_RANGE[(s_, NQB - 1)][1]

                    def emit_S(gi):
                        sx, bx, ux = STEP_U[gi]
                        k0x = QB * bx + 128 * ux
                        sbank = [pf[0], pf[1], pf[4], pf[5]][gi % 4]
                        if gi >= 4:
                            tensor.wait_ge(s_sX.h, gi - 3)
                        inst = tensor.matmul(sbank[:], kf[:, k0x:k0x + 128],
                                             qf[:, QB * bx:QB * (bx + 1)], start=True, stop=True)
                        inst.then_inc(s_sS.h, 1)
                        s_sS.v += 1

                    for gpre in range(gs0, min(gs0 + LA, gs1)):
                        emit_S(gpre)
                    for b_ in range(NQB):
                        g0, g1 = BLOCK_RANGE[(s_, b_)]
                        blkidx = s_ * NQB + b_
                        if blkidx >= 1:
                            tensor.wait_ge(s_sR.h, blkidx)
                            tensor.wait_ge(s_sE.h, blkidx)
                        for i, gi in enumerate(range(g0, g1)):
                            if gi + LA < gs1:
                                emit_S(gi + LA)
                            u_ = STEP_U[gi][2]
                            k0 = QB * b_ + 128 * u_
                            tensor.wait_ge(s_sM.h, gi + 1)
                            tensor.matmul(pf[2][:], onesb[:], ptm[gi % 4][:],
                                          start=(i == 0), stop=(i == g1 - g0 - 1))
                            ct = k0 // 128
                            inst = tensor.matmul(pf[3][:], vt[:, 128 * ct:128 * (ct + 1)],
                                                 ptm[gi % 4][:],
                                                 start=(i == 0), stop=(i == g1 - g0 - 1))
                            inst.then_inc(s_sAV.h, 1)
                            s_sAV.v += 1
                return emit
            block.tensor(make_tensor())

            def make_scalar(s_=s_):
                def emit(scalar):
                    for b_ in range(NQB):
                        g0, g1 = BLOCK_RANGE[(s_, b_)]
                        blkidx = s_ * NQB + b_
                        for i, gi in enumerate(range(g0, g1)):
                            u_ = STEP_U[gi][2]
                            scalar.wait_ge(s_sS.h, gi + 1)
                            if gi >= 4:
                                scalar.wait_ge(s_sM.h, gi - 3)
                            iu = 3 - u_
                            sbank = [pf[0], pf[1], pf[4], pf[5]][gi % 4]
                            scalar.activation(out=ptx[gi % 4][:], in_=sbank[:], func=AF.Exp,
                                              bias=sebias[:, 64 * s_ + iu:64 * s_ + iu + 1],
                                              scale=1.0 / 128).then_inc(s_sX.h, 1)
                            s_sX.v += 1
                        scalar.wait_ge(s_sAV.h, g1)
                        if blkidx >= 1:
                            scalar.wait_ge(s_sON.h, blkidx)
                        scalar.activation(out=osb[:], in_=pf[3][:], func=AF.Copy).then_inc(s_sE.h, 1)
                        s_sE.v += 1
                return emit
            block.scalar(make_scalar())

            def make_vector(s_=s_):
                def emit(vector):
                    if s_ == 0:
                        vector.wait_ge(s_patt.h, 16)
                    for gvt in range(8):
                        vector.wait_ge(s_vt.h, 8 * s_ + gvt + 1)
                        vector.tensor_copy(out=vt[:, 1024 * gvt:1024 * (gvt + 1)],
                                           in_=pb[gvt % 2][:]).then_inc(s_vte.h, 1)
                        s_vte.v += 1
                    for b_ in range(NQB):
                        g0, g1 = BLOCK_RANGE[(s_, b_)]
                        blkidx = s_ * NQB + b_
                        for i, gi in enumerate(range(g0, g1)):
                            u_ = STEP_U[gi][2]
                            vector.wait_ge(s_sX.h, gi + 1)
                            if gi >= 4:
                                vector.wait_ge(s_sAV.h, gi - 3)
                            v_ = u_ if 0 <= u_ else 4
                            pslice = spatt[:, 512 * (5 * s_ + v_):512 * (5 * s_ + v_ + 1)]
                            vector.tensor_tensor(out=ptm[gi % 4][:], in0=ptx[gi % 4][:],
                                                 in1=pslice, op=ALU.mult).then_inc(s_sM.h, 1)
                            s_sM.v += 1
                        vector.wait_ge(s_sAV.h, g1)
                        vector.reciprocal(out=rsb[:], in_=pf[2][:]).then_inc(s_sR.h, 1)
                        s_sR.v += 1
                        vector.wait_ge(s_sR.h, s_sR.v)
                        vector.wait_ge(s_sE.h, blkidx + 1)
                        if blkidx >= 1:
                            vector.wait_ge(s_oas.h, 16 * blkidx)
                        vector.tensor_tensor(out=onb[:], in0=osb[:], in1=rsb[:],
                                             op=ALU.mult).then_inc(s_sON.h, 1)
                        s_sON.v += 1
                return emit
            block.vector(make_vector())

            def make_gp(s_=s_):
                def emit(gpsimd):
                    for b_ in range(NQB):
                        blkidx = s_ * NQB + b_
                        gpsimd.wait_ge(s_sON.h, blkidx + 1)
                        gpsimd.dma_start(out=oaslab[128 * s_:128 * (s_ + 1), QB * b_:QB * (b_ + 1)],
                                         in_=onb[:]).then_inc(s_oas.h, 16)
                        s_oas.v += 16
                return emit
            block.gpsimd(make_gp())

            lo, hi = DQ_W_S[s_]

            def mk_dql(lo=lo, hi=hi, last=(s_ == NSLOT - 1)):
                def emit(gpsimd):
                    dq_chunk(gpsimd, lo, hi, flush=last)
                return emit
            block.gpsimd(mk_dql())

            def mk_dqv(lo=lo, hi=hi):
                def emit(vector):
                    dq_vec(vector, lo, hi)
                return emit
            block.vector(mk_dqv())



        FCD = 3

        @block.vector
        def _(vector):
            vector.tensor_copy(out=fscr[:], in_=epsv[:]).then_inc(s_fcd.h, 1)

        @block.scalar
        def _(scalar):
            scalar.activation(out=fsc2[:], in_=epsv[:], func=AF.Copy).then_inc(s_fcd.h, 1)

        @block.gpsimd
        def _(gpsimd):
            gpsimd.wait_ge(s_oas.h, s_oas.v)
            gpsimd.wait_ge(s_dqs[0].h, s_dqs[0].v)
            gpsimd.wait_ge(s_dqs[1].h, s_dqs[1].v)
            gpsimd.sem_inc(s_fcd.h, 1)

        sC.close()
        sDQ.close()
        sD = contextlib.ExitStack()
        eD = sD.enter_context
        wpld = [eD(nc.sbuf_tensor(f"p3_wl{i}", [128, D], F32)) for i in range(2)]
        wpcb = eD(nc.sbuf_tensor("p3_w", [128, NSLOT * D], BF16))
        oalhs = [eD(nc.sbuf_tensor(f"p3_o{i}", [128, NSLOT * 512], BF16)) for i in range(2)]
        yev = [eD(nc.sbuf_tensor(f"p3_y{i}", [128, D], BF16)) for i in range(2)]
        bprep = eD(nc.sbuf_tensor("p3_bp", [128, D], F32))
        ylt = eD(nc.sbuf_tensor("p3_yl", [128, D], BF16))
        ybt = eD(nc.sbuf_tensor("p3_yb", [128, D], F32))
        lnb = eD(nc.sbuf_tensor("p3_ln", [128, D], F32))
        xt3 = eD(nc.sbuf_tensor("p3_x", [128, D], F32))
        x1b = eD(nc.sbuf_tensor("p3_x1", [128, D], F32))
        h2t = eD(nc.sbuf_tensor("p3_h2", [128, D], BF16))
        h2stg = eD(nc.sbuf_tensor("p3_h2s", [128, D], BF16))

        P3_TILES = [(r, q, 0) for r in range(NC) for q in range(4)] + \
                   [(r, q, 1) for r in range(NC) for q in range(4)]

        @block.sync
        def _(sync):
            sync.wait_ge(s_fcd.h, FCD)
            brep = bass.AP(bproj.tensor, 0, [[0, 128], [1, D]])
            sync.dma_start(out=bprep[:], in_=brep).then_inc(s_bp.h, 16)
            s_bp.v += 16
            for sl in range(NSLOT):
                if sl >= 2:
                    sync.wait_ge(s_p3wc.h, sl - 1)
                sp = s_p3w[sl % 2]
                sync.dma_start(out=wpld[sl % 2][:],
                               in_=wprojc[128 * sl:128 * (sl + 1), :]).then_inc(sp.h, 16)
                sp.v += 16
            for lg in range(16):
                r, hf = lg % 8, lg // 8
                if lg >= 2:
                    sync.wait_ge(s_p3m.h, 8 * 4 * (lg - 1))
                src = bass.AP(oaslab.tensor, 1024 * r + 512 * hf,
                              [[S, 128], [128 * S, NSLOT], [1, 512]])
                sp = s_p3l[lg % 2]
                sync.dma_start(out=oalhs[lg % 2].ap().rearrange("p (s t) -> p s t", t=512),
                               in_=src).then_inc(sp.h, 16)
                sp.v += 16

        @block.vector
        def _(vector):
            vector.wait_ge(s_fcd.h, FCD)
            for sl in range(NSLOT):
                vector.wait_ge(s_p3w[sl % 2].h, 16 * (sl // 2 + 1))
                vector.tensor_copy(out=wpcb[:, D * sl:D * (sl + 1)],
                                   in_=wpld[sl % 2][:]).then_inc(s_p3wc.h, 1)
                s_p3wc.v += 1

        @block.tensor
        def _(tensor):
            tensor.wait_ge(s_fcd.h, FCD)
            tensor.wait_ge(s_p3wc.h, NSLOT)
            for ti, (r, q, hf) in enumerate(P3_TILES):
                lg = 8 * hf + r
                tensor.wait_ge(s_p3l[lg % 2].h, 16 * (lg // 2 + 1))
                for ec in range(8):
                    eidx = 8 * ti + ec
                    if eidx >= 4:
                        tensor.wait_ge(s_p3e.h, eidx - 3)
                    for sl in range(NSLOT):
                        inst = tensor.matmul(pf[ec % 4][:],
                                             oalhs[lg % 2][:, 512 * sl + 128 * q:512 * sl + 128 * (q + 1)],
                                             wpcb[:, D * sl + 512 * ec:D * sl + 512 * (ec + 1)],
                                             start=(sl == 0), stop=(sl == NSLOT - 1))
                    inst.then_inc(s_p3m.h, 1)
                    s_p3m.v += 1

        @block.scalar
        def _(scalar):
            for ti in range(len(P3_TILES)):
                if ti >= 2:
                    scalar.wait_ge(s_p3s[ti % 2].h, 16 * (ti // 2))
                for ec in range(8):
                    eidx = 8 * ti + ec
                    scalar.wait_ge(s_p3m.h, eidx + 1)
                    scalar.activation(out=yev[ti % 2][:, 512 * ec:512 * (ec + 1)], in_=pf[ec % 4][:],
                                      func=AF.Copy).then_inc(s_p3e.h, 1)
                    s_p3e.v += 1

        @block.gpsimd
        def _(gpsimd):
            for ti, (r, q, hf) in enumerate(P3_TILES):
                ydst = y1dA if hf == 0 else y1dB
                gpsimd.wait_ge(s_p3e.h, 8 * (ti + 1))
                sp = s_p3s[ti % 2]
                gpsimd.dma_start(out=ydst[QB * r + 128 * q:QB * r + 128 * (q + 1), :],
                                 in_=yev[ti % 2][:]).then_inc(sp.h, 16)
                sp.v += 16
                if ti == NC * 4 - 1:
                    gpsimd.wait_ge(s_p3s[0].h, 16 * NC * 2)
                    gpsimd.wait_ge(s_p3s[1].h, 16 * NC * 2)
                    gpsimd.collective_compute("ReduceScatter", ALU.add, RG,
                                              [y1dA.opt()], [y1rA.opt()]).then_inc(s_rsa1.h, 1)
                    s_rsa1.v += 1
            gpsimd.wait_ge(s_p3s[0].h, 16 * NC * 4)
            gpsimd.wait_ge(s_p3s[1].h, 16 * NC * 4)
            gpsimd.collective_compute("ReduceScatter", ALU.add, RG,
                                      [y1dB.opt()], [y1rB.opt()]).then_inc(s_rsb1.h, 1)
            s_rsb1.v += 1

        @block.sync
        def _(sync):
            for hf in range(2):
                sync.wait_ge((s_rsa1 if hf == 0 else s_rsb1).h, 1)
                ysrc = y1rA if hf == 0 else y1rB
                for q in range(4):
                    rowi = 4 * hf + q
                    if rowi >= 1:
                        sync.wait_ge(s_r5v.h, 5 * rowi)
                    sync.dma_start(out=ylt[:], in_=ysrc[128 * q:128 * (q + 1), :]).then_inc(s_r5l.h, 16)
                    sync.dma_start(out=xt3[:],
                                   in_=xrows[QB * hf + 128 * q:QB * hf + 128 * (q + 1), :]).then_inc(s_r5l.h, 16)
                    s_r5l.v += 32

        @block.vector
        def _(vector):
            for hf in range(2):
                for q in range(4):
                    rowi = 4 * hf + q
                    vector.wait_ge(s_r5l.h, 32 * (rowi + 1))
                    if rowi == 0:
                        vector.wait_ge(s_bp.h, 16)
                    if rowi >= 1:
                        vector.wait_ge(s_r5r.h, 2 * rowi)
                    vedge(vector, vector.tensor_tensor(out=ybt[:], in0=ylt[:], in1=bprep[:], op=ALU.add))
                    for ch in range(8):
                        inst = vector.bn_stats(out=stat6[:, 6 * ch:6 * ch + 6], in_=ybt[:, 512 * ch:512 * (ch + 1)])
                    vedge(vector, inst)
                    vector.bn_aggr(out=stat2[:], in_=stat6[:]).then_inc(s_r5v.h, 1)
                    vector.wait_ge(s_r5r.h, 2 * rowi + 1)
                    vector.tensor_scalar(out=lnb[:], in0=ybt[:], scalar1=stat2[:, 0:1],
                                         scalar2=lnt2[:], op0=ALU.subtract, op1=ALU.mult).then_inc(s_r5v.h, 1)
                    vector.wait_ge(s_r5v.h, 5 * rowi + 2)
                    if rowi >= 1:
                        vector.wait_ge(s_x1s.h, 16 * rowi)
                    vector.tensor_tensor(out=x1b[:], in0=xt3[:], in1=lnb[:], op=ALU.add).then_inc(s_r5v.h, 1)
                    vector.wait_ge(s_r5v.h, 5 * rowi + 3)
                    for ch in range(8):
                        inst = vector.bn_stats(out=stat6[:, 6 * ch:6 * ch + 6], in_=x1b[:, 512 * ch:512 * (ch + 1)])
                    vedge(vector, inst)
                    vector.bn_aggr(out=stat2[:], in_=stat6[:]).then_inc(s_r5v.h, 1)
                    vector.wait_ge(s_r5r.h, 2 * rowi + 2)
                    if rowi >= 1:
                        vector.wait_ge(s_h2t.h, 4 * rowi)
                    vector.tensor_scalar(out=h2t[:], in0=x1b[:], scalar1=stat2[:, 0:1],
                                         scalar2=lnt2[:], op0=ALU.subtract, op1=ALU.mult).then_inc(s_r5v.h, 1)
                    s_r5v.v += 5
                    for g in range(4):
                        gg = 4 * rowi + g
                        vector.wait_ge(s_h2t.h, gg + 1)
                        if g == 0 and rowi >= 1:
                            if rowi <= 4:
                                vector.wait_ge(s_h2sA.h, 16 * min(rowi, 4))
                            if rowi > 4:
                                vector.wait_ge(s_h2sB.h, 16 * (rowi - 4))
                        vector.tensor_copy(out=h2stg[:, 1024 * g:1024 * (g + 1)],
                                           in_=pb[g % 2][:]).then_inc(s_h2e.h, 1)
                        s_h2e.v += 1

        @block.scalar
        def _(scalar):
            for rowi in range(8):
                for k2 in range(2):
                    scalar.wait_ge(s_r5v.h, 5 * rowi + (1 if k2 == 0 else 4))
                    sedge(scalar, scalar.activation(out=lnt1[:], in_=stat2[:, 1:2], func=AF.Ln, bias=epsv[:]))
                    scalar.activation(out=lnt2[:], in_=lnt1[:], func=AF.Exp, scale=-0.5).then_inc(s_r5r.h, 1)
                    s_r5r.v += 1

        @block.tensor
        def _(tensor):
            for rowi in range(8):
                tensor.wait_ge(s_r5v.h, 5 * rowi + 5)
                for g in range(4):
                    gg = 4 * rowi + g
                    if gg >= 2:
                        tensor.wait_ge(s_h2e.h, gg - 1)
                    for k in range(8):
                        ch = 8 * g + k
                        inst = tensor.transpose(pb[g % 2][:, 128 * k:128 * (k + 1)],
                                                h2t[:, 128 * ch:128 * (ch + 1)], identb[:])
                    inst.then_inc(s_h2t.h, 1)
                    s_h2t.v += 1

        @block.gpsimd
        def _(gpsimd):
            for hf in range(2):
                h2dst = h2slabA if hf == 0 else h2slabB
                sh2 = s_h2sA if hf == 0 else s_h2sB
                for q in range(4):
                    rowi = 4 * hf + q
                    gpsimd.wait_ge(s_r5v.h, 5 * rowi + 3)
                    gpsimd.dma_start(out=x1d[QB * hf + 128 * q:QB * hf + 128 * (q + 1), :],
                                     in_=x1b[:]).then_inc(s_x1s.h, 16)
                    s_x1s.v += 16
                    gpsimd.wait_ge(s_h2e.h, 4 * (rowi + 1))
                    dstap = bass.AP(h2dst.tensor, 128 * q, [[QB, 128], [128 * QB, 32], [1, 128]])
                    gpsimd.dma_start(out=dstap,
                                     in_=h2stg.ap().rearrange("p (j t) -> p j t", t=128)).then_inc(sh2.h, 16)
                    sh2.v += 16

        @block.gpsimd
        def _(gpsimd):
            gpsimd.wait_ge(s_h2sA.h, 16 * 4)
            gpsimd.collective_compute("AllGather", ALU.bypass, RG,
                                      [h2slabA.opt()], [h2gA.opt()]).then_inc(s_agh2a.h, 1)
            s_agh2a.v += 1
            gpsimd.wait_ge(s_h2sB.h, 16 * 4)
            gpsimd.collective_compute("AllGather", ALU.bypass, RG,
                                      [h2slabB.opt()], [h2gB.opt()]).then_inc(s_agh2b.h, 1)
            s_agh2b.v += 1

        FDE = 3

        @block.vector
        def _(vector):
            vector.tensor_copy(out=fscr[:], in_=epsv[:]).then_inc(s_fde.h, 1)

        @block.scalar
        def _(scalar):
            scalar.activation(out=fsc2[:], in_=epsv[:], func=AF.Copy).then_inc(s_fde.h, 1)

        @block.gpsimd
        def _(gpsimd):
            gpsimd.wait_ge(s_x1s.h, s_x1s.v)
            gpsimd.wait_ge(s_h2sA.h, s_h2sA.v)
            gpsimd.wait_ge(s_h2sB.h, s_h2sB.v)
            gpsimd.sem_inc(s_fde.h, 1)

        sD.close()
        sM.close()
        sE = contextlib.ExitStack()
        eE = sE.enter_context
        h2sb = eE(nc.sbuf_tensor("f_h2", [128, NCH * 1024], BF16))
        wgsb0 = eE(nc.sbuf_tensor("f_wg", [128, NCH * 128], BF16))
        wusb0 = eE(nc.sbuf_tensor("f_wu", [128, NCH * 128], BF16))
        wgsb = [wgsb0, wgsb0]
        wusb = [wusb0, wusb0]
        wdsb = eE(nc.sbuf_tensor("f_wd", [128, NNT * D], BF16))
        intr = eE(nc.sbuf_tensor("f_int", [128, NNT * 1024], BF16))
        silb = [eE(nc.sbuf_tensor(f"f_sil{i}", [128, 512], BF16)) for i in range(2)]
        y2ev = silb

        NGG = NNT * 4
        NDG = 64

        @block.sync
        def _(sync):
            sync.wait_ge(s_fde.h, FDE)
            for i in range(8):
                hf = i // 4
                rp = i % 4
                h2src = h2gA if hf == 0 else h2gB
                sync.wait_ge((s_agh2a if hf == 0 else s_agh2b).h, 1)
                if i >= 1:
                    sync.wait_ge(s_fg.h, NGG * i)
                for rr in range(2):
                    srcap = bass.AP(h2src.tensor, D * QB * (2 * rp + rr),
                                    [[QB, 128], [128 * QB, NCH], [1, QB]])
                    dstap = h2sb.ap().rearrange("p (j rt) -> p j rt", rt=1024)[:, :, 512 * rr:512 * (rr + 1)]
                    sync.dma_start(out=dstap, in_=srcap).then_inc(s_fh.h, 16)
                s_fh.v += 32
                for nt in range(NNT):
                    nw = NT_SIZES[nt]
                    widx = i * NNT + nt
                    if widx >= 1:
                        sync.wait_ge(s_fg.h, 4 * widx - 2)
                    sync.dma_start(out=wgsb[widx % 2][:, 0:NCH * nw],
                                   in_=wgbf[128 * nt:128 * (nt + 1), 0:NCH * nw]).then_inc(s_fwg.h, 16)
                    s_fwg.v += 16
                    if widx >= 1:
                        sync.wait_ge(s_fg.h, 4 * widx)
                    sync.dma_start(out=wusb[widx % 2][:, 0:NCH * nw],
                                   in_=wubf[128 * nt:128 * (nt + 1), 0:NCH * nw]).then_inc(s_fwu.h, 16)
                    s_fwu.v += 16
                    if i >= 1 and nt == 0:
                        sync.wait_ge(s_fy.h, NDG * i)
                    sync.dma_start(out=wdsb[0:nw, D * nt:D * (nt + 1)],
                                   in_=wdbf[128 * nt:128 * nt + nw, :]).then_inc(s_fwd.h, 16)
                    s_fwd.v += 16

        @block.tensor
        def _(tensor):
            tensor.wait_ge(s_fde.h, FDE)
            for i in range(8):
                tensor.wait_ge(s_fh.h, 32 * (i + 1))
                for nt in range(NNT):
                    nw = NT_SIZES[nt]
                    widx = i * NNT + nt
                    tensor.wait_ge(s_fwg.h, 16 * (widx + 1))
                    for mat in range(2):
                        if mat == 1:
                            tensor.wait_ge(s_fwu.h, 16 * (widx + 1))
                        wsb = (wgsb if mat == 0 else wusb)[widx % 2]
                        for rr in range(2):
                            bank = pf[2 * mat + rr]
                            if widx >= 1:
                                if mat == 0:
                                    tensor.wait_ge(s_fsil.h, 2 * widx - 2 + rr + 1)
                                else:
                                    tensor.wait_ge(s_fint.h, 2 * widx - 2 + rr + 1)
                            for j in range(NCH):
                                inst = tensor.matmul(bank[0:nw, 0:512],
                                                     wsb[:, nw * j:nw * j + nw],
                                                     h2sb[:, 1024 * j + 512 * rr:1024 * j + 512 * (rr + 1)],
                                                     start=(j == 0), stop=(j == NCH - 1))
                            inst.then_inc(s_fg.h, 1)
                            s_fg.v += 1
                tensor.wait_ge(s_fint.h, 2 * NNT * (i + 1))
                tensor.wait_ge(s_fwd.h, 16 * NNT * (i + 1))
                for ts in range(8):
                    for ec in range(8):
                        didx = NDG * i + 8 * ts + ec
                        if didx >= 2:
                            tensor.wait_ge(s_fye.h, didx - 1)
                        for nt in range(NNT):
                            nw = NT_SIZES[nt]
                            inst = tensor.matmul(pf[4 + didx % 2][:],
                                                 intr[0:nw, 1024 * nt + 128 * ts:1024 * nt + 128 * (ts + 1)],
                                                 wdsb[0:nw, D * nt + 512 * ec:D * nt + 512 * (ec + 1)],
                                                 start=(nt == 0), stop=(nt == NNT - 1))
                        inst.then_inc(s_fy.h, 1)
                        s_fy.v += 1

        @block.scalar
        def _(scalar):
            for i in range(8):
                if i >= 1:
                    scalar.wait_ge(s_fys[0].h, 16 * NDG * i // 2)
                    scalar.wait_ge(s_fys[1].h, 16 * NDG * i // 2)
                for nt in range(NNT):
                    widx = i * NNT + nt
                    nw = NT_SIZES[nt]
                    for rr in range(2):
                        sidx = 2 * widx + rr
                        scalar.wait_ge(s_fg.h, 4 * widx + rr + 1)
                        if sidx >= 2:
                            scalar.wait_ge(s_fint.h, sidx - 1)
                        scalar.activation(out=silb[sidx % 2][0:nw, :], in_=pf[rr][0:nw, 0:512],
                                          func=AF.Silu).then_inc(s_fsil.h, 1)
                        s_fsil.v += 1
                for ts in range(8):
                    for ec in range(8):
                        didx = NDG * i + 8 * ts + ec
                        scalar.wait_ge(s_fy.h, didx + 1)
                        if didx >= 2:
                            scalar.wait_ge(s_fys[didx % 2].h, 16 * (didx // 2))
                        scalar.activation(out=y2ev[didx % 2][:], in_=pf[4 + didx % 2][:],
                                          func=AF.Copy).then_inc(s_fye.h, 1)
                        s_fye.v += 1

        @block.vector
        def _(vector):
            vector.wait_ge(s_fde.h, FDE)
            for i in range(8):
                if i >= 1:
                    vector.wait_ge(s_fy.h, NDG * i)
                for nt in range(NNT):
                    widx = i * NNT + nt
                    nw = NT_SIZES[nt]
                    for rr in range(2):
                        sidx = 2 * widx + rr
                        vector.wait_ge(s_fsil.h, sidx + 1)
                        vector.wait_ge(s_fg.h, 4 * widx + 2 + rr + 1)
                        vector.tensor_tensor(out=intr[0:nw, 1024 * nt + 512 * rr:1024 * nt + 512 * (rr + 1)],
                                             in0=silb[sidx % 2][0:nw, :],
                                             in1=pf[2 + rr][0:nw, 0:512],
                                             op=ALU.mult).then_inc(s_fint.h, 1)
                        s_fint.v += 1

        @block.gpsimd
        def _(gpsimd):
            for i in range(8):
                hf = i // 4
                rp = i % 4
                ydst = y2dA if hf == 0 else y2dB
                for ts in range(8):
                    for ec in range(8):
                        didx = NDG * i + 8 * ts + ec
                        gpsimd.wait_ge(s_fye.h, didx + 1)
                        r_ = 2 * rp + ts // 4
                        tq = ts % 4
                        sp = s_fys[didx % 2]
                        gpsimd.dma_start(out=ydst[QB * r_ + 128 * tq:QB * r_ + 128 * (tq + 1),
                                                  512 * ec:512 * (ec + 1)],
                                         in_=y2ev[didx % 2][:]).then_inc(sp.h, 16)
                        sp.v += 16
                if i == 3:
                    gpsimd.wait_ge(s_fys[0].h, 16 * NDG * 2)
                    gpsimd.wait_ge(s_fys[1].h, 16 * NDG * 2)
                    gpsimd.collective_compute("ReduceScatter", ALU.add, RG,
                                              [y2dA.opt()], [y2rA.opt()]).then_inc(s_rsa2.h, 1)
                    s_rsa2.v += 1
            gpsimd.wait_ge(s_fys[0].h, 16 * NDG * 4)
            gpsimd.wait_ge(s_fys[1].h, 16 * NDG * 4)
            gpsimd.collective_compute("ReduceScatter", ALU.add, RG,
                                      [y2dB.opt()], [y2rB.opt()]).then_inc(s_rsb2.h, 1)
            s_rsb2.v += 1

        FEF = 3

        @block.vector
        def _(vector):
            vector.tensor_copy(out=fscr[:], in_=epsv[:]).then_inc(s_fef.h, 1)

        @block.scalar
        def _(scalar):
            scalar.activation(out=fsc2[:], in_=epsv[:], func=AF.Copy).then_inc(s_fef.h, 1)

        @block.gpsimd
        def _(gpsimd):
            gpsimd.wait_ge(s_fys[0].h, s_fys[0].v)
            gpsimd.wait_ge(s_fys[1].h, s_fys[1].v)
            gpsimd.sem_inc(s_fef.h, 1)

        sE.close()
        sF = contextlib.ExitStack()
        eF = sF.enter_context
        y5l2 = [eF(nc.sbuf_tensor(f"p5_y{i}", [128, D], BF16)) for i in range(2)]
        x5l2 = [eF(nc.sbuf_tensor(f"p5_x{i}", [128, D], F32)) for i in range(2)]
        ln5 = eF(nc.sbuf_tensor("p5_ln", [128, D], F32))
        dln = eF(nc.sbuf_tensor("p5_d", [128, D], F32))
        o5 = eF(nc.sbuf_tensor("p5_o", [128, D], F32))
        amx = eF(nc.sbuf_tensor("p5_amx", [128, 1], F32))
        msk = eF(nc.sbuf_tensor("p5_msk", [128, 1], F32))
        rst2 = eF(nc.sbuf_tensor("p5_rst2", [128, 1], F32))

        @block.sync
        def _(sync):
            sync.wait_ge(s_fef.h, FEF)
            for hf in range(2):
                sync.wait_ge((s_rsa2 if hf == 0 else s_rsb2).h, 1)
                ysrc = y2rA if hf == 0 else y2rB
                for q in range(4):
                    rowi = 4 * hf + q
                    if rowi >= 2:
                        sync.wait_ge(s_p5v.h, 3 * (rowi - 1))
                    sp = s_p5l[rowi % 2]
                    sync.dma_start(out=y5l2[rowi % 2][:], in_=ysrc[128 * q:128 * (q + 1), :]).then_inc(sp.h, 16)
                    sync.dma_start(out=x5l2[rowi % 2][:],
                                   in_=x1d[QB * hf + 128 * q:QB * hf + 128 * (q + 1), :]).then_inc(sp.h, 16)
                    sp.v += 32

        @block.vector
        def _(vector):
            vector.wait_ge(s_fef.h, FEF)
            for rowi in range(8):
                y5l = y5l2[rowi % 2]
                x5l = x5l2[rowi % 2]
                vector.wait_ge(s_p5l[rowi % 2].h, 32 * (rowi // 2 + 1))
                if rowi >= 1:
                    vector.wait_ge(s_p5r.h, rowi)
                for ch in range(8):
                    inst = vector.bn_stats(out=stat6[:, 6 * ch:6 * ch + 6], in_=y5l[:, 512 * ch:512 * (ch + 1)])
                vedge(vector, inst)
                vector.bn_aggr(out=stat2[:], in_=stat6[:]).then_inc(s_p5v.h, 1)
                vector.wait_ge(s_p5v.h, 3 * rowi + 1)
                vedge(vector, vector.tensor_scalar(out=msk[:], in0=stat2[:, 1:2], scalar1=65520.0,
                                                   scalar2=None, op0=ALU.is_lt))
                vector.wait_ge(s_p5r.h, rowi + 1)
                vedge(vector, vector.tensor_tensor(out=rst2[:], in0=msk[:], in1=lnt2[:], op=ALU.mult))
                vector.tensor_scalar(out=ln5[:], in0=y5l[:], scalar1=stat2[:, 0:1],
                                     scalar2=rst2[:], op0=ALU.subtract, op1=ALU.mult).then_inc(s_p5v.h, 1)
                vector.wait_ge(s_p5v.h, 3 * rowi + 2)
                if rowi >= 1:
                    vector.wait_ge(s_p5s.h, 16 * rowi)
                vector.tensor_tensor(out=o5[:], in0=x5l[:], in1=ln5[:], op=ALU.add).then_inc(s_p5v.h, 1)
                s_p5v.v += 3

        @block.scalar
        def _(scalar):
            for rowi in range(8):
                scalar.wait_ge(s_p5v.h, 3 * rowi + 1)
                sedge(scalar, scalar.activation(out=lnt1[:], in_=stat2[:, 1:2], func=AF.Ln, bias=epsv[:]))
                scalar.activation(out=lnt2[:], in_=lnt1[:], func=AF.Exp, scale=-0.5).then_inc(s_p5r.h, 1)
                s_p5r.v += 1

        @block.gpsimd
        def _(gpsimd):
            for hf in range(2):
                for q in range(4):
                    rowi = 4 * hf + q
                    gpsimd.wait_ge(s_p5v.h, 3 * rowi + 3)
                    gpsimd.dma_start(out=out[QB * hf + 128 * q:QB * hf + 128 * (q + 1), :],
                                     in_=o5[:]).then_inc(s_p5s.h, 16)
                    s_p5s.v += 16
            if dbg:
                srcs = {"h0": h0g, "qkv": qkvd.rearrange("a p t -> (a p) t"), "oa": oaslab,
                        "y1a": y1rA, "x1": x1d, "h2a": h2gA, "h2b": h2gB,
                        "y2a": y2rA, "y2b": y2rB}
                for name, ap_ in dbg.items():
                    gpsimd.dma_start(out=ap_, in_=srcs[name]).then_inc(s_p5s.h, 16)
                    s_p5s.v += 16
            gpsimd.wait_ge(s_p5s.h, s_p5s.v)

        sF.close()

    return nc



def stage_inputs(inputs):
    x = np.asarray(inputs["L0_attn_hidden_states"]).reshape(S, D)
    w_attn = np.asarray(inputs["L0_attn_w_attn"])
    b_attn = np.asarray(inputs["L0_attn_b_attn"])
    w_proj = np.asarray(inputs["L0_attn_w_proj"])
    b_proj = np.asarray(inputs["L0_attn_b_proj"])
    gqw = np.asarray(inputs["L0_ffn_gate_qweight"])
    gsc = np.asarray(inputs["L0_ffn_gate_scales"])
    gzr = np.asarray(inputs["L0_ffn_gate_zeros"])
    uqw = np.asarray(inputs["L0_ffn_up_qweight"])
    usc = np.asarray(inputs["L0_ffn_up_scales"])
    uzr = np.asarray(inputs["L0_ffn_up_zeros"])
    dqw = np.asarray(inputs["L0_ffn_down_qweight"])
    dsc_ = np.asarray(inputs["L0_ffn_down_scales"])
    dzr = np.asarray(inputs["L0_ffn_down_zeros"])

    bf = ml_dtypes.bfloat16
    p = np.arange(128)
    shifts = (4 * (p % 8)).astype(np.int32).reshape(128, 1)
    identf = np.eye(128, dtype=np.float32)
    onesb = np.ones((128, 128), dtype=bf)
    g_of_m = None

    in_maps = []
    for c in range(NC):
        d = {}
        d["xrows"] = np.ascontiguousarray(x[TPC * c:TPC * (c + 1), :])
        cols = []
        bcols = []
        for s in range(NSLOT):
            h = 8 * s + c
            for g0, base in ((0, 0), (1, D), (2, 2 * D)):
                cols.append(w_attn[:, base + HD * h:base + HD * (h + 1)])
                bcols.append(b_attn[base + HD * h:base + HD * (h + 1)])
        d["wqkv"] = np.ascontiguousarray(np.concatenate(cols, axis=1))
        d["bqkvT"] = np.ascontiguousarray(np.stack(bcols, axis=1))
        d["wprojc"] = np.ascontiguousarray(
            np.concatenate([w_proj[HD * (8 * s + c):HD * (8 * s + c + 1), :] for s in range(NSLOT)], axis=0))
        d["bproj"] = np.ascontiguousarray(b_proj.reshape(1, D))

        pt = np.zeros((NSLOT * 5 * 128, QB), dtype=bf)
        eb = np.zeros((128, NSLOT * 64), dtype=np.float32)
        jj = np.arange(QB)
        for s in range(NSLOT):
            sig = float(SLOPES[8 * s + c])
            for v in range(4):
                arg = sig * (128.0 * v + p[:, None] - jj[None, :])
                m = (jj[None, :] >= 128 * v + p[:, None])
                with np.errstate(over="ignore", under="ignore"):
                    tile = np.where(m, np.exp(np.minimum(arg, 0.0)), 0.0)
                pt[(5 * s + v) * 128:(5 * s + v + 1) * 128, :] = tile.astype(bf)
            arg = sig * (p[:, None] - jj[None, :] - 128.0)
            with np.errstate(under="ignore"):
                pt[(5 * s + 4) * 128:(5 * s + 5) * 128, :] = np.exp(arg).astype(bf)
            for iu in range(64):
                u = 3 - iu
                if u <= -1:
                    eb[:, 64 * s + iu] = np.float32(max(128.0 * sig * (u + 1), -1e30))
        d["patt"] = pt
        d["ebias"] = eb
        d["shifts"] = shifts
        d["identf"] = identf
        d["onesb"] = onesb

        nsl0 = NSL * c
        d["gq"] = np.ascontiguousarray(gqw[:, nsl0:nsl0 + NSL])
        d["gs"] = np.ascontiguousarray(gsc[:, nsl0:nsl0 + NSL])
        d["gz"] = np.ascontiguousarray(gzr[:, nsl0 // 8:(nsl0 + NSL) // 8])
        d["uq"] = np.ascontiguousarray(uqw[:, nsl0:nsl0 + NSL])
        d["us"] = np.ascontiguousarray(usc[:, nsl0:nsl0 + NSL])
        d["uz"] = np.ascontiguousarray(uzr[:, nsl0 // 8:(nsl0 + NSL) // 8])
        d["dq"] = np.ascontiguousarray(dqw[nsl0 // 8:(nsl0 + NSL) // 8, :])
        if g_of_m is None:
            pass
        gm = (nsl0 + np.arange(NSL)) // 128
        d["dsx"] = np.ascontiguousarray(dsc_[gm, :])
        d["dzx"] = np.ascontiguousarray(dzr[gm, :])
        in_maps.append(d)
    return in_maps


def kernel(**inputs):
    in_maps = stage_inputs(inputs)
    nc = build_nc(debug=DEBUG_DUMPS)
    res = run_bass_kernel_spmd(nc, in_maps, core_ids=list(range(NC)))
    outs = [np.asarray(res.results[c]["out"]) for c in range(NC)]
    full = np.concatenate(outs, axis=0).reshape(1, S, D).astype(np.float32)
    if DEBUG_DUMPS:
        kernel._dbg = res.results
    return full
